# version 07
# baseline (speedup 1.0000x reference)
import os
import contextlib
import numpy as np
import ml_dtypes
import concourse.bass as bass
import concourse.mybir as mybir
from concourse.bass_utils import run_bass_kernel_spmd

F32 = mybir.dt.float32
BF16 = mybir.dt.bfloat16
I16 = mybir.dt.int16
AF = mybir.ActivationFunctionType
ALU = mybir.AluOpType
AX = mybir.AxisListType

MAGIC = float(3 * 2 ** 22)
B, S, D, I3 = 4, 4096, 1024, 3072
T = S // 2
NH, DH, CK = 16, 64, 128
NCH = T // CK
CT = D // 128
OT = I3 // 128
QMAX = 32767.0
EPS = 1e-5
STAGE = float(os.environ.get("KSTAGE", "3"))


def _host_consts():
    f64 = np.float64
    inv = 1.0 / (10000.0 ** (np.arange(0, DH, 2, dtype=f64) / DH))
    t = np.arange(S, dtype=f64)
    fr = np.outer(t, inv)
    cos_t = np.concatenate([np.cos(fr), np.cos(fr)], axis=1).astype(np.float32)
    sn = np.sin(fr)
    sinS_t = np.concatenate([-sn, sn], axis=1).astype(np.float32)
    gam = 1.0 - np.exp2(-5.0 - np.arange(NH, dtype=f64))
    i = np.arange(CK)
    dm = i[None, :] - i[:, None]
    low = (dm >= 0).astype(f64)
    decayT = np.zeros((CK, NH * CK), np.float32)
    for h in range(NH):
        decayT[:, h * CK:(h + 1) * CK] = (
            np.power(gam[h], np.maximum(dm, 0)) * low / 8.0).astype(np.float32)
    qscT = np.zeros((128, CT * CK), np.float32)
    for m in range(CT):
        for hf in range(2):
            qscT[hf * 64:(hf + 1) * 64, m * CK:(m + 1) * CK] = (
                np.power(gam[2 * m + hf], i + 1) / 8.0).astype(np.float32)[None, :]
    kscT = np.zeros((CK, D), np.float32)
    for h in range(NH):
        kscT[:, h * DH:(h + 1) * DH] = np.power(
            gam[h], CK - 1 - i).astype(np.float32)[:, None]
    gctab = np.zeros((128, 512), np.float32)
    for m in range(CT):
        for hf in range(2):
            gctab[hf * 64:(hf + 1) * 64, m * DH:(m + 1) * DH] = float(
                np.power(gam[2 * m + hf], CK))
    ident = np.eye(128, dtype=np.float32)
    return dict(cos_t=cos_t, sinS_t=sinS_t, decayT=decayT, qscT=qscT,
                kscT=kscT, gctab=gctab, ident=ident)


class Sem:
    def __init__(self, es, nc, name):
        self.h = es.enter_context(nc.semaphore(name))
        self.n = 0

    def inc(self, instr, k=1):
        instr.then_inc(self.h, k)
        self.n += k
        return self.n


_FENCED = {"tensor_tensor", "tensor_scalar", "tensor_reduce", "reciprocal",
           "tensor_copy", "memset", "activation", "copy",
           "scalar_tensor_tensor", "mul", "add"}


class DSlot:

    def __init__(self, es, nc, name, nslots):
        self.sems = [Sem(es, nc, "%s%d" % (name, i)) for i in range(nslots)]

    def issue(self, instr, slot):
        return self.sems[slot].inc(instr, 16)

    def wait(self, eng, slot):
        eng.wait_ge(self.sems[slot].h, self.sems[slot].n)

    def n(self, slot):
        return self.sems[slot].n

    def h(self, slot):
        return self.sems[slot].h


class FE:

    def __init__(self, eng, sem):
        self._e = eng
        self._s = sem

    def wait_ge(self, h, n):
        self._e.wait_ge(h, n)

    def dma_start(self, *a, **k):
        return self._e.dma_start(*a, **k)

    def dma_start_transpose(self, *a, **k):
        return self._e.dma_start_transpose(*a, **k)

    def __getattr__(self, name):
        fn = getattr(self._e, name)
        if name in _FENCED:
            s = self._s

            def wrapped(*a, **k):
                ins = fn(*a, **k)
                s.n += 1
                ins.then_inc(s.h, 1)
                self._e.wait_ge(s.h, s.n)
                return ins
            return wrapped
        return fn


def build_nc():
    nc = bass.Bass(target_bir_lowering=False)

    def din(name, shape, dt=F32):
        return nc.dram_tensor(name, shape, dt, kind="ExternalInput").ap()

    x_own = din("x_own", [CT, 128, T])
    x_oth = din("x_oth", [CT, 128, T])
    wqT = din("wqT", [D, D]); wkT = din("wkT", [D, D]); wvT = din("wvT", [D, D])
    wgTT = din("wgTT", [D, D]); woT = din("woT", [D, D])
    w_gate = din("w_gate", [I3, D]); w_up = din("w_up", [I3, D])
    w_down = din("w_down", [D, I3])
    cos_own = din("cos_own", [T, DH]); sinS_own = din("sinS_own", [T, DH])
    cos_oth = din("cos_oth", [T, DH]); sinS_oth = din("sinS_oth", [T, DH])
    decayT_i = din("decayT", [CK, NH * CK])
    qscT_i = din("qscT", [128, CT * CK])
    kscT_i = din("kscT", [CK, D])
    gctab_i = din("gctab", [128, 512])
    maskS_i = din("maskS", [128, 1])
    ident_i = din("ident", [128, 128])

    y = nc.dram_tensor("y", [CT, 128, T], F32, kind="ExternalOutput").ap()

    def dsc(name, shape, dt):
        return nc.dram_tensor(name, shape, dt).ap()

    wg_q = dsc("wg_q", [I3, D], BF16)
    wu_q = dsc("wu_q", [I3, D], BF16)
    wd_q = dsc("wd_q", [D, I3], BF16)
    U_d = dsc("U_d", [OT, 128, T], F32)
    G_d = dsc("G_d", [OT, 128, T], F32)
    g16_d = dsc("g16_d", [OT, 128, T], I16)
    g216_d = dsc("g216_d", [OT, 128, T], I16)
    u16_d = dsc("u16_d", [OT, 128, T], I16)
    x1_d = dsc("x1_d", [CT, 128, T], F32)
    gated_d = dsc("gated_d", [CT, 128, T], BF16)
    trb_d = dsc("trb_d", [T, D], BF16)
    trb2_d = dsc("trb2_d", [T, D], BF16)

    es = contextlib.ExitStack()
    ones_col = es.enter_context(nc.sbuf_tensor("ones_col", [128, 1], F32))
    ones_row = es.enter_context(nc.sbuf_tensor("ones_row", [1, 128], F32))
    COLS = es.enter_context(nc.sbuf_tensor("COLS", [128, 16], F32))
    SSC = es.enter_context(nc.sbuf_tensor("SSC", [1, 64], F32))
    MXC = es.enter_context(nc.sbuf_tensor("MXC", [128, 128], F32))
    IDN = es.enter_context(nc.sbuf_tensor("IDN", [128, 128], F32))
    MSKC = es.enter_context(nc.sbuf_tensor("MSKC", [128, 1], F32))
    SST = es.enter_context(nc.sbuf_tensor("SST", [128, 512], F32))
    SSB = es.enter_context(nc.sbuf_tensor("SSB", [128, 512], BF16))
    PB = [es.enter_context(nc.psum_tensor("pb%d" % k, [128, 512], F32))
          for k in range(8)]

    qS = Sem(es, nc, "qS")
    gD = Sem(es, nc, "gD")
    qA = Sem(es, nc, "qA")
    qV = Sem(es, nc, "qV")
    qG = Sem(es, nc, "qG")
    vD = Sem(es, nc, "vD")
    aD = Sem(es, nc, "aD")
    pD = Sem(es, nc, "pD")
    dsWR = DSlot(es, nc, "dsWR", 2)
    dsWQ = DSlot(es, nc, "dsWQ", 2)
    dsEV = DSlot(es, nc, "dsEV", 2)
    dsGS = DSlot(es, nc, "dsGS", 2)
    dsGI = DSlot(es, nc, "dsGI", 3)
    dsUS = DSlot(es, nc, "dsUS", 2)
    dsUI = DSlot(es, nc, "dsUI", 2)
    dsG2 = DSlot(es, nc, "dsG2", 2)
    dsL4 = DSlot(es, nc, "dsL4", 2)
    dsGT = DSlot(es, nc, "dsGT", 2)
    dsX1 = DSlot(es, nc, "dsX1", 2)

    _uid = [0]

    def un(base):
        _uid[0] += 1
        return "%s_%d" % (base, _uid[0])

    def VEC(blk, fn):
        blk.vector(lambda e: fn(FE(e, vD)))

    def SCA(blk, fn):
        blk.scalar(lambda e: fn(FE(e, aD)))

    def PLC(blk, fn):
        blk.gpsimd(lambda e: fn(FE(e, gD)))

    def EW(blk, fn, pool):
        (PLC if pool else VEC)(blk, fn)

    def ewm(pool):
        s = gD if pool else vD
        return (s.h, s.n)

    iRX, iSX, iRUP, iSUP, iRG, iSG, iK2, iK3, iOS, iSO = range(10)
    (jEPS, jMX, jSX, jRX, jT0, jT1, jSG, jSUP, jSG2, jRG2, jK2, jSO, jRO,
     jK3, jMG, jMUP, jMP, jMP2, jRG, jRUP) = range(20)

    _strbox = [None]

    def STRr(r):
        return _strbox[0][0:1, r * T:(r + 1) * T]

    def final_max(blk, ncols, out_j):
        def v1(v):
            v.tensor_reduce(MXC[:, 127:128], MXC[:, 0:ncols],
                                   op=ALU.max, axis=AX.X)
        VEC(blk, v1)
        mv = vD.n

        def t1(te):
            te.wait_ge(vD.h, mv)
            pD.inc(te.transpose(PB[7][0:1, 0:128], MXC[:, 127:128], IDN[:]))
        blk.tensor(t1)
        mp = pD.n

        def v2(v):
            v.wait_ge(pD.h, mp)
            v.tensor_reduce(SSC[0:1, out_j:out_j + 1], PB[7][0:1, 0:128],
                            op=ALU.max, axis=AX.X)
            v.memset(MXC[:], 0.0)
        VEC(blk, v2)

    def bcast_scalar(blk, src_j, dst_i):
        mv = vD.n
        ma = aD.n

        def t1(te):
            te.wait_ge(vD.h, mv)
            te.wait_ge(aD.h, ma)
            pD.inc(te.matmul(PB[7][:, 0:1], ones_row[:],
                             SSC[0:1, src_j:src_j + 1], start=True, stop=True))
        blk.tensor(t1)
        mp = pD.n

        def s1(sc):
            sc.wait_ge(pD.h, mp)
            sc.copy(COLS[:, dst_i:dst_i + 1], PB[7][:, 0:1])
        SCA(blk, s1)

    def qdq_scale(blk, m_j, s_j, r_j):
        def v1(v):
            v.wait_ge(vD.h, vD.n)
            v.tensor_scalar(SSC[0:1, s_j:s_j + 1], SSC[0:1, m_j:m_j + 1],
                            1.0 / QMAX, 1e-12, op0=ALU.mult, op1=ALU.max)
            v.reciprocal(SSC[0:1, r_j:r_j + 1], SSC[0:1, s_j:s_j + 1])
        VEC(blk, v1)

    def emit_ln(XH, SQ, qdq, XQ):
        esL = contextlib.ExitStack()
        _strbox[0] = esL.enter_context(nc.sbuf_tensor(un("STR"), [1, 2 * T],
                                                      F32))
        mq0 = qS.n
        with nc.Block() as blk:
            pe_done = []
            for ct in range(CT):
                ma_prev = aD.n

                def s_sq(sc, ct=ct, mq0=mq0):
                    if ct == 0:
                        sc.wait_ge(qS.h, mq0)
                    if ct >= 2:
                        sc.wait_ge(pD.h, pe_done[ct - 2])
                    sc.activation(SQ[:, ct % 2, :], XH[:, ct, :],
                                         AF.Square)
                SCA(blk, s_sq)
                ma = aD.n

                def t_mm(te, ct=ct, ma=ma):
                    te.wait_ge(aD.h, ma)
                    for q in range(4):
                        st = (ct == 0)
                        sp = (ct == CT - 1)
                        te.matmul(PB[q][0:1, :], ones_col[:],
                                  XH[:, ct, q * 512:(q + 1) * 512],
                                  start=st, stop=sp)
                        mm = te.matmul(PB[4 + q][0:1, :], ones_col[:],
                                       SQ[:, ct % 2, q * 512:(q + 1) * 512],
                                       start=st, stop=sp)
                    pD.inc(mm)
                blk.tensor(t_mm)
                pe_done.append(pD.n)

            mp = pD.n

            def s_ev(sc):
                sc.wait_ge(pD.h, mp)
                for q in range(4):
                    sc.copy(STRr(0)[0:1, q * 512:(q + 1) * 512], PB[q][0:1, :])
                    sc.copy(STRr(1)[0:1, q * 512:(q + 1) * 512],
                                   PB[4 + q][0:1, :])
            SCA(blk, s_ev)
            ma = aD.n

            def v_st(v):
                v.wait_ge(aD.h, ma)
                v.tensor_scalar(STRr(0), STRr(0), 1.0 / D, None, op0=ALU.mult)
                v.tensor_scalar(STRr(1), STRr(1), 1.0 / D, None, op0=ALU.mult)
                v.tensor_tensor(SQ[0:1, 0, :], STRr(0), STRr(0), op=ALU.mult)
                v.tensor_tensor(STRr(1), STRr(1), SQ[0:1, 0, :],
                                       op=ALU.subtract)
            VEC(blk, v_st)
            mv = vD.n

            def s_sd(sc):
                sc.wait_ge(vD.h, mv)
                sc.activation(STRr(1), STRr(1), AF.Sqrt,
                                     bias=SSC[0:1, jEPS:jEPS + 1])
            SCA(blk, s_sd)
            ma2 = aD.n

            def v_rs(v):
                v.wait_ge(aD.h, ma2)
                v.reciprocal(STRr(1), STRr(1))
            VEC(blk, v_rs)

        if not qdq:
            with nc.Block() as blk:
                last_reads = [vD.n, vD.n]
                for q in range(4):
                    mv0 = last_reads[0]
                    mv1 = last_reads[1]

                    def t_bc(te, q=q, mv0=mv0, mv1=mv1):
                        te.wait_ge(vD.h, max(mv0, mv1, 0))
                        te.matmul(PB[4][:, :], ones_row[:],
                                  STRr(0)[0:1, q * 512:(q + 1) * 512],
                                  start=True, stop=True)
                        pD.inc(te.matmul(PB[5][:, :], ones_row[:],
                                         STRr(1)[0:1, q * 512:(q + 1) * 512],
                                         start=True, stop=True))
                    blk.tensor(t_bc)
                    mp = pD.n

                    def v_nm(v, q=q, mp=mp):
                        v.wait_ge(pD.h, mp)
                        for ct in range(CT):
                            sl = slice(q * 512, (q + 1) * 512)
                            v.tensor_tensor(SQ[:, 0, 0:512], XH[:, ct, sl],
                                            PB[4][:, :], op=ALU.subtract)
                            mm = v.tensor_tensor(XQ[:, ct, sl],
                                                 SQ[:, 0, 0:512],
                                                 PB[5][:, :], op=ALU.mult)
                        mm
                    VEC(blk, v_nm)
                    last_reads = [vD.n, vD.n]
            esL.close()
            return

        with nc.Block() as blk:
            for q in range(4):
                mv = vD.n

                def t_bc(te, q=q, mv=mv):
                    te.wait_ge(vD.h, mv)
                    te.matmul(PB[4][:, :], ones_row[:],
                              STRr(0)[0:1, q * 512:(q + 1) * 512],
                              start=True, stop=True)
                    pD.inc(te.matmul(PB[5][:, :], ones_row[:],
                                     STRr(1)[0:1, q * 512:(q + 1) * 512],
                                     start=True, stop=True))
                blk.tensor(t_bc)
                mp = pD.n

                def v_nm(v, q=q, mp=mp):
                    v.wait_ge(pD.h, mp)
                    for ct in range(CT):
                        sl = slice(q * 512, (q + 1) * 512)
                        v.tensor_tensor(SQ[:, 0, 0:512], XH[:, ct, sl],
                                        PB[4][:, :], op=ALU.subtract)
                        v.tensor_tensor(SQ[:, 1, 0:512], SQ[:, 0, 0:512],
                                        PB[5][:, :], op=ALU.mult)
                        mm = v.tensor_reduce(
                            MXC[:, q * 8 + ct:q * 8 + ct + 1],
                            SQ[:, 1, 0:512], op=ALU.max, axis=AX.X,
                            apply_absolute_value=True)
                    mm
                VEC(blk, v_nm)

        with nc.Block() as blk:
            final_max(blk, 32, jMX)
            qdq_scale(blk, jMX, jSX, jRX)
            bcast_scalar(blk, jSX, iSX)

            def v_rr(v):
                v.wait_ge(vD.h, vD.n)
                v.tensor_scalar(STRr(1), STRr(1), SSC[0:1, jRX:jRX + 1], None,
                                op0=ALU.mult)
                v.tensor_tensor(STRr(0), STRr(0), STRr(1), op=ALU.mult)
            VEC(blk, v_rr)

        with nc.Block() as blk:
            ma0 = aD.n
            for q in range(4):
                mv = vD.n

                def t_bc(te, q=q, mv=mv):
                    te.wait_ge(vD.h, mv)
                    te.wait_ge(aD.h, ma0)
                    te.matmul(PB[4][:, :], ones_row[:],
                              STRr(1)[0:1, q * 512:(q + 1) * 512],
                              start=True, stop=True)
                    pD.inc(te.matmul(PB[5][:, :], ones_row[:],
                                     STRr(0)[0:1, q * 512:(q + 1) * 512],
                                     start=True, stop=True))
                blk.tensor(t_bc)
                mp = pD.n

                def v_qz(v, q=q, mp=mp):
                    v.wait_ge(pD.h, mp)
                    for ct in range(CT):
                        sl = slice(q * 512, (q + 1) * 512)
                        v.tensor_tensor(SQ[:, 0, 0:512], XH[:, ct, sl],
                                        PB[4][:, :], op=ALU.mult)
                        v.tensor_tensor(SQ[:, 1, 0:512], SQ[:, 0, 0:512],
                                        PB[5][:, :], op=ALU.subtract)
                        v.tensor_scalar(SQ[:, 0, 0:512], SQ[:, 1, 0:512],
                                        MAGIC, MAGIC, op0=ALU.add,
                                        op1=ALU.subtract)
                        mm = v.tensor_scalar(XQ[:, ct, sl], SQ[:, 0, 0:512],
                                             COLS[:, iSX:iSX + 1], None,
                                             op0=ALU.mult)
                    mm
                VEC(blk, v_qz)
        esL.close()

    def emit_ffn(XH):
        esF = contextlib.ExitStack()
        WT = esF.enter_context(nc.sbuf_tensor(un("WT_f"), [128, OT, D], BF16))
        WTg = WT[:].rearrange("p o d -> p (o d)").rearrange(
            "p (i o) -> p i o", i=CT)

        with contextlib.ExitStack() as esQ:
            XQ = esQ.enter_context(nc.sbuf_tensor(un("XQ_f"), [128, CT, T], BF16))
            EV = esQ.enter_context(nc.sbuf_tensor(un("EV_f"), [128, 2, 512], F32))
            SQ = esQ.enter_context(nc.sbuf_tensor(un("SQ_f"), [128, 2, T], F32))
            emit_ln(XH, SQ, True, XQ)

            for wsta, wdst, mj, sj, rj, ci in (
                    (wg_q, G_d, jMG, jSG, jRG, iRG),
                    (wu_q, U_d, jMUP, jSUP, jRUP, iRUP)):
                with nc.Block() as blk:
                    mp_wt = pD.n
                    mq0 = dsWQ.n(0)
                    mq1 = dsWQ.n(1)

                    def s_ld(sy, wsta=wsta, mp_wt=mp_wt, mq0=mq0, mq1=mq1):
                        sy.wait_ge(pD.h, mp_wt)
                        sy.wait_ge(qS.h, qS.n)
                        sy.wait_ge(dsWQ.h(0), mq0)
                        sy.wait_ge(dsWQ.h(1), mq1)
                        for i in range(CT):
                            qS.inc(sy.dma_start_transpose(
                                WTg[:, i, :],
                                wsta[:, i * 128:(i + 1) * 128]), 16)
                    blk.sync(s_ld)
                    m_ld = qS.n
                    mv_xq = vD.n

                    evac_done = []
                    red_done = []
                    dma_done = []
                    g = 0
                    for ot in range(OT):
                        for q in range(4):
                            def t_g(te, ot=ot, q=q, g=g):
                                if g == 0:
                                    te.wait_ge(qS.h, m_ld)
                                    te.wait_ge(vD.h, mv_xq)
                                if g >= 2:
                                    te.wait_ge(aD.h, evac_done[g - 2])
                                for i in range(CT):
                                    mm = te.matmul(
                                        PB[g % 2][:, :],
                                        WTg[:, i, ot * 128:(ot + 1) * 128],
                                        XQ[:, i, q * 512:(q + 1) * 512],
                                        start=(i == 0), stop=(i == CT - 1))
                                pD.inc(mm)
                            blk.tensor(t_g)
                            mp = pD.n

                            def s_g(sc, ot=ot, q=q, g=g, mp=mp, wdst=wdst,
                                    mev=dsEV.n(g % 2), mv_xq=mv_xq):
                                sc.wait_ge(pD.h, mp)
                                sc.wait_ge(dsEV.h(g % 2), mev)
                                sc.wait_ge(vD.h, red_done[g - 2]
                                           if g >= 2 else mv_xq)
                                sc.copy(EV[:, g % 2, :],
                                               PB[g % 2][:, :])
                                dsEV.issue(sc.dma_start(
                                    out=wdst[ot, :, q * 512:(q + 1) * 512],
                                    in_=EV[:, g % 2, :]), g % 2)
                            SCA(blk, s_g)
                            evac_done.append(aD.n)
                            dma_done.append(0)

                            def v_g(v, g=g, ma=aD.n):
                                v.wait_ge(aD.h, ma)
                                v.tensor_reduce(
                                    MXC[:, g:g + 1], EV[:, g % 2, :],
                                    op=ALU.max, axis=AX.X,
                                    apply_absolute_value=True)
                            VEC(blk, v_g)
                            red_done.append(vD.n)
                            g += 1

                with nc.Block() as blk:
                    final_max(blk, 96, mj)
                    qdq_scale(blk, mj, sj, rj)
                    bcast_scalar(blk, rj, ci)
                    if sj == jSG:
                        bcast_scalar(blk, sj, iSG)

        with contextlib.ExitStack() as es2:
            GS = es2.enter_context(nc.sbuf_tensor(un("GS_f"), [128, 2, T], F32))
            GI = es2.enter_context(nc.sbuf_tensor(un("GI_f"), [128, 3, T], I16))
            SIG = es2.enter_context(nc.sbuf_tensor(un("SIG_f"), [128, 2, T], F32))
            U16 = es2.enter_context(nc.sbuf_tensor(un("U16_f"), [128, 2, T], F32))
            with nc.Block() as blk:
                ts_done = []
                sig_done = []
                fin_done = []
                red_done = []
                ev0, ev1 = dsEV.n(0), dsEV.n(1)
                for ot in range(OT):
                    pool = (ot % 2 == 1)

                    def s_ld(sy, ot=ot, ev0=ev0, ev1=ev1):
                        if ot == 0:
                            sy.wait_ge(dsEV.h(0), ev0)
                            sy.wait_ge(dsEV.h(1), ev1)
                        if ot >= 2:
                            sy.wait_ge(*ts_done[ot - 2])
                        dsGS.issue(sy.dma_start(out=GS[:, ot % 2, :],
                                                in_=G_d[ot]), ot % 2)
                    blk.sync(s_ld)

                    def v_a(v, ot=ot, mld=dsGS.n(ot % 2),
                            mgi=dsGI.n(ot % 3)):
                        v.wait_ge(dsGS.h(ot % 2), mld)
                        if ot >= 2:
                            v.wait_ge(vD.h, red_done[ot - 2])
                        if ot >= 3:
                            v.wait_ge(aD.h, sig_done[ot - 3])
                            v.wait_ge(*fin_done[ot - 3])
                            v.wait_ge(dsGI.h(ot % 3), mgi)
                        v.tensor_scalar(SIG[:, ot % 2, :], GS[:, ot % 2, :],
                                        COLS[:, iRG:iRG + 1], MAGIC,
                                        op0=ALU.mult, op1=ALU.add)
                        v.tensor_scalar(GI[:, ot % 3, :],
                                               SIG[:, ot % 2, :], MAGIC, None,
                                               op0=ALU.subtract)
                    EW(blk, v_a, pool)
                    ts_done.append(ewm(pool))

                    def s_sig(sc, ot=ot, mts=ts_done[ot]):
                        sc.wait_ge(*mts)
                        sc.activation(SIG[:, ot % 2, :],
                                             GI[:, ot % 3, :], AF.Sigmoid,
                                             scale=COLS[:, iSG:iSG + 1])
                    SCA(blk, s_sig)
                    sig_done.append(aD.n)

                    def g_st(g, ot=ot, mts=ts_done[ot]):
                        g.wait_ge(*mts)
                        dsGI.issue(g.dma_start(out=g16_d[ot],
                                               in_=GI[:, ot % 3, :]), ot % 3)
                    blk.gpsimd(g_st)

                    def v_b(v, ot=ot, ma=aD.n):
                        v.wait_ge(aD.h, ma)
                        v.tensor_scalar(U16[:, ot % 2, :], SIG[:, ot % 2, :],
                                        65536.0, MAGIC, op0=ALU.mult,
                                        op1=ALU.add)
                        v.tensor_scalar(U16[:, ot % 2, :], U16[:, ot % 2, :],
                                        MAGIC, 65535.0, op0=ALU.subtract,
                                        op1=ALU.min)
                        v.tensor_tensor(SIG[:, ot % 2, :], GI[:, ot % 3, :],
                                        U16[:, ot % 2, :], op=ALU.mult)
                    EW(blk, v_b, pool)
                    fin_done.append(ewm(pool))

                    def v_red(v, ot=ot, mfb=fin_done[ot]):
                        v.wait_ge(*mfb)
                        v.tensor_reduce(MXC[:, ot:ot + 1],
                                               SIG[:, ot % 2, :],
                                               op=ALU.max, axis=AX.X,
                                               apply_absolute_value=True)
                    VEC(blk, v_red)
                    red_done.append(vD.n)

            with nc.Block() as blk:
                final_max(blk, OT, jMP)

                def v_s2(v):
                    v.wait_ge(vD.h, vD.n)
                    v.tensor_tensor(SSC[0:1, jT0:jT0 + 1],
                                    SSC[0:1, jMP:jMP + 1],
                                    SSC[0:1, jSG:jSG + 1], op=ALU.mult)
                    v.tensor_scalar(SSC[0:1, jSG2:jSG2 + 1],
                                    SSC[0:1, jT0:jT0 + 1],
                                    float(2.0 ** -16 / QMAX), 1e-12,
                                    op0=ALU.mult, op1=ALU.max)
                    v.reciprocal(SSC[0:1, jRG2:jRG2 + 1],
                                 SSC[0:1, jSG2:jSG2 + 1])
                    v.tensor_tensor(SSC[0:1, jT0:jT0 + 1],
                                    SSC[0:1, jSG:jSG + 1],
                                    SSC[0:1, jRG2:jRG2 + 1], op=ALU.mult)
                    v.tensor_scalar(SSC[0:1, jK2:jK2 + 1],
                                           SSC[0:1, jT0:jT0 + 1],
                                           float(2.0 ** -16), None,
                                           op0=ALU.mult)
                VEC(blk, v_s2)
                bcast_scalar(blk, jK2, iK2)

        with contextlib.ExitStack() as es3:
            US = es3.enter_context(nc.sbuf_tensor(un("US_f"), [128, 2, T], F32))
            GI = es3.enter_context(nc.sbuf_tensor(un("GI3_f"), [128, 2, T], I16))
            UI = es3.enter_context(nc.sbuf_tensor(un("UI3_f"), [128, 2, T], I16))
            G2 = es3.enter_context(nc.sbuf_tensor(un("G23_f"), [128, 2, T], I16))
            SG3 = es3.enter_context(nc.sbuf_tensor(un("SG3_f"), [128, 2, T], F32))
            with nc.Block() as blk:
                a_done = []; b_done = []; sg_done = []; red_done = []
                gi_gates = [dsGI.n(i) for i in range(3)]
                ev_gates = [dsEV.n(0), dsEV.n(1)]
                for ot in range(OT):
                    pool = (ot % 2 == 1)

                    def s_ld(sy, ot=ot):
                        if ot == 0:
                            for i in range(3):
                                sy.wait_ge(dsGI.h(i), gi_gates[i])
                            sy.wait_ge(dsEV.h(0), ev_gates[0])
                            sy.wait_ge(dsEV.h(1), ev_gates[1])
                        if ot >= 2:
                            sy.wait_ge(*b_done[ot - 2])
                            sy.wait_ge(aD.h, sg_done[ot - 2])
                        dsUS.issue(sy.dma_start(out=US[:, ot % 2, :],
                                                in_=U_d[ot]), ot % 2)
                        dsUS.issue(sy.dma_start(out=GI[:, ot % 2, :],
                                                in_=g16_d[ot]), ot % 2)
                    blk.sync(s_ld)

                    def v_a(v, ot=ot, mld=dsUS.n(ot % 2),
                            mui=dsUI.n(ot % 2)):
                        v.wait_ge(dsUS.h(ot % 2), mld)
                        if ot >= 2:
                            v.wait_ge(*b_done[ot - 2])
                            v.wait_ge(vD.h, red_done[ot - 2])
                            v.wait_ge(dsUI.h(ot % 2), mui)
                        v.tensor_scalar(SG3[:, ot % 2, :], US[:, ot % 2, :],
                                        COLS[:, iRUP:iRUP + 1], MAGIC,
                                        op0=ALU.mult, op1=ALU.add)
                        v.tensor_scalar(UI[:, ot % 2, :],
                                               SG3[:, ot % 2, :],
                                               MAGIC, None, op0=ALU.subtract)
                    EW(blk, v_a, pool)
                    a_done.append(ewm(pool))

                    def g_su(g, ot=ot, mva=a_done[ot]):
                        g.wait_ge(*mva)
                        dsUI.issue(g.dma_start(out=u16_d[ot],
                                               in_=UI[:, ot % 2, :]), ot % 2)
                    blk.gpsimd(g_su)

                    def s_sig(sc, ot=ot, mva=a_done[ot]):
                        sc.wait_ge(*mva)
                        sc.activation(SG3[:, ot % 2, :],
                                             GI[:, ot % 2, :], AF.Sigmoid,
                                             scale=COLS[:, iSG:iSG + 1])
                    SCA(blk, s_sig)
                    sg_done.append(aD.n)

                    g2m = {}

                    def v_b(v, ot=ot, ma=aD.n, g2m=g2m, pool=pool,
                            mg2=dsG2.n(ot % 2)):
                        v.wait_ge(aD.h, ma)
                        if ot >= 2:
                            v.wait_ge(dsG2.h(ot % 2), mg2)
                        v.tensor_scalar(SG3[:, ot % 2, :], SG3[:, ot % 2, :],
                                        65536.0, MAGIC, op0=ALU.mult,
                                        op1=ALU.add)
                        v.tensor_scalar(SG3[:, ot % 2, :], SG3[:, ot % 2, :],
                                        MAGIC, 65535.0, op0=ALU.subtract,
                                        op1=ALU.min)
                        v.tensor_tensor(SG3[:, ot % 2, :], GI[:, ot % 2, :],
                                        SG3[:, ot % 2, :], op=ALU.mult)
                        v.tensor_scalar(SG3[:, ot % 2, :], SG3[:, ot % 2, :],
                                        COLS[:, iK2:iK2 + 1], MAGIC,
                                        op0=ALU.mult, op1=ALU.add)
                        v.tensor_scalar(G2[:, ot % 2, :],
                                               SG3[:, ot % 2, :],
                                               MAGIC, None, op0=ALU.subtract)
                        g2m[0] = (gD.h, gD.n) if pool else (vD.h, vD.n)
                        v.tensor_tensor(SG3[:, ot % 2, :], G2[:, ot % 2, :],
                                        UI[:, ot % 2, :], op=ALU.mult)
                    EW(blk, v_b, pool)
                    b_done.append(ewm(pool))

                    def v_red(v, ot=ot, mfb=b_done[ot]):
                        v.wait_ge(*mfb)
                        v.tensor_reduce(MXC[:, ot:ot + 1],
                                               SG3[:, ot % 2, :],
                                               op=ALU.max, axis=AX.X,
                                               apply_absolute_value=True)
                    VEC(blk, v_red)
                    red_done.append(vD.n)

                    def g_sg(g, ot=ot, mg=g2m[0]):
                        g.wait_ge(*mg)
                        dsG2.issue(g.dma_start(out=g216_d[ot],
                                               in_=G2[:, ot % 2, :]), ot % 2)
                    blk.gpsimd(g_sg)

            with nc.Block() as blk:
                final_max(blk, OT, jMP2)

                def v_so(v):
                    v.wait_ge(vD.h, vD.n)
                    v.tensor_tensor(SSC[0:1, jT0:jT0 + 1],
                                    SSC[0:1, jMP2:jMP2 + 1],
                                    SSC[0:1, jSG2:jSG2 + 1], op=ALU.mult)
                    v.tensor_tensor(SSC[0:1, jT0:jT0 + 1],
                                    SSC[0:1, jT0:jT0 + 1],
                                    SSC[0:1, jSUP:jSUP + 1], op=ALU.mult)
                    v.tensor_scalar(SSC[0:1, jSO:jSO + 1],
                                    SSC[0:1, jT0:jT0 + 1],
                                    1.0 / QMAX, 1e-12, op0=ALU.mult,
                                    op1=ALU.max)
                    v.reciprocal(SSC[0:1, jRO:jRO + 1], SSC[0:1, jSO:jSO + 1])
                    v.tensor_tensor(SSC[0:1, jT1:jT1 + 1],
                                    SSC[0:1, jSG2:jSG2 + 1],
                                    SSC[0:1, jSUP:jSUP + 1], op=ALU.mult)
                    v.tensor_tensor(SSC[0:1, jK3:jK3 + 1],
                                    SSC[0:1, jT1:jT1 + 1],
                                    SSC[0:1, jRO:jRO + 1], op=ALU.mult)
                    v.tensor_scalar(SSC[0:1, jT0:jT0 + 1],
                                           SSC[0:1, jSO:jSO + 1], 0.5, None,
                                           op0=ALU.mult)
                VEC(blk, v_so)
                bcast_scalar(blk, jK3, iK3)
                bcast_scalar(blk, jT0, iOS)

        with contextlib.ExitStack() as es4:
            OB = es4.enter_context(nc.sbuf_tensor(un("OB_f"), [128, OT, 1024],
                                                  BF16))
            G2 = es4.enter_context(nc.sbuf_tensor(un("G24_f"), [128, 2, 1024],
                                                  I16))
            UI = es4.enter_context(nc.sbuf_tensor(un("UI4_f"), [128, 2, 1024],
                                                  I16))
            PR = es4.enter_context(nc.sbuf_tensor(un("PR4_f"), [128, 2, 1024],
                                                  F32))
            with nc.Block() as blk:
                mp_wt = pD.n

                mg_wd = qG.n

                def s_wd(sy, mp_wt=mp_wt, mg_wd=mg_wd):
                    sy.wait_ge(pD.h, mp_wt)
                    sy.wait_ge(qS.h, qS.n)
                    sy.wait_ge(qG.h, mg_wd)
                    for o in range(OT):
                        qS.inc(sy.dma_start_transpose(
                            WT[:, o, :], wd_q[:, o * 128:(o + 1) * 128]), 16)
                blk.sync(s_wd)
                m_wd = qS.n

                g4_gates = [dsUI.n(0), dsUI.n(1), dsG2.n(0), dsG2.n(1)]
                ob_done = []
                for th in range(2):
                    th_pgate = pD.n
                    for ot in range(OT):
                        gk2 = th * OT + ot
                        pool = (ot % 2 == 1)

                        def s_l4(sy, th=th, ot=ot, gk2=gk2):
                            if th == 0 and ot == 0:
                                sy.wait_ge(dsUI.h(0), g4_gates[0])
                                sy.wait_ge(dsUI.h(1), g4_gates[1])
                                sy.wait_ge(dsG2.h(0), g4_gates[2])
                                sy.wait_ge(dsG2.h(1), g4_gates[3])
                            if gk2 >= 2:
                                sy.wait_ge(*ob_done[gk2 - 2])
                            sl = slice(th * 1024, (th + 1) * 1024)
                            dsL4.issue(sy.dma_start(out=G2[:, ot % 2, :],
                                                    in_=g216_d[ot, :, sl]),
                                       ot % 2)
                            dsL4.issue(sy.dma_start(out=UI[:, ot % 2, :],
                                                    in_=u16_d[ot, :, sl]),
                                       ot % 2)
                        blk.sync(s_l4)

                        def v_ob(v, ot=ot, mld=dsL4.n(ot % 2),
                                 th_pgate=th_pgate):
                            v.wait_ge(dsL4.h(ot % 2), mld)
                            v.wait_ge(pD.h, th_pgate)
                            v.tensor_tensor(PR[:, ot % 2, :], G2[:, ot % 2, :],
                                            UI[:, ot % 2, :], op=ALU.mult)
                            v.tensor_scalar(PR[:, ot % 2, :], PR[:, ot % 2, :],
                                            COLS[:, iK3:iK3 + 1], MAGIC,
                                            op0=ALU.mult, op1=ALU.add)
                            v.tensor_scalar(
                                OB[:, ot, :], PR[:, ot % 2, :],
                                MAGIC, COLS[:, iOS:iOS + 1],
                                op0=ALU.subtract, op1=ALU.mult)
                        EW(blk, v_ob, pool)
                        ob_done.append(ewm(pool))
                    m_ob = vD.n
                    m_obg = gD.n

                    ev_done = {}
                    for dt in range(CT):
                        for qq in range(2):
                            gk = dt * 2 + qq
                            def t_dn(te, th=th, dt=dt, qq=qq, gk=gk,
                                     m_ob=m_ob, m_obg=m_obg):
                                if gk == 0:
                                    te.wait_ge(qS.h, m_wd)
                                    te.wait_ge(vD.h, m_ob)
                                    te.wait_ge(gD.h, m_obg)
                                if gk >= 2:
                                    te.wait_ge(vD.h, ev_done[gk - 2])
                                tsl = slice(qq * 512, (qq + 1) * 512)
                                for o in range(OT):
                                    mm = te.matmul(
                                        PB[gk % 2][:, :],
                                        WT[:, o, dt * 128:(dt + 1) * 128],
                                        OB[:, o, tsl],
                                        start=(o == 0), stop=(o == OT - 1))
                                pD.inc(mm)
                            blk.tensor(t_dn)
                            mp = pD.n

                            def v_rs(v, th=th, dt=dt, qq=qq, gk=gk, mp=mp):
                                v.wait_ge(pD.h, mp)
                                sl = slice(th * 1024 + qq * 512,
                                           th * 1024 + (qq + 1) * 512)
                                v.tensor_tensor(
                                    XH[:, dt, sl], XH[:, dt, sl],
                                    PB[gk % 2][:, :], op=ALU.add)
                            VEC(blk, v_rs)
                            ev_done[gk] = vD.n
        esF.close()

    with es:
        esXA = contextlib.ExitStack()
        XA = esXA.enter_context(nc.sbuf_tensor("XA", [128, CT, T], F32))
        esW = contextlib.ExitStack()
        WRAW = esW.enter_context(nc.sbuf_tensor("WRAW", [128, 2, I3], F32))
        WQB = esW.enter_context(nc.sbuf_tensor("WQB", [128, 2, I3], BF16))
        WST = esW.enter_context(nc.sbuf_tensor("WST", [128, 2, I3], F32))
        SCM = esW.enter_context(nc.sbuf_tensor("SCM", [128, 2, 96], F32))
        SCR = esW.enter_context(nc.sbuf_tensor("SCR", [128, 2, 96], F32))

        with nc.Block() as blk:
            def s_0(sy):
                qS.inc(sy.dma_start(out=IDN[:], in_=ident_i), 16)
                qS.inc(sy.dma_start(out=MSKC[:], in_=maskS_i), 16)
                qS.inc(sy.dma_start(out=XA[:],
                                    in_=x_own.rearrange("c p t -> p c t")), 16)
            blk.sync(s_0)

            def v_0(v):
                v.memset(ones_col[:], 1.0)
                v.memset(ones_row[:], 1.0)
                v.memset(SSC[:], 0.0)
                v.memset(SSC[0:1, jEPS:jEPS + 1], EPS)
                v.memset(MXC[:], 0.0)
            VEC(blk, v_0)

        for wsrc, wdst, osz, rl in ((w_gate, wg_q, OT, D), (w_up, wu_q, OT, D),
                                    (w_down, wd_q, CT, I3)):
            nb = rl // 32
            with nc.Block() as blk:
                dq_done = []
                mv_blk = vD.n
                mg_blk = gD.n
                wsv = wsrc.rearrange("(k p) i -> k p i", p=128)
                wdv = wdst.rearrange("(k p) i -> k p i", p=128)
                for k in range(osz):
                    kp = k % 2
                    pool = (kp == 1)

                    def s_lw(sy, k=k, rl=rl, kp=kp):
                        if k >= 2:
                            sy.wait_ge(*dq_done[k - 2])
                        else:
                            sy.wait_ge(vD.h, mv_blk)
                            sy.wait_ge(gD.h, mg_blk)
                        dsWR.issue(sy.dma_start(out=WRAW[:, kp, 0:rl],
                                                in_=wsv[k]), kp)
                    blk.sync(s_lw)

                    def v_sc(v, k=k, rl=rl, nb=nb, kp=kp,
                             mwr=dsWR.n(kp)):
                        v.wait_ge(dsWR.h(kp), mwr)
                        wv = WRAW[:, kp, 0:rl].rearrange(
                            "p (b j) -> p b j", j=32)
                        v.tensor_reduce(SCM[:, kp, 0:nb], wv, op=ALU.max,
                                        axis=AX.X, apply_absolute_value=True)
                        v.tensor_scalar(SCM[:, kp, 0:nb], SCM[:, kp, 0:nb],
                                        1.0 / 7.0, 1e-12, op0=ALU.mult,
                                        op1=ALU.max)
                        v.reciprocal(SCR[:, kp, 0:nb], SCM[:, kp, 0:nb])
                    VEC(blk, v_sc)
                    mv_sc = vD.n

                    def v_dq(v, k=k, rl=rl, nb=nb, kp=kp, mv_sc=mv_sc,
                             mwq=dsWQ.n(kp)):
                        v.wait_ge(vD.h, mv_sc)
                        v.wait_ge(dsWQ.h(kp), mwq)
                        wv = WRAW[:, kp, 0:rl].rearrange(
                            "p (b j) -> p b j", j=32)
                        rb = SCR[:, kp, 0:nb].unsqueeze(2).broadcast_to(
                            [128, nb, 32])
                        sb = SCM[:, kp, 0:nb].unsqueeze(2).broadcast_to(
                            [128, nb, 32])
                        sv = WST[:, kp, 0:rl].rearrange(
                            "p (b j) -> p b j", j=32)
                        v.tensor_tensor(sv, wv, rb, op=ALU.mult)
                        v.tensor_scalar(WST[:, kp, 0:rl], WST[:, kp, 0:rl],
                                        MAGIC, MAGIC,
                                        op0=ALU.add, op1=ALU.subtract)
                        qv = WQB[:, kp, 0:rl].rearrange(
                            "p (b j) -> p b j", j=32)
                        v.tensor_tensor(qv, sv, sb, op=ALU.mult)
                    EW(blk, v_dq, pool)
                    dq_done.append(ewm(pool))

                    def g_st(g, k=k, rl=rl, kp=kp, mdq=dq_done[k]):
                        g.wait_ge(*mdq)
                        dsWQ.issue(g.dma_start(out=wdv[k],
                                               in_=WQB[:, kp, 0:rl]),
                                   kp)
                    blk.gpsimd(g_st)

        esW.close()

        emit_ffn(XA)

        if STAGE == 1:
            if os.environ.get("KTAP"):
                dbg_ssc = nc.dram_tensor("dbg_ssc", [1, 64], F32,
                                         kind="ExternalOutput").ap()
                dbg_cols = nc.dram_tensor("dbg_cols", [128, 16], F32,
                                          kind="ExternalOutput").ap()
                dbg_g = nc.dram_tensor("dbg_g", [128, T], F32,
                                       kind="ExternalOutput").ap()
                dbg_u = nc.dram_tensor("dbg_u", [128, T], F32,
                                       kind="ExternalOutput").ap()
                dbg_g16 = nc.dram_tensor("dbg_g16", [128, T], I16,
                                         kind="ExternalOutput").ap()
                dbg_g216 = nc.dram_tensor("dbg_g216", [128, T], I16,
                                          kind="ExternalOutput").ap()
                dbg_u16 = nc.dram_tensor("dbg_u16", [128, T], I16,
                                         kind="ExternalOutput").ap()
                dbg_wgq = nc.dram_tensor("dbg_wgq", [256, D], BF16,
                                         kind="ExternalOutput").ap()
                with nc.Block() as blk:
                    def s_tap(sy, mv=vD.n):
                        sy.wait_ge(vD.h, mv)
                        qS.inc(sy.dma_start(out=dbg_wgq, in_=wg_q[0:256, :]), 16)
                        qS.inc(sy.dma_start(out=dbg_ssc, in_=SSC[:]), 16)
                        qS.inc(sy.dma_start(out=dbg_cols, in_=COLS[:]), 16)
                        qS.inc(sy.dma_start(out=dbg_g, in_=G_d[0]), 16)
                        qS.inc(sy.dma_start(out=dbg_u, in_=U_d[0]), 16)
                        qS.inc(sy.dma_start(out=dbg_g16, in_=g16_d[0]), 16)
                        qS.inc(sy.dma_start(out=dbg_g216, in_=g216_d[0]), 16)
                        qS.inc(sy.dma_start(out=dbg_u16, in_=u16_d[0]), 16)
                    blk.sync(s_tap)
            with nc.Block() as blk:
                def s_out(sy, mv=vD.n):
                    sy.wait_ge(vD.h, mv)
                    qS.inc(sy.dma_start(
                        out=y.rearrange("c p t -> p c t"), in_=XA[:]), 16)
                    sy.wait_ge(qS.h, qS.n)
                blk.sync(s_out)
            esXA.close()
            return nc

        xaow_d = dsc("xaow_d", [CT, 128, T], BF16)
        with contextlib.ExitStack() as esL2:
            XAOW = esL2.enter_context(nc.sbuf_tensor("XAOW", [128, CT, T],
                                                     BF16))
            SQ2 = esL2.enter_context(nc.sbuf_tensor("SQ2", [128, 2, T], F32))
            emit_ln(XA, SQ2, False, XAOW)
            with nc.Block() as blk:
                def s_sp(sy, mv=vD.n):
                    sy.wait_ge(vD.h, mv)
                    qS.inc(sy.dma_start(out=x1_d.rearrange("c p t -> p c t"),
                                        in_=XA[:]), 16)
                    qS.inc(sy.dma_start(out=xaow_d.rearrange("c p t -> p c t"),
                                        in_=XAOW[:]), 16)
                blk.sync(s_sp)
        esXA.close()

        with contextlib.ExitStack() as esB:
            XB = esB.enter_context(nc.sbuf_tensor("XB", [128, CT, T], F32))
            with nc.Block() as blk:
                def s_lb(sy, mq_sp=qS.n):
                    sy.wait_ge(qS.h, mq_sp)
                    qS.inc(sy.dma_start(
                        out=XB[:], in_=x_oth.rearrange("c p t -> p c t")), 16)
                blk.sync(s_lb)
            emit_ffn(XB)
            XAB = esB.enter_context(nc.sbuf_tensor("XAB", [128, CT, T], BF16))
            SQ3 = esB.enter_context(nc.sbuf_tensor("SQ3", [128, 2, T], F32))
            emit_ln(XB, SQ3, False, XAB)

            WK = esB.enter_context(nc.sbuf_tensor("WK", [128, CT, D], BF16))
            WV = esB.enter_context(nc.sbuf_tensor("WV", [128, CT, D], BF16))
            KTMP = esB.enter_context(nc.sbuf_tensor("KTMP", [128, D], BF16))
            ROT = esB.enter_context(nc.sbuf_tensor("ROT", [128, D], BF16))
            VTMP = esB.enter_context(nc.sbuf_tensor("VTMP", [128, D], BF16))
            T0 = esB.enter_context(nc.sbuf_tensor("T0", [128, D], F32))
            T1 = esB.enter_context(nc.sbuf_tensor("T1", [128, D], F32))
            KS = esB.enter_context(nc.sbuf_tensor("KS", [128, D], BF16))
            COSB = esB.enter_context(nc.sbuf_tensor("COSB", [128, NCH, DH],
                                                    F32))
            SINB = esB.enter_context(nc.sbuf_tensor("SINB", [128, NCH, DH],
                                                    F32))
            GCT = esB.enter_context(nc.sbuf_tensor("GCT", [128, 512], F32))
            KSC = esB.enter_context(nc.sbuf_tensor("KSC", [128, D], F32))
            with nc.Block() as blk:
                def s_lw2(sy):
                    qS.inc(sy.dma_start(
                        out=COSB[:],
                        in_=cos_oth.rearrange("(n p) d -> p n d", p=128)), 16)
                    qS.inc(sy.dma_start(
                        out=SINB[:],
                        in_=sinS_oth.rearrange("(n p) d -> p n d", p=128)), 16)
                    qS.inc(sy.dma_start(out=GCT[:], in_=gctab_i), 16)
                    qS.inc(sy.dma_start(out=KSC[:], in_=kscT_i), 16)
                blk.sync(s_lw2)

                def g_lw2(g):
                    qG.inc(g.dma_start(
                        out=WK[:],
                        in_=wkT.rearrange("(c p) d -> p c d", p=128)), 16)
                    qG.inc(g.dma_start(
                        out=WV[:],
                        in_=wvT.rearrange("(c p) d -> p c d", p=128)), 16)
                blk.gpsimd(g_lw2)
                m_w2 = qG.n
                m_c2 = qS.n
                mv_xab = vD.n

                def v_z(v):
                    v.memset(SST[:], 0.0)
                blk.vector(v_z)

                ev_k = []
                for tt in range(NCH):
                    def t_kv(te, tt=tt):
                        if tt == 0:
                            te.wait_ge(qG.h, m_w2)
                            te.wait_ge(qS.h, m_c2)
                            te.wait_ge(vD.h, mv_xab)
                        else:
                            te.wait_ge(aD.h, ev_k[tt - 1][0])
                            te.wait_ge(vD.h, ev_k[tt - 1][1])
                        for dq in range(2):
                            for c in range(CT):
                                mmk = te.matmul(
                                    PB[dq][:, :],
                                    XAB[:, c, tt * 128:(tt + 1) * 128],
                                    WK[:, c, dq * 512:(dq + 1) * 512],
                                    start=(c == 0), stop=(c == CT - 1))
                        for dq in range(2):
                            for c in range(CT):
                                mmv = te.matmul(
                                    PB[2 + dq][:, :],
                                    XAB[:, c, tt * 128:(tt + 1) * 128],
                                    WV[:, c, dq * 512:(dq + 1) * 512],
                                    start=(c == 0), stop=(c == CT - 1))
                        pD.inc(mmv)
                    blk.tensor(t_kv)
                    mpk = pD.n

                    def s_ev(sc, tt=tt, mpk=mpk):
                        sc.wait_ge(pD.h, mpk)
                        sc.copy(KTMP[:, 0:512], PB[0][:, :])
                        sc.copy(KTMP[:, 512:1024], PB[1][:, :])
                        sc.copy(VTMP[:, 0:512], PB[2][:, :])
                        sc.copy(VTMP[:, 512:1024], PB[3][:, :])
                        kv = KTMP[:].rearrange("p (h g j) -> p h g j", g=2,
                                               j=32)
                        rv = ROT[:].rearrange("p (h g j) -> p h g j", g=2,
                                              j=32)
                        sc.copy(rv[:, :, 0, :], kv[:, :, 1, :])
                        sc.copy(rv[:, :, 1, :], kv[:, :, 0, :])
                    SCA(blk, s_ev)
                    mak = aD.n

                    def v_rp(v, tt=tt, mak=mak):
                        v.wait_ge(aD.h, mak)
                        cb = COSB[:, tt, :].unsqueeze(1).broadcast_to(
                            [128, NH, DH])
                        sb = SINB[:, tt, :].unsqueeze(1).broadcast_to(
                            [128, NH, DH])
                        kv = KTMP[:].rearrange("p (h d) -> p h d", d=DH)
                        rv = ROT[:].rearrange("p (h d) -> p h d", d=DH)
                        v.tensor_tensor(T0[:].rearrange("p (h d) -> p h d",
                                                        d=DH), kv, cb,
                                        op=ALU.mult)
                        v.tensor_tensor(T1[:].rearrange("p (h d) -> p h d",
                                                        d=DH), rv, sb,
                                        op=ALU.mult)
                        v.tensor_tensor(KTMP[:], T0[:], T1[:], op=ALU.add)
                        v.tensor_tensor(KS[:], KTMP[:], KSC[:], op=ALU.mult)
                    VEC(blk, v_rp)
                    mvk = vD.n

                    def t_st(te, tt=tt, mvk=mvk):
                        te.wait_ge(vD.h, mvk)
                        for h in range(NH):
                            base = 64 * (h % 2)
                            col = (h // 2) * DH
                            mm = te.matmul(
                                PB[4][base:base + 64, col:col + DH],
                                KS[:, h * DH:(h + 1) * DH],
                                VTMP[:, h * DH:(h + 1) * DH],
                                start=True, stop=True)
                        pD.inc(mm)
                    blk.tensor(t_st)
                    mps = pD.n

                    def v_su(v, tt=tt, mps=mps):
                        v.wait_ge(pD.h, mps)
                        v.tensor_tensor(SST[:], SST[:], GCT[:], op=ALU.mult)
                        v.tensor_tensor(SST[:], SST[:], PB[4][:, :],
                                        op=ALU.add)
                    VEC(blk, v_su)
                    ev_k.append((aD.n, vD.n))

            with nc.Block() as blk:
                def v_mk(v):
                    v.wait_ge(vD.h, vD.n)
                    v.tensor_scalar(SST[:], SST[:], MSKC[:, 0:1], None,
                                    op0=ALU.mult)
                VEC(blk, v_mk)

                def s_sb(sc, mv=vD.n):
                    sc.wait_ge(vD.h, mv)
                    sc.copy(SSB[:], SST[:])
                SCA(blk, s_sb)

            if STAGE < 1.7:
                with nc.Block() as blk:
                    def s_o15(sy, mv=vD.n, ma=aD.n):
                        sy.wait_ge(vD.h, mv)
                        sy.wait_ge(aD.h, ma)
                        qS.inc(sy.dma_start(
                            out=y.rearrange("c p t -> p c t"), in_=XB[:]), 16)
                        sy.wait_ge(qS.h, qS.n)
                    blk.sync(s_o15)
                return nc

        esAT = contextlib.ExitStack()
        KT = esAT.enter_context(nc.sbuf_tensor("KT", [128, NCH, D], BF16))
        VT = esAT.enter_context(nc.sbuf_tensor("VT", [128, NCH, D], BF16))
        GSIL = esAT.enter_context(nc.sbuf_tensor("GSIL", [128, CT, T], BF16))
        with contextlib.ExitStack() as esA2:
            XAP = esA2.enter_context(nc.sbuf_tensor("XAP", [128, CT, T],
                                                    BF16))
            WB1 = esA2.enter_context(nc.sbuf_tensor("WB1", [128, CT, D],
                                                    BF16))
            KTMP = esA2.enter_context(nc.sbuf_tensor("KTMP2", [128, D], BF16))
            ROT = esA2.enter_context(nc.sbuf_tensor("ROT2", [128, D], BF16))
            T0 = esA2.enter_context(nc.sbuf_tensor("T02", [128, D], F32))
            T1 = esA2.enter_context(nc.sbuf_tensor("T12", [128, D], F32))
            COSB = esA2.enter_context(nc.sbuf_tensor("COSB2", [128, NCH, DH],
                                                     F32))
            SINB = esA2.enter_context(nc.sbuf_tensor("SINB2", [128, NCH, DH],
                                                     F32))

            with nc.Block() as blk:
                def s_lo(sy):
                    qS.inc(sy.dma_start(
                        out=XAP[:], in_=xaow_d.rearrange("c p t -> p c t")),
                        16)
                    qS.inc(sy.dma_start(
                        out=COSB[:],
                        in_=cos_own.rearrange("(n p) d -> p n d", p=128)), 16)
                    qS.inc(sy.dma_start(
                        out=SINB[:],
                        in_=sinS_own.rearrange("(n p) d -> p n d", p=128)), 16)
                blk.sync(s_lo)

            def proj_tm(wsrc, dst, rope):
                with nc.Block() as blk:
                    mp_prev = pD.n

                    def g_w(g, wsrc=wsrc, mp_prev=mp_prev):
                        g.wait_ge(pD.h, mp_prev)
                        qG.inc(g.dma_start(
                            out=WB1[:],
                            in_=wsrc.rearrange("(c p) d -> p c d", p=128)),
                            16)
                    blk.gpsimd(g_w)
                    m_w = qG.n
                    m_ld = qS.n
                    ev_m = []
                    for tt in range(NCH):
                        def t_p(te, tt=tt):
                            if tt == 0:
                                te.wait_ge(qG.h, m_w)
                                te.wait_ge(qS.h, m_ld)
                            if tt >= 2:
                                te.wait_ge(vD.h, ev_m[tt - 2][1])
                                te.wait_ge(aD.h, ev_m[tt - 2][0])
                            for dq in range(2):
                                for c in range(CT):
                                    mm = te.matmul(
                                        PB[2 * (tt % 2) + dq][:, :],
                                        XAP[:, c, tt * 128:(tt + 1) * 128],
                                        WB1[:, c, dq * 512:(dq + 1) * 512],
                                        start=(c == 0), stop=(c == CT - 1))
                            pD.inc(mm)
                        blk.tensor(t_p)
                        mpk = pD.n

                        def s_e(sc, tt=tt, mpk=mpk, rope=rope, dst=dst,
                                m_ld=m_ld):
                            sc.wait_ge(pD.h, mpk)
                            if tt == 0:
                                sc.wait_ge(qS.h, m_ld)
                            if rope and tt >= 1:
                                sc.wait_ge(vD.h, ev_m[tt - 1][1])
                            tgt = KTMP if rope else dst
                            off = 0 if rope else tt
                            if rope:
                                sc.copy(KTMP[:, 0:512],
                                        PB[2 * (tt % 2)][:, :])
                                sc.copy(KTMP[:, 512:1024],
                                        PB[2 * (tt % 2) + 1][:, :])
                                kv = KTMP[:].rearrange(
                                    "p (h g j) -> p h g j", g=2, j=32)
                                rv = ROT[:].rearrange(
                                    "p (h g j) -> p h g j", g=2, j=32)
                                sc.copy(rv[:, :, 0, :], kv[:, :, 1, :])
                                sc.copy(rv[:, :, 1, :], kv[:, :, 0, :])
                            else:
                                sc.copy(dst[:, tt, 0:512],
                                        PB[2 * (tt % 2)][:, :])
                                sc.copy(dst[:, tt, 512:1024],
                                        PB[2 * (tt % 2) + 1][:, :])
                        SCA(blk, s_e)
                        mak = aD.n

                        def v_r(v, tt=tt, mak=mak, rope=rope, dst=dst):
                            v.wait_ge(aD.h, mak)
                            if rope:
                                cb = COSB[:, tt, :].unsqueeze(1).broadcast_to(
                                    [128, NH, DH])
                                sb = SINB[:, tt, :].unsqueeze(1).broadcast_to(
                                    [128, NH, DH])
                                kv = KTMP[:].rearrange("p (h d) -> p h d",
                                                       d=DH)
                                rv = ROT[:].rearrange("p (h d) -> p h d",
                                                      d=DH)
                                v.tensor_tensor(
                                    T0[:].rearrange("p (h d) -> p h d", d=DH),
                                    kv, cb, op=ALU.mult)
                                v.tensor_tensor(
                                    T1[:].rearrange("p (h d) -> p h d", d=DH),
                                    rv, sb, op=ALU.mult)
                                v.tensor_tensor(dst[:, tt, :], T0[:], T1[:],
                                                op=ALU.add)
                        VEC(blk, v_r)
                        ev_m.append((aD.n, vD.n))

            proj_tm(wqT, KT, True)
            with nc.Block() as blk:
                def s_bq(sy, mv=vD.n):
                    sy.wait_ge(vD.h, mv)
                    qS.inc(sy.dma_start(
                        out=trb_d.rearrange("(n p) d -> p n d", p=128),
                        in_=KT[:]), 16)
                blk.sync(s_bq)
            proj_tm(wkT, KT, True)
            proj_tm(wvT, VT, False)

            with nc.Block() as blk:
                mp_prev = pD.n

                def g_wg(g, mp_prev=mp_prev):
                    g.wait_ge(pD.h, mp_prev)
                    qG.inc(g.dma_start(
                        out=WB1[:],
                        in_=wgTT.rearrange("(c p) d -> p c d", p=128)), 16)
                blk.gpsimd(g_wg)
                m_w = qG.n
                ev_g = []
                gk = 0
                for dg in range(CT):
                    for q in range(4):
                        def t_gp(te, dg=dg, q=q, gk=gk):
                            if gk == 0:
                                te.wait_ge(qG.h, m_w)
                            if gk >= 2:
                                te.wait_ge(aD.h, ev_g[gk - 2])
                            for c in range(CT):
                                mm = te.matmul(
                                    PB[4 + gk % 2][:, :],
                                    WB1[:, c, dg * 128:(dg + 1) * 128],
                                    XAP[:, c, q * 512:(q + 1) * 512],
                                    start=(c == 0), stop=(c == CT - 1))
                            pD.inc(mm)
                        blk.tensor(t_gp)
                        mpg = pD.n

                        def s_gs(sc, dg=dg, q=q, gk=gk, mpg=mpg):
                            sc.wait_ge(pD.h, mpg)
                            sc.activation(
                                GSIL[:, dg, q * 512:(q + 1) * 512],
                                PB[4 + gk % 2][:, :], AF.Silu)
                        SCA(blk, s_gs)
                        ev_g.append(aD.n)
                        gk += 1

            with nc.Block() as blk:
                def s_bn(sy, mv=vD.n, ma=aD.n):
                    sy.wait_ge(vD.h, mv)
                    sy.wait_ge(aD.h, ma)
                    qS.inc(sy.dma_start(
                        out=trb2_d.rearrange("(n p) d -> p n d", p=128),
                        in_=KT[:]), 16)
                blk.sync(s_bn)

            if STAGE < 1.8:
                with nc.Block() as blk:
                    def g_o17(g, mv=vD.n, ma=aD.n):
                        g.wait_ge(vD.h, mv)
                        g.wait_ge(aD.h, ma)
                        qG.inc(g.dma_start(
                            out=y.rearrange("c p t -> p c t"),
                            in_=GSIL[:]), 16)
                        g.wait_ge(qG.h, qG.n)
                    blk.gpsimd(g_o17)
                esA2.close()
                esAT.close()
                return nc

        if STAGE == 1.85:
            with contextlib.ExitStack() as esT:
                QF85 = esT.enter_context(
                    nc.sbuf_tensor("QF85", [128, CT, T], BF16))
                KF85 = esT.enter_context(
                    nc.sbuf_tensor("KF85", [128, CT, T], BF16))
                YF85 = esT.enter_context(
                    nc.sbuf_tensor("YF85", [128, T], F32))
                with nc.Block() as blk:
                    def s_tr85(sy):
                        sy.wait_ge(qS.h, qS.n)
                        for i in range(CT):
                            qS.inc(sy.dma_start_transpose(
                                QF85[:, i, :],
                                trb_d[:, i * 128:(i + 1) * 128]), 16)
                        for i in range(CT):
                            qS.inc(sy.dma_start_transpose(
                                KF85[:, i, :],
                                trb2_d[:, i * 128:(i + 1) * 128]), 16)
                    blk.sync(s_tr85)
                    m_tr85 = qS.n
                    srcs = [(QF85[:, 0, :], T), (QF85[:, 7, :], T),
                            (KF85[:, 0, :], T), (KF85[:, 7, :], T),
                            (SST[:], 512), (SSB[:], 512),
                            (KT[:, 0, :], D), (VT[:, 0, :], D)]
                    for i85, (src, w85) in enumerate(srcs):
                        def v_c85(v, i85=i85, src=src, w85=w85,
                                  m_tr85=m_tr85, mq=qS.n, ma=aD.n):
                            v.wait_ge(qS.h, m_tr85 if i85 == 0 else mq)
                            if i85 == 0:
                                v.wait_ge(aD.h, ma)
                            if w85 < T:
                                v.memset(YF85[:], 0.0)
                            v.tensor_copy(YF85[:, 0:w85], src)
                        VEC(blk, v_c85)
                        mv85 = vD.n

                        def s_s85(sy, i85=i85, mv85=mv85):
                            sy.wait_ge(vD.h, mv85)
                            qS.inc(sy.dma_start(out=y[i85], in_=YF85[:]), 16)
                            if i85 == 7:
                                sy.wait_ge(qS.h, qS.n)
                        blk.sync(s_s85)
            esAT.close()
            return nc

        with contextlib.ExitStack() as esA3:
            QF = esA3.enter_context(nc.sbuf_tensor("QF", [128, CT, T], BF16))
            KF = esA3.enter_context(nc.sbuf_tensor("KF", [128, CT, T], BF16))
            DCT = esA3.enter_context(nc.sbuf_tensor("DCT", [128, NH * CK],
                                                    F32))
            QSC = esA3.enter_context(nc.sbuf_tensor("QSC", [128, CT * CK],
                                                    F32))
            KSC2 = esA3.enter_context(nc.sbuf_tensor("KSC2", [128, D], F32))
            GCT2 = esA3.enter_context(nc.sbuf_tensor("GCT2", [128, 512], F32))
            QS = esA3.enter_context(nc.sbuf_tensor("QS", [128, CT, CK], BF16))
            KS2 = esA3.enter_context(nc.sbuf_tensor("KS2", [128, D], BF16))
            ATS = esA3.enter_context(nc.sbuf_tensor("ATS", [128, 4, 512],
                                                    BF16))
            GTD = esA3.enter_context(nc.sbuf_tensor("GTD", [128, 2, 512],
                                                    BF16))

            with nc.Block() as blk:
                def s_tr(sy):
                    sy.wait_ge(qS.h, qS.n)
                    for i in range(CT):
                        qS.inc(sy.dma_start_transpose(
                            QF[:, i, :], trb_d[:, i * 128:(i + 1) * 128]), 16)
                    for i in range(CT):
                        qS.inc(sy.dma_start_transpose(
                            KF[:, i, :], trb2_d[:, i * 128:(i + 1) * 128]),
                            16)
                    qS.inc(sy.dma_start(out=DCT[:], in_=decayT_i), 16)
                    qS.inc(sy.dma_start(out=QSC[:], in_=qscT_i), 16)
                    qS.inc(sy.dma_start(out=KSC2[:], in_=kscT_i), 16)
                    qS.inc(sy.dma_start(out=GCT2[:], in_=gctab_i), 16)
                blk.sync(s_tr)
                m_tr = qS.n

                prev = {(-1, k): 0 for k in
                        ("mm", "ats", "su", "ssb", "gated", "gst")}
                QSCv = QSC[:].rearrange("p (c k) -> p c k", c=CT)
                GSILv = GSIL
                NSC = int(os.environ.get("KNSC", str(NCH)))
                SCP = int(os.environ.get("KSCPART", "5"))
                for n_ in range(NSC):
                    def v_qk(v, n_=n_):
                        if n_ == 0:
                            v.wait_ge(qS.h, m_tr)
                        v.wait_ge(pD.h, prev[(n_ - 1, "mm")])
                        v.tensor_tensor(QS[:],
                                        QF[:, :, n_ * 128:(n_ + 1) * 128],
                                        QSCv, op=ALU.mult)
                        v.tensor_tensor(KS2[:], KT[:, n_, :], KSC2[:],
                                        op=ALU.mult)
                    VEC(blk, v_qk)
                    prev[(n_, "qk")] = vD.n

                    if SCP < 2:
                        prev[(n_, "ats")] = vD.n
                        prev[(n_, "mm")] = pD.n
                        prev[(n_, "su")] = vD.n
                        prev[(n_, "ssb")] = aD.n
                        prev[(n_, "gated")] = vD.n
                        continue
                    at_m = {}
                    NBG = int(os.environ.get("KNBG", "4"))
                    for bg in range(NBG):
                        def t_sc(te, n_=n_, bg=bg):
                            if bg == 0:
                                te.wait_ge(vD.h, prev[(n_, "qk")])
                                te.wait_ge(pD.h, prev[(n_ - 1, "mm")])
                            if bg >= 2:
                                te.wait_ge(vD.h, at_m[bg - 2])
                            for hh in range(4):
                                h = bg * 4 + hh
                                base = 64 * (h % 2)
                                for _ in range(4):
                                    te.nop()
                                mm = te.matmul(
                                    PB[bg % 2][:, hh * 128:(hh + 1) * 128],
                                    KF[base:base + 64, h // 2,
                                       n_ * 128:(n_ + 1) * 128],
                                    QF[base:base + 64, h // 2,
                                       n_ * 128:(n_ + 1) * 128],
                                    start=True, stop=True)
                            pD.inc(mm)
                        blk.tensor(t_sc)
                        msc = pD.n

                        def v_at(v, n_=n_, bg=bg, msc=msc):
                            v.wait_ge(pD.h, msc)
                            v.tensor_tensor(
                                ATS[:, bg, :], PB[bg % 2][:, :],
                                DCT[:, bg * 512:(bg + 1) * 512], op=ALU.mult)
                        VEC(blk, v_at)
                        at_m[bg] = vD.n
                    prev[(n_, "ats")] = vD.n
                    if NBG < 4:
                        prev[(n_, "mm")] = pD.n
                        prev[(n_, "su")] = vD.n
                        prev[(n_, "ssb")] = aD.n
                        prev[(n_, "gated")] = vD.n
                        continue

                    if SCP < 3:
                        prev[(n_, "mm")] = pD.n
                        prev[(n_, "su")] = vD.n
                        prev[(n_, "ssb")] = aD.n
                        prev[(n_, "gated")] = vD.n
                        continue

                    def t_ov(te, n_=n_):
                        te.wait_ge(vD.h, prev[(n_, "ats")])
                        te.wait_ge(vD.h, prev[(n_ - 1, "gated")])
                        te.wait_ge(vD.h, prev[(n_ - 1, "su")])
                        te.wait_ge(aD.h, prev[(n_ - 1, "ssb")])
                        for _ in range(4):
                            te.nop()
                        for h in range(NH):
                            base = 64 * (h % 2)
                            bank = 2 + (h // 8)
                            col = 128 * ((h // 2) % 4)
                            te.matmul(
                                PB[bank][base:base + 64, col:col + 128],
                                VT[:, n_, h * DH:(h + 1) * DH],
                                ATS[:, h // 4,
                                    (h % 4) * 128:(h % 4 + 1) * 128],
                                start=True, stop=False)
                            for _ in range(4 if base else 0):
                                te.nop()
                            te.matmul(
                                PB[bank][base:base + 64, col:col + 128],
                                SSB[base:base + 64,
                                    (h // 2) * DH:(h // 2) * DH + DH],
                                QS[base:base + 64, h // 2, :],
                                start=False, stop=True)
                            for _ in range(4 if base else 0):
                                te.nop()
                        for h in range(NH):
                            base = 64 * (h % 2)
                            col = (h // 2) * DH
                            mm = te.matmul(
                                PB[4][base:base + 64, col:col + DH],
                                KS2[:, h * DH:(h + 1) * DH],
                                VT[:, n_, h * DH:(h + 1) * DH],
                                start=True, stop=True)
                        pD.inc(mm)
                    blk.tensor(t_ov)
                    prev[(n_, "mm")] = pD.n

                    if SCP < 4:
                        prev[(n_, "su")] = vD.n
                        prev[(n_, "ssb")] = aD.n
                        prev[(n_, "gated")] = vD.n
                        continue

                    def v_su(v, n_=n_):
                        v.wait_ge(pD.h, prev[(n_, "mm")])
                        v.tensor_tensor(SST[:], SST[:], GCT2[:], op=ALU.mult)
                        v.tensor_tensor(SST[:], SST[:], PB[4][:, :],
                                        op=ALU.add)
                    VEC(blk, v_su)
                    prev[(n_, "su")] = vD.n

                    def s_sb2(sc, n_=n_):
                        sc.wait_ge(vD.h, prev[(n_, "su")])
                        sc.copy(SSB[:], SST[:])
                    SCA(blk, s_sb2)
                    prev[(n_, "ssb")] = aD.n

                    if SCP < 5:
                        prev[(n_, "gated")] = vD.n
                        continue

                    gt_m = {}
                    for b2 in range(2):
                        def v_gt(v, n_=n_, b2=b2, mgt=dsGT.n(b2)):
                            v.wait_ge(pD.h, prev[(n_, "mm")])
                            v.wait_ge(dsGT.h(b2), mgt)
                            v.tensor_tensor(
                                GTD[:, b2, :].rearrange("p (c k) -> p c k",
                                                        k=128),
                                PB[2 + b2][:, :].rearrange(
                                    "p (c k) -> p c k", k=128),
                                GSILv[:, 4 * b2:4 * b2 + 4,
                                      n_ * 128:(n_ + 1) * 128], op=ALU.mult)
                        VEC(blk, v_gt)
                        gt_m[b2] = vD.n
                    prev[(n_, "gated")] = vD.n

                    def g_gst(g, n_=n_, gt_m=dict(gt_m)):
                        for b2 in range(2):
                            g.wait_ge(vD.h, gt_m[b2])
                            dsGT.issue(g.dma_start(
                                out=gated_d[4 * b2:4 * b2 + 4, :,
                                            n_ * 128:(n_ + 1) * 128
                                            ].rearrange("c p k -> p c k"),
                                in_=GTD[:, b2, :].rearrange(
                                    "p (c k) -> p c k", k=128)), b2)
                    blk.gpsimd(g_gst)
                    prev[(n_, "gst")] = 0
        esAT.close()

        if STAGE < 2:
            dbg18 = nc.dram_tensor("dbg18", [CT, 128, T], I16,
                                   kind="ExternalOutput").ap()
            with (
                nc.sbuf_tensor("Y18F", [128, T], F32) as Y18F,
                nc.sbuf_tensor("G18B", [128, T], BF16) as G18B,
                nc.sbuf_tensor("N18I", [128, T], I16) as N18I,
                nc.Block() as blk,
            ):
                def v_z18(v):
                    v.memset(Y18F[:], 0.0)
                VEC(blk, v_z18)
                mvz = vD.n

                def s_y18(sy, mvz=mvz):
                    sy.wait_ge(vD.h, mvz)
                    for c18 in range(CT):
                        qS.inc(sy.dma_start(out=y[c18], in_=Y18F[:]), 16)
                blk.sync(s_y18)
                for c18 in range(CT):
                    def s_l18(sy, c18=c18, mg0=dsGT.n(0), mg1=dsGT.n(1),
                              mvp=vD.n):
                        if c18 == 0:
                            sy.wait_ge(dsGT.h(0), mg0)
                            sy.wait_ge(dsGT.h(1), mg1)
                        sy.wait_ge(vD.h, mvp)
                        qS.inc(sy.dma_start(out=G18B[:], in_=gated_d[c18]),
                               16)
                    blk.sync(s_l18)
                    mq18 = qS.n

                    def v_n18(v, mq18=mq18):
                        v.wait_ge(qS.h, mq18)
                        v.tensor_tensor(N18I[:], G18B[:], G18B[:],
                                        op=ALU.is_equal)
                    VEC(blk, v_n18)
                    mv18 = vD.n

                    def s_s18(sy, c18=c18, mv18=mv18):
                        sy.wait_ge(vD.h, mv18)
                        qS.inc(sy.dma_start(out=dbg18[c18], in_=N18I[:]), 16)
                        if c18 == CT - 1:
                            sy.wait_ge(qS.h, qS.n)
                    blk.sync(s_s18)
            return nc

        esX2 = contextlib.ExitStack()
        X2 = esX2.enter_context(nc.sbuf_tensor("X2", [128, CT, T], F32))
        with contextlib.ExitStack() as esA4:
            GSTF = esA4.enter_context(nc.sbuf_tensor("GSTF", [128, CT, T],
                                                     BF16))
            WO = esA4.enter_context(nc.sbuf_tensor("WO", [128, CT, D], BF16))
            X1S = esA4.enter_context(nc.sbuf_tensor("X1S", [128, 2, T], F32))
            with nc.Block() as blk:
                gt_gates = [dsGT.n(0), dsGT.n(1)]
                mq_x1 = qS.n

                def s_lg(sy, gt_gates=gt_gates, mq_x1=mq_x1):
                    sy.wait_ge(dsGT.h(0), gt_gates[0])
                    sy.wait_ge(dsGT.h(1), gt_gates[1])
                    sy.wait_ge(qS.h, mq_x1)
                    qS.inc(sy.dma_start(
                        out=GSTF[:],
                        in_=gated_d.rearrange("c p t -> p c t")), 16)
                blk.sync(s_lg)

                def g_lw(g):
                    qG.inc(g.dma_start(
                        out=WO[:],
                        in_=woT.rearrange("(c p) d -> p c d", p=128)), 16)
                blk.gpsimd(g_lw)
                m_lg = qS.n
                m_lw = qG.n
                ev_w = {-1: vD.n, -2: vD.n}
                for dt in range(CT):
                    def s_lx(sy, dt=dt):
                        sy.wait_ge(vD.h, ev_w[dt - 2])
                        dsX1.issue(sy.dma_start(out=X1S[:, dt % 2, :],
                                                in_=x1_d[dt]), dt % 2)
                    blk.sync(s_lx)
                    m_lx = dsX1.n(dt % 2)
                    for q in range(4):
                        gk = dt * 4 + q

                        def t_wo(te, dt=dt, q=q, gk=gk, m_lx=m_lx):
                            if gk == 0:
                                te.wait_ge(qS.h, m_lg)
                                te.wait_ge(qG.h, m_lw)
                            if gk >= 2:
                                te.wait_ge(vD.h, ev_w[-2] if gk < 2 else
                                           ev_w.get(("g", gk - 2), 0))
                            for c in range(CT):
                                mm = te.matmul(
                                    PB[gk % 2][:, :],
                                    WO[:, c, dt * 128:(dt + 1) * 128],
                                    GSTF[:, c, q * 512:(q + 1) * 512],
                                    start=(c == 0), stop=(c == CT - 1))
                            pD.inc(mm)
                        blk.tensor(t_wo)
                        mpw = pD.n

                        def v_x2(v, dt=dt, q=q, mpw=mpw, m_lx=m_lx):
                            v.wait_ge(pD.h, mpw)
                            v.wait_ge(dsX1.h(dt % 2), m_lx)
                            v.tensor_tensor(
                                X2[:, dt, q * 512:(q + 1) * 512],
                                X1S[:, dt % 2, q * 512:(q + 1) * 512],
                                PB[gk % 2][:, :], op=ALU.add)
                        VEC(blk, v_x2)
                        ev_w[("g", gk)] = vD.n
                    ev_w[dt] = vD.n

        if STAGE >= 3:
            emit_ffn(X2)

        with nc.Block() as blk:
            def s_out(sy, mv=vD.n):
                sy.wait_ge(vD.h, mv)
                qS.inc(sy.dma_start(
                    out=y.rearrange("c p t -> p c t"), in_=X2[:]), 16)
                sy.wait_ge(qS.h, qS.n)
            blk.sync(s_out)
        esX2.close()



    return nc


_CONSTS = None


def kernel(**inputs):
    global _CONSTS
    if _CONSTS is None:
        _CONSTS = _host_consts()
    cns = _CONSTS
    x = np.ascontiguousarray(inputs["L0_attn_x"])
    xT = np.ascontiguousarray(np.swapaxes(x, 1, 2))

    def wT(a):
        return np.ascontiguousarray(np.asarray(a).T)

    base = dict(
        wqT=wT(inputs["L0_attn_wq"]), wkT=wT(inputs["L0_attn_wk"]),
        wvT=wT(inputs["L0_attn_wv"]), wgTT=wT(inputs["L0_attn_wg"]),
        woT=wT(inputs["L0_attn_wo"]),
        w_gate=np.ascontiguousarray(inputs["L0_ffn_w_gate"]),
        w_up=np.ascontiguousarray(inputs["L0_ffn_w_up"]),
        w_down=np.ascontiguousarray(inputs["L0_ffn_w_down"]),
        decayT=cns["decayT"], qscT=cns["qscT"], kscT=cns["kscT"],
        gctab=cns["gctab"], ident=cns["ident"],
    )
    in_maps = []
    for c in range(8):
        b, h = c // 2, c % 2
        own = slice(h * T, (h + 1) * T)
        oth = slice((1 - h) * T, (2 - h) * T)
        m = dict(base)
        m["x_own"] = np.ascontiguousarray(
            xT[b, :, own].reshape(CT, 128, T))
        m["x_oth"] = np.ascontiguousarray(
            xT[b, :, oth].reshape(CT, 128, T))
        m["cos_own"] = np.ascontiguousarray(cns["cos_t"][own])
        m["sinS_own"] = np.ascontiguousarray(cns["sinS_t"][own])
        m["cos_oth"] = np.ascontiguousarray(cns["cos_t"][oth])
        m["sinS_oth"] = np.ascontiguousarray(cns["sinS_t"][oth])
        m["maskS"] = np.full((128, 1), float(h), np.float32)
        in_maps.append(m)

    nc = build_nc()
    res = run_bass_kernel_spmd(nc, in_maps, core_ids=list(range(8)))
    out = np.empty((B, S, D), np.float32)
    for c in range(8):
        b, h = c // 2, c % 2
        yv = res.results[c]["y"].reshape(D, T)
        out[b, h * T:(h + 1) * T, :] = yv.T
    return out


# version 08
# speedup vs baseline: 1.0057x; 1.0057x over previous
import os
import contextlib
import numpy as np
import ml_dtypes
import concourse.bass as bass
import concourse.mybir as mybir
from concourse.bass_utils import run_bass_kernel_spmd

F32 = mybir.dt.float32
BF16 = mybir.dt.bfloat16
I16 = mybir.dt.int16
AF = mybir.ActivationFunctionType
ALU = mybir.AluOpType
AX = mybir.AxisListType

MAGIC = float(3 * 2 ** 22)
B, S, D, I3 = 4, 4096, 1024, 3072
T = S // 2
NH, DH, CK = 16, 64, 128
NCH = T // CK
CT = D // 128
OT = I3 // 128
QMAX = 32767.0
EPS = 1e-5
STAGE = float(os.environ.get("KSTAGE", "3"))


def _host_consts():
    f64 = np.float64
    inv = 1.0 / (10000.0 ** (np.arange(0, DH, 2, dtype=f64) / DH))
    t = np.arange(S, dtype=f64)
    fr = np.outer(t, inv)
    cos_t = np.concatenate([np.cos(fr), np.cos(fr)], axis=1).astype(np.float32)
    sn = np.sin(fr)
    sinS_t = np.concatenate([-sn, sn], axis=1).astype(np.float32)
    gam = 1.0 - np.exp2(-5.0 - np.arange(NH, dtype=f64))
    i = np.arange(CK)
    dm = i[None, :] - i[:, None]
    low = (dm >= 0).astype(f64)
    decayT = np.zeros((CK, NH * CK), np.float32)
    for h in range(NH):
        decayT[:, h * CK:(h + 1) * CK] = (
            np.power(gam[h], np.maximum(dm, 0)) * low / 8.0).astype(np.float32)
    qscT = np.zeros((128, CT * CK), np.float32)
    for m in range(CT):
        for hf in range(2):
            qscT[hf * 64:(hf + 1) * 64, m * CK:(m + 1) * CK] = (
                np.power(gam[2 * m + hf], i + 1) / 8.0).astype(np.float32)[None, :]
    kscT = np.zeros((CK, D), np.float32)
    for h in range(NH):
        kscT[:, h * DH:(h + 1) * DH] = np.power(
            gam[h], CK - 1 - i).astype(np.float32)[:, None]
    gctab = np.zeros((128, 512), np.float32)
    for m in range(CT):
        for hf in range(2):
            gctab[hf * 64:(hf + 1) * 64, m * DH:(m + 1) * DH] = float(
                np.power(gam[2 * m + hf], CK))
    ident = np.eye(128, dtype=np.float32)
    return dict(cos_t=cos_t, sinS_t=sinS_t, decayT=decayT, qscT=qscT,
                kscT=kscT, gctab=gctab, ident=ident)


class Sem:
    def __init__(self, es, nc, name):
        self.h = es.enter_context(nc.semaphore(name))
        self.n = 0

    def inc(self, instr, k=1):
        instr.then_inc(self.h, k)
        self.n += k
        return self.n


_FENCED = {"tensor_tensor", "tensor_scalar", "tensor_reduce", "reciprocal",
           "tensor_copy", "memset", "activation", "copy",
           "scalar_tensor_tensor", "mul", "add"}


class DSlot:

    def __init__(self, es, nc, name, nslots):
        self.sems = [Sem(es, nc, "%s%d" % (name, i)) for i in range(nslots)]

    def issue(self, instr, slot):
        return self.sems[slot].inc(instr, 16)

    def wait(self, eng, slot):
        eng.wait_ge(self.sems[slot].h, self.sems[slot].n)

    def n(self, slot):
        return self.sems[slot].n

    def h(self, slot):
        return self.sems[slot].h


class FE:

    def __init__(self, eng, sem):
        self._e = eng
        self._s = sem

    def wait_ge(self, h, n):
        self._e.wait_ge(h, n)

    def dma_start(self, *a, **k):
        return self._e.dma_start(*a, **k)

    def dma_start_transpose(self, *a, **k):
        return self._e.dma_start_transpose(*a, **k)

    def __getattr__(self, name):
        fn = getattr(self._e, name)
        if name in _FENCED:
            s = self._s

            def wrapped(*a, **k):
                ins = fn(*a, **k)
                s.n += 1
                ins.then_inc(s.h, 1)
                self._e.wait_ge(s.h, s.n)
                return ins
            return wrapped
        return fn


def build_nc():
    nc = bass.Bass(target_bir_lowering=False)

    def din(name, shape, dt=F32):
        return nc.dram_tensor(name, shape, dt, kind="ExternalInput").ap()

    x_own = din("x_own", [CT, 128, T])
    x_oth = din("x_oth", [CT, 128, T])
    wqT = din("wqT", [D, D]); wkT = din("wkT", [D, D]); wvT = din("wvT", [D, D])
    wgTT = din("wgTT", [D, D]); woT = din("woT", [D, D])
    w_gate = din("w_gate", [I3, D]); w_up = din("w_up", [I3, D])
    w_down = din("w_down", [D, I3])
    cos_own = din("cos_own", [T, DH]); sinS_own = din("sinS_own", [T, DH])
    cos_oth = din("cos_oth", [T, DH]); sinS_oth = din("sinS_oth", [T, DH])
    decayT_i = din("decayT", [CK, NH * CK])
    qscT_i = din("qscT", [128, CT * CK])
    kscT_i = din("kscT", [CK, D])
    gctab_i = din("gctab", [128, 512])
    maskS_i = din("maskS", [128, 1])
    ident_i = din("ident", [128, 128])

    y = nc.dram_tensor("y", [CT, 128, T], F32, kind="ExternalOutput").ap()

    def dsc(name, shape, dt):
        return nc.dram_tensor(name, shape, dt).ap()

    wg_q = dsc("wg_q", [I3, D], BF16)
    wu_q = dsc("wu_q", [I3, D], BF16)
    wd_q = dsc("wd_q", [D, I3], BF16)
    U_d = dsc("U_d", [OT, 128, T], F32)
    G_d = dsc("G_d", [OT, 128, T], F32)
    g16_d = dsc("g16_d", [OT, 128, T], I16)
    g216_d = dsc("g216_d", [OT, 128, T], I16)
    u16_d = dsc("u16_d", [OT, 128, T], I16)
    x1_d = dsc("x1_d", [CT, 128, T], F32)
    gated_d = dsc("gated_d", [CT, 128, T], BF16)
    trb_d = dsc("trb_d", [T, D], BF16)
    trb2_d = dsc("trb2_d", [T, D], BF16)

    es = contextlib.ExitStack()
    ones_col = es.enter_context(nc.sbuf_tensor("ones_col", [128, 1], F32))
    ones_row = es.enter_context(nc.sbuf_tensor("ones_row", [1, 128], F32))
    COLS = es.enter_context(nc.sbuf_tensor("COLS", [128, 16], F32))
    SSC = es.enter_context(nc.sbuf_tensor("SSC", [1, 64], F32))
    MXC = es.enter_context(nc.sbuf_tensor("MXC", [128, 128], F32))
    IDN = es.enter_context(nc.sbuf_tensor("IDN", [128, 128], F32))
    MSKC = es.enter_context(nc.sbuf_tensor("MSKC", [128, 1], F32))
    SST = es.enter_context(nc.sbuf_tensor("SST", [128, 512], F32))
    SSB = es.enter_context(nc.sbuf_tensor("SSB", [128, 512], BF16))
    PB = [es.enter_context(nc.psum_tensor("pb%d" % k, [128, 512], F32))
          for k in range(8)]

    qS = Sem(es, nc, "qS")
    gD = Sem(es, nc, "gD")
    qA = Sem(es, nc, "qA")
    qV = Sem(es, nc, "qV")
    qG = Sem(es, nc, "qG")
    vD = Sem(es, nc, "vD")
    aD = Sem(es, nc, "aD")
    pD = Sem(es, nc, "pD")
    dsWR = DSlot(es, nc, "dsWR", 2)
    dsWQ = DSlot(es, nc, "dsWQ", 2)
    dsEV = DSlot(es, nc, "dsEV", 2)
    dsGS = DSlot(es, nc, "dsGS", 2)
    dsGI = DSlot(es, nc, "dsGI", 3)
    dsUS = DSlot(es, nc, "dsUS", 2)
    dsUI = DSlot(es, nc, "dsUI", 2)
    dsG2 = DSlot(es, nc, "dsG2", 2)
    dsL4 = DSlot(es, nc, "dsL4", 2)
    dsGT = DSlot(es, nc, "dsGT", 2)
    dsX1 = DSlot(es, nc, "dsX1", 2)

    _uid = [0]

    def un(base):
        _uid[0] += 1
        return "%s_%d" % (base, _uid[0])

    def VEC(blk, fn):
        blk.vector(lambda e: fn(FE(e, vD)))

    def SCA(blk, fn):
        blk.scalar(lambda e: fn(FE(e, aD)))

    def PLC(blk, fn):
        blk.gpsimd(lambda e: fn(FE(e, gD)))

    def EW(blk, fn, pool):
        (PLC if pool else VEC)(blk, fn)

    def ewm(pool):
        s = gD if pool else vD
        return (s.h, s.n)

    iRX, iSX, iRUP, iSUP, iRG, iSG, iK2, iK3, iOS, iSO = range(10)
    (jEPS, jMX, jSX, jRX, jT0, jT1, jSG, jSUP, jSG2, jRG2, jK2, jSO, jRO,
     jK3, jMG, jMUP, jMP, jMP2, jRG, jRUP) = range(20)

    _strbox = [None]

    def STRr(r):
        return _strbox[0][0:1, r * T:(r + 1) * T]

    def final_max(blk, ncols, out_j):
        def v1(v):
            v.tensor_reduce(MXC[:, 127:128], MXC[:, 0:ncols],
                                   op=ALU.max, axis=AX.X)
        VEC(blk, v1)
        mv = vD.n

        def t1(te):
            te.wait_ge(vD.h, mv)
            pD.inc(te.transpose(PB[7][0:1, 0:128], MXC[:, 127:128], IDN[:]))
        blk.tensor(t1)
        mp = pD.n

        def v2(v):
            v.wait_ge(pD.h, mp)
            v.tensor_reduce(SSC[0:1, out_j:out_j + 1], PB[7][0:1, 0:128],
                            op=ALU.max, axis=AX.X)
            v.memset(MXC[:], 0.0)
        VEC(blk, v2)

    def bcast_scalar(blk, src_j, dst_i):
        mv = vD.n
        ma = aD.n

        def t1(te):
            te.wait_ge(vD.h, mv)
            te.wait_ge(aD.h, ma)
            pD.inc(te.matmul(PB[7][:, 0:1], ones_row[:],
                             SSC[0:1, src_j:src_j + 1], start=True, stop=True))
        blk.tensor(t1)
        mp = pD.n

        def s1(sc):
            sc.wait_ge(pD.h, mp)
            sc.copy(COLS[:, dst_i:dst_i + 1], PB[7][:, 0:1])
        SCA(blk, s1)

    def qdq_scale(blk, m_j, s_j, r_j):
        def v1(v):
            v.wait_ge(vD.h, vD.n)
            v.tensor_scalar(SSC[0:1, s_j:s_j + 1], SSC[0:1, m_j:m_j + 1],
                            1.0 / QMAX, 1e-12, op0=ALU.mult, op1=ALU.max)
            v.reciprocal(SSC[0:1, r_j:r_j + 1], SSC[0:1, s_j:s_j + 1])
        VEC(blk, v1)

    def emit_ln(XH, SQ, qdq, XQ):
        esL = contextlib.ExitStack()
        _strbox[0] = esL.enter_context(nc.sbuf_tensor(un("STR"), [1, 2 * T],
                                                      F32))
        mq0 = qS.n
        with nc.Block() as blk:
            pe_done = []
            for ct in range(CT):
                ma_prev = aD.n

                def s_sq(sc, ct=ct, mq0=mq0):
                    if ct == 0:
                        sc.wait_ge(qS.h, mq0)
                    if ct >= 2:
                        sc.wait_ge(pD.h, pe_done[ct - 2])
                    sc.activation(SQ[:, ct % 2, :], XH[:, ct, :],
                                         AF.Square)
                SCA(blk, s_sq)
                ma = aD.n

                def t_mm(te, ct=ct, ma=ma):
                    te.wait_ge(aD.h, ma)
                    for q in range(4):
                        st = (ct == 0)
                        sp = (ct == CT - 1)
                        te.matmul(PB[q][0:1, :], ones_col[:],
                                  XH[:, ct, q * 512:(q + 1) * 512],
                                  start=st, stop=sp)
                        mm = te.matmul(PB[4 + q][0:1, :], ones_col[:],
                                       SQ[:, ct % 2, q * 512:(q + 1) * 512],
                                       start=st, stop=sp)
                    pD.inc(mm)
                blk.tensor(t_mm)
                pe_done.append(pD.n)

            mp = pD.n

            def s_ev(sc):
                sc.wait_ge(pD.h, mp)
                for q in range(4):
                    sc.copy(STRr(0)[0:1, q * 512:(q + 1) * 512], PB[q][0:1, :])
                    sc.copy(STRr(1)[0:1, q * 512:(q + 1) * 512],
                                   PB[4 + q][0:1, :])
            SCA(blk, s_ev)
            ma = aD.n

            def v_st(v):
                v.wait_ge(aD.h, ma)
                v.tensor_scalar(STRr(0), STRr(0), 1.0 / D, None, op0=ALU.mult)
                v.tensor_scalar(STRr(1), STRr(1), 1.0 / D, None, op0=ALU.mult)
                v.tensor_tensor(SQ[0:1, 0, :], STRr(0), STRr(0), op=ALU.mult)
                v.tensor_tensor(STRr(1), STRr(1), SQ[0:1, 0, :],
                                       op=ALU.subtract)
            VEC(blk, v_st)
            mv = vD.n

            def s_sd(sc):
                sc.wait_ge(vD.h, mv)
                sc.activation(STRr(1), STRr(1), AF.Sqrt,
                                     bias=SSC[0:1, jEPS:jEPS + 1])
            SCA(blk, s_sd)
            ma2 = aD.n

            def v_rs(v):
                v.wait_ge(aD.h, ma2)
                v.reciprocal(STRr(1), STRr(1))
            VEC(blk, v_rs)

        if not qdq:
            with nc.Block() as blk:
                last_reads = [vD.n, vD.n]
                for q in range(4):
                    mv0 = last_reads[0]
                    mv1 = last_reads[1]

                    def t_bc(te, q=q, mv0=mv0, mv1=mv1):
                        te.wait_ge(vD.h, max(mv0, mv1, 0))
                        te.matmul(PB[4][:, :], ones_row[:],
                                  STRr(0)[0:1, q * 512:(q + 1) * 512],
                                  start=True, stop=True)
                        pD.inc(te.matmul(PB[5][:, :], ones_row[:],
                                         STRr(1)[0:1, q * 512:(q + 1) * 512],
                                         start=True, stop=True))
                    blk.tensor(t_bc)
                    mp = pD.n

                    def v_nm(v, q=q, mp=mp):
                        v.wait_ge(pD.h, mp)
                        for ct in range(CT):
                            sl = slice(q * 512, (q + 1) * 512)
                            v.tensor_tensor(SQ[:, 0, 0:512], XH[:, ct, sl],
                                            PB[4][:, :], op=ALU.subtract)
                            mm = v.tensor_tensor(XQ[:, ct, sl],
                                                 SQ[:, 0, 0:512],
                                                 PB[5][:, :], op=ALU.mult)
                        mm
                    VEC(blk, v_nm)
                    last_reads = [vD.n, vD.n]
            esL.close()
            return

        with nc.Block() as blk:
            for q in range(4):
                mv = vD.n

                def t_bc(te, q=q, mv=mv):
                    te.wait_ge(vD.h, mv)
                    te.matmul(PB[4][:, :], ones_row[:],
                              STRr(0)[0:1, q * 512:(q + 1) * 512],
                              start=True, stop=True)
                    pD.inc(te.matmul(PB[5][:, :], ones_row[:],
                                     STRr(1)[0:1, q * 512:(q + 1) * 512],
                                     start=True, stop=True))
                blk.tensor(t_bc)
                mp = pD.n

                def v_nm(v, q=q, mp=mp):
                    v.wait_ge(pD.h, mp)
                    for ct in range(CT):
                        sl = slice(q * 512, (q + 1) * 512)
                        v.tensor_tensor(SQ[:, 0, 0:512], XH[:, ct, sl],
                                        PB[4][:, :], op=ALU.subtract)
                        v.tensor_tensor(SQ[:, 1, 0:512], SQ[:, 0, 0:512],
                                        PB[5][:, :], op=ALU.mult)
                        mm = v.tensor_reduce(
                            MXC[:, q * 8 + ct:q * 8 + ct + 1],
                            SQ[:, 1, 0:512], op=ALU.max, axis=AX.X,
                            apply_absolute_value=True)
                    mm
                VEC(blk, v_nm)

        with nc.Block() as blk:
            final_max(blk, 32, jMX)
            qdq_scale(blk, jMX, jSX, jRX)
            bcast_scalar(blk, jSX, iSX)

            def v_rr(v):
                v.wait_ge(vD.h, vD.n)
                v.tensor_scalar(STRr(1), STRr(1), SSC[0:1, jRX:jRX + 1], None,
                                op0=ALU.mult)
                v.tensor_tensor(STRr(0), STRr(0), STRr(1), op=ALU.mult)
            VEC(blk, v_rr)

        with nc.Block() as blk:
            ma0 = aD.n
            for q in range(4):
                mv = vD.n

                def t_bc(te, q=q, mv=mv):
                    te.wait_ge(vD.h, mv)
                    te.wait_ge(aD.h, ma0)
                    te.matmul(PB[4][:, :], ones_row[:],
                              STRr(1)[0:1, q * 512:(q + 1) * 512],
                              start=True, stop=True)
                    pD.inc(te.matmul(PB[5][:, :], ones_row[:],
                                     STRr(0)[0:1, q * 512:(q + 1) * 512],
                                     start=True, stop=True))
                blk.tensor(t_bc)
                mp = pD.n

                def v_qz(v, q=q, mp=mp):
                    v.wait_ge(pD.h, mp)
                    for ct in range(CT):
                        sl = slice(q * 512, (q + 1) * 512)
                        v.tensor_tensor(SQ[:, 0, 0:512], XH[:, ct, sl],
                                        PB[4][:, :], op=ALU.mult)
                        v.tensor_tensor(SQ[:, 1, 0:512], SQ[:, 0, 0:512],
                                        PB[5][:, :], op=ALU.subtract)
                        v.tensor_scalar(SQ[:, 0, 0:512], SQ[:, 1, 0:512],
                                        MAGIC, MAGIC, op0=ALU.add,
                                        op1=ALU.subtract)
                        mm = v.tensor_scalar(XQ[:, ct, sl], SQ[:, 0, 0:512],
                                             COLS[:, iSX:iSX + 1], None,
                                             op0=ALU.mult)
                    mm
                VEC(blk, v_qz)
        esL.close()

    def emit_ffn(XH):
        esF = contextlib.ExitStack()
        WT = esF.enter_context(nc.sbuf_tensor(un("WT_f"), [128, OT, D], BF16))
        WTg = WT[:].rearrange("p o d -> p (o d)").rearrange(
            "p (i o) -> p i o", i=CT)

        with contextlib.ExitStack() as esQ:
            XQ = esQ.enter_context(nc.sbuf_tensor(un("XQ_f"), [128, CT, T], BF16))
            EV = esQ.enter_context(nc.sbuf_tensor(un("EV_f"), [128, 2, 512], F32))
            SQ = esQ.enter_context(nc.sbuf_tensor(un("SQ_f"), [128, 2, T], F32))
            emit_ln(XH, SQ, True, XQ)

            for wsta, wdst, mj, sj, rj, ci in (
                    (wg_q, G_d, jMG, jSG, jRG, iRG),
                    (wu_q, U_d, jMUP, jSUP, jRUP, iRUP)):
                with nc.Block() as blk:
                    mp_wt = pD.n
                    mq0 = dsWQ.n(0)
                    mq1 = dsWQ.n(1)

                    def s_ld(sy, wsta=wsta, mp_wt=mp_wt, mq0=mq0, mq1=mq1):
                        sy.wait_ge(pD.h, mp_wt)
                        sy.wait_ge(qS.h, qS.n)
                        sy.wait_ge(dsWQ.h(0), mq0)
                        sy.wait_ge(dsWQ.h(1), mq1)
                        for i in range(CT):
                            qS.inc(sy.dma_start_transpose(
                                WTg[:, i, :],
                                wsta[:, i * 128:(i + 1) * 128]), 16)
                    blk.sync(s_ld)
                    m_ld = qS.n
                    mv_xq = vD.n

                    evac_done = []
                    red_done = []
                    dma_done = []
                    g = 0
                    for ot in range(OT):
                        for q in range(4):
                            def t_g(te, ot=ot, q=q, g=g):
                                if g == 0:
                                    te.wait_ge(qS.h, m_ld)
                                    te.wait_ge(vD.h, mv_xq)
                                if g >= 2:
                                    te.wait_ge(aD.h, evac_done[g - 2])
                                for i in range(CT):
                                    mm = te.matmul(
                                        PB[g % 2][:, :],
                                        WTg[:, i, ot * 128:(ot + 1) * 128],
                                        XQ[:, i, q * 512:(q + 1) * 512],
                                        start=(i == 0), stop=(i == CT - 1))
                                pD.inc(mm)
                            blk.tensor(t_g)
                            mp = pD.n

                            def s_g(sc, ot=ot, q=q, g=g, mp=mp, wdst=wdst,
                                    mev=dsEV.n(g % 2), mv_xq=mv_xq):
                                sc.wait_ge(pD.h, mp)
                                sc.wait_ge(dsEV.h(g % 2), mev)
                                sc.wait_ge(vD.h, red_done[g - 2]
                                           if g >= 2 else mv_xq)
                                sc.copy(EV[:, g % 2, :],
                                               PB[g % 2][:, :])
                                dsEV.issue(sc.dma_start(
                                    out=wdst[ot, :, q * 512:(q + 1) * 512],
                                    in_=EV[:, g % 2, :]), g % 2)
                            SCA(blk, s_g)
                            evac_done.append(aD.n)
                            dma_done.append(0)

                            def v_g(v, g=g, ma=aD.n):
                                v.wait_ge(aD.h, ma)
                                v.tensor_reduce(
                                    MXC[:, g:g + 1], EV[:, g % 2, :],
                                    op=ALU.max, axis=AX.X,
                                    apply_absolute_value=True)
                            VEC(blk, v_g)
                            red_done.append(vD.n)
                            g += 1

                with nc.Block() as blk:
                    final_max(blk, 96, mj)
                    qdq_scale(blk, mj, sj, rj)
                    bcast_scalar(blk, rj, ci)
                    if sj == jSG:
                        bcast_scalar(blk, sj, iSG)

        with contextlib.ExitStack() as es2:
            GS = es2.enter_context(nc.sbuf_tensor(un("GS_f"), [128, 2, T], F32))
            GI = es2.enter_context(nc.sbuf_tensor(un("GI_f"), [128, 3, T], I16))
            SIG = es2.enter_context(nc.sbuf_tensor(un("SIG_f"), [128, 2, T], F32))
            U16 = es2.enter_context(nc.sbuf_tensor(un("U16_f"), [128, 2, T], F32))
            with nc.Block() as blk:
                ts_done = []
                sig_done = []
                fin_done = []
                red_done = []
                ev0, ev1 = dsEV.n(0), dsEV.n(1)
                for ot in range(OT):
                    pool = (ot % 2 == 1)

                    def s_ld(sy, ot=ot, ev0=ev0, ev1=ev1):
                        if ot == 0:
                            sy.wait_ge(dsEV.h(0), ev0)
                            sy.wait_ge(dsEV.h(1), ev1)
                        if ot >= 2:
                            sy.wait_ge(*ts_done[ot - 2])
                        dsGS.issue(sy.dma_start(out=GS[:, ot % 2, :],
                                                in_=G_d[ot]), ot % 2)
                    blk.sync(s_ld)

                    def v_a(v, ot=ot, mld=dsGS.n(ot % 2),
                            mgi=dsGI.n(ot % 3)):
                        v.wait_ge(dsGS.h(ot % 2), mld)
                        if ot >= 2:
                            v.wait_ge(vD.h, red_done[ot - 2])
                        if ot >= 3:
                            v.wait_ge(aD.h, sig_done[ot - 3])
                            v.wait_ge(*fin_done[ot - 3])
                            v.wait_ge(dsGI.h(ot % 3), mgi)
                        v.tensor_scalar(SIG[:, ot % 2, :], GS[:, ot % 2, :],
                                        COLS[:, iRG:iRG + 1], MAGIC,
                                        op0=ALU.mult, op1=ALU.add)
                        v.tensor_scalar(GI[:, ot % 3, :],
                                               SIG[:, ot % 2, :], MAGIC, None,
                                               op0=ALU.subtract)
                    EW(blk, v_a, pool)
                    ts_done.append(ewm(pool))

                    def s_sig(sc, ot=ot, mts=ts_done[ot]):
                        sc.wait_ge(*mts)
                        sc.activation(SIG[:, ot % 2, :],
                                             GI[:, ot % 3, :], AF.Sigmoid,
                                             scale=COLS[:, iSG:iSG + 1])
                    SCA(blk, s_sig)
                    sig_done.append(aD.n)

                    def g_st(g, ot=ot, mts=ts_done[ot]):
                        g.wait_ge(*mts)
                        dsGI.issue(g.dma_start(out=g16_d[ot],
                                               in_=GI[:, ot % 3, :]), ot % 3)
                    blk.scalar(g_st)

                    def v_b(v, ot=ot, ma=aD.n):
                        v.wait_ge(aD.h, ma)
                        v.tensor_scalar(U16[:, ot % 2, :], SIG[:, ot % 2, :],
                                        65536.0, MAGIC, op0=ALU.mult,
                                        op1=ALU.add)
                        v.tensor_scalar(U16[:, ot % 2, :], U16[:, ot % 2, :],
                                        MAGIC, 65535.0, op0=ALU.subtract,
                                        op1=ALU.min)
                        v.tensor_tensor(SIG[:, ot % 2, :], GI[:, ot % 3, :],
                                        U16[:, ot % 2, :], op=ALU.mult)
                    EW(blk, v_b, pool)
                    fin_done.append(ewm(pool))

                    def v_red(v, ot=ot, mfb=fin_done[ot]):
                        v.wait_ge(*mfb)
                        v.tensor_reduce(MXC[:, ot:ot + 1],
                                               SIG[:, ot % 2, :],
                                               op=ALU.max, axis=AX.X,
                                               apply_absolute_value=True)
                    VEC(blk, v_red)
                    red_done.append(vD.n)

            with nc.Block() as blk:
                final_max(blk, OT, jMP)

                def v_s2(v):
                    v.wait_ge(vD.h, vD.n)
                    v.tensor_tensor(SSC[0:1, jT0:jT0 + 1],
                                    SSC[0:1, jMP:jMP + 1],
                                    SSC[0:1, jSG:jSG + 1], op=ALU.mult)
                    v.tensor_scalar(SSC[0:1, jSG2:jSG2 + 1],
                                    SSC[0:1, jT0:jT0 + 1],
                                    float(2.0 ** -16 / QMAX), 1e-12,
                                    op0=ALU.mult, op1=ALU.max)
                    v.reciprocal(SSC[0:1, jRG2:jRG2 + 1],
                                 SSC[0:1, jSG2:jSG2 + 1])
                    v.tensor_tensor(SSC[0:1, jT0:jT0 + 1],
                                    SSC[0:1, jSG:jSG + 1],
                                    SSC[0:1, jRG2:jRG2 + 1], op=ALU.mult)
                    v.tensor_scalar(SSC[0:1, jK2:jK2 + 1],
                                           SSC[0:1, jT0:jT0 + 1],
                                           float(2.0 ** -16), None,
                                           op0=ALU.mult)
                VEC(blk, v_s2)
                bcast_scalar(blk, jK2, iK2)

        with contextlib.ExitStack() as es3:
            US = es3.enter_context(nc.sbuf_tensor(un("US_f"), [128, 2, T], F32))
            GI = es3.enter_context(nc.sbuf_tensor(un("GI3_f"), [128, 2, T], I16))
            UI = es3.enter_context(nc.sbuf_tensor(un("UI3_f"), [128, 2, T], I16))
            G2 = es3.enter_context(nc.sbuf_tensor(un("G23_f"), [128, 2, T], I16))
            SG3 = es3.enter_context(nc.sbuf_tensor(un("SG3_f"), [128, 2, T], F32))
            with nc.Block() as blk:
                a_done = []; b_done = []; sg_done = []; red_done = []
                gi_gates = [dsGI.n(i) for i in range(3)]
                ev_gates = [dsEV.n(0), dsEV.n(1)]
                for ot in range(OT):
                    pool = (ot % 2 == 1)

                    def s_ld(sy, ot=ot):
                        if ot == 0:
                            for i in range(3):
                                sy.wait_ge(dsGI.h(i), gi_gates[i])
                            sy.wait_ge(dsEV.h(0), ev_gates[0])
                            sy.wait_ge(dsEV.h(1), ev_gates[1])
                        if ot >= 2:
                            sy.wait_ge(*b_done[ot - 2])
                            sy.wait_ge(aD.h, sg_done[ot - 2])
                        dsUS.issue(sy.dma_start(out=US[:, ot % 2, :],
                                                in_=U_d[ot]), ot % 2)
                        dsUS.issue(sy.dma_start(out=GI[:, ot % 2, :],
                                                in_=g16_d[ot]), ot % 2)
                    blk.sync(s_ld)

                    def v_a(v, ot=ot, mld=dsUS.n(ot % 2),
                            mui=dsUI.n(ot % 2)):
                        v.wait_ge(dsUS.h(ot % 2), mld)
                        if ot >= 2:
                            v.wait_ge(*b_done[ot - 2])
                            v.wait_ge(vD.h, red_done[ot - 2])
                            v.wait_ge(dsUI.h(ot % 2), mui)
                        v.tensor_scalar(SG3[:, ot % 2, :], US[:, ot % 2, :],
                                        COLS[:, iRUP:iRUP + 1], MAGIC,
                                        op0=ALU.mult, op1=ALU.add)
                        v.tensor_scalar(UI[:, ot % 2, :],
                                               SG3[:, ot % 2, :],
                                               MAGIC, None, op0=ALU.subtract)
                    EW(blk, v_a, pool)
                    a_done.append(ewm(pool))

                    def g_su(g, ot=ot, mva=a_done[ot]):
                        g.wait_ge(*mva)
                        dsUI.issue(g.dma_start(out=u16_d[ot],
                                               in_=UI[:, ot % 2, :]), ot % 2)
                    blk.scalar(g_su)

                    def s_sig(sc, ot=ot, mva=a_done[ot]):
                        sc.wait_ge(*mva)
                        sc.activation(SG3[:, ot % 2, :],
                                             GI[:, ot % 2, :], AF.Sigmoid,
                                             scale=COLS[:, iSG:iSG + 1])
                    SCA(blk, s_sig)
                    sg_done.append(aD.n)

                    g2m = {}

                    def v_b(v, ot=ot, ma=aD.n, g2m=g2m, pool=pool,
                            mg2=dsG2.n(ot % 2)):
                        v.wait_ge(aD.h, ma)
                        if ot >= 2:
                            v.wait_ge(dsG2.h(ot % 2), mg2)
                        v.tensor_scalar(SG3[:, ot % 2, :], SG3[:, ot % 2, :],
                                        65536.0, MAGIC, op0=ALU.mult,
                                        op1=ALU.add)
                        v.tensor_scalar(SG3[:, ot % 2, :], SG3[:, ot % 2, :],
                                        MAGIC, 65535.0, op0=ALU.subtract,
                                        op1=ALU.min)
                        v.tensor_tensor(SG3[:, ot % 2, :], GI[:, ot % 2, :],
                                        SG3[:, ot % 2, :], op=ALU.mult)
                        v.tensor_scalar(SG3[:, ot % 2, :], SG3[:, ot % 2, :],
                                        COLS[:, iK2:iK2 + 1], MAGIC,
                                        op0=ALU.mult, op1=ALU.add)
                        v.tensor_scalar(G2[:, ot % 2, :],
                                               SG3[:, ot % 2, :],
                                               MAGIC, None, op0=ALU.subtract)
                        g2m[0] = (gD.h, gD.n) if pool else (vD.h, vD.n)
                        v.tensor_tensor(SG3[:, ot % 2, :], G2[:, ot % 2, :],
                                        UI[:, ot % 2, :], op=ALU.mult)
                    EW(blk, v_b, pool)
                    b_done.append(ewm(pool))

                    def v_red(v, ot=ot, mfb=b_done[ot]):
                        v.wait_ge(*mfb)
                        v.tensor_reduce(MXC[:, ot:ot + 1],
                                               SG3[:, ot % 2, :],
                                               op=ALU.max, axis=AX.X,
                                               apply_absolute_value=True)
                    VEC(blk, v_red)
                    red_done.append(vD.n)

                    def g_sg(g, ot=ot, mg=g2m[0]):
                        g.wait_ge(*mg)
                        dsG2.issue(g.dma_start(out=g216_d[ot],
                                               in_=G2[:, ot % 2, :]), ot % 2)
                    blk.scalar(g_sg)

            with nc.Block() as blk:
                final_max(blk, OT, jMP2)

                def v_so(v):
                    v.wait_ge(vD.h, vD.n)
                    v.tensor_tensor(SSC[0:1, jT0:jT0 + 1],
                                    SSC[0:1, jMP2:jMP2 + 1],
                                    SSC[0:1, jSG2:jSG2 + 1], op=ALU.mult)
                    v.tensor_tensor(SSC[0:1, jT0:jT0 + 1],
                                    SSC[0:1, jT0:jT0 + 1],
                                    SSC[0:1, jSUP:jSUP + 1], op=ALU.mult)
                    v.tensor_scalar(SSC[0:1, jSO:jSO + 1],
                                    SSC[0:1, jT0:jT0 + 1],
                                    1.0 / QMAX, 1e-12, op0=ALU.mult,
                                    op1=ALU.max)
                    v.reciprocal(SSC[0:1, jRO:jRO + 1], SSC[0:1, jSO:jSO + 1])
                    v.tensor_tensor(SSC[0:1, jT1:jT1 + 1],
                                    SSC[0:1, jSG2:jSG2 + 1],
                                    SSC[0:1, jSUP:jSUP + 1], op=ALU.mult)
                    v.tensor_tensor(SSC[0:1, jK3:jK3 + 1],
                                    SSC[0:1, jT1:jT1 + 1],
                                    SSC[0:1, jRO:jRO + 1], op=ALU.mult)
                    v.tensor_scalar(SSC[0:1, jT0:jT0 + 1],
                                           SSC[0:1, jSO:jSO + 1], 0.5, None,
                                           op0=ALU.mult)
                VEC(blk, v_so)
                bcast_scalar(blk, jK3, iK3)
                bcast_scalar(blk, jT0, iOS)

        with contextlib.ExitStack() as es4:
            OB = es4.enter_context(nc.sbuf_tensor(un("OB_f"), [128, OT, 1024],
                                                  BF16))
            G2 = es4.enter_context(nc.sbuf_tensor(un("G24_f"), [128, 2, 1024],
                                                  I16))
            UI = es4.enter_context(nc.sbuf_tensor(un("UI4_f"), [128, 2, 1024],
                                                  I16))
            PR = es4.enter_context(nc.sbuf_tensor(un("PR4_f"), [128, 2, 1024],
                                                  F32))
            with nc.Block() as blk:
                mp_wt = pD.n

                mg_wd = qG.n

                def s_wd(sy, mp_wt=mp_wt, mg_wd=mg_wd):
                    sy.wait_ge(pD.h, mp_wt)
                    sy.wait_ge(qS.h, qS.n)
                    sy.wait_ge(qG.h, mg_wd)
                    for o in range(OT):
                        qS.inc(sy.dma_start_transpose(
                            WT[:, o, :], wd_q[:, o * 128:(o + 1) * 128]), 16)
                blk.sync(s_wd)
                m_wd = qS.n

                g4_gates = [dsUI.n(0), dsUI.n(1), dsG2.n(0), dsG2.n(1)]
                ob_done = []
                for th in range(2):
                    th_pgate = pD.n
                    for ot in range(OT):
                        gk2 = th * OT + ot
                        pool = (ot % 2 == 1)

                        def s_l4(sy, th=th, ot=ot, gk2=gk2):
                            if th == 0 and ot == 0:
                                sy.wait_ge(dsUI.h(0), g4_gates[0])
                                sy.wait_ge(dsUI.h(1), g4_gates[1])
                                sy.wait_ge(dsG2.h(0), g4_gates[2])
                                sy.wait_ge(dsG2.h(1), g4_gates[3])
                            if gk2 >= 2:
                                sy.wait_ge(*ob_done[gk2 - 2])
                            sl = slice(th * 1024, (th + 1) * 1024)
                            dsL4.issue(sy.dma_start(out=G2[:, ot % 2, :],
                                                    in_=g216_d[ot, :, sl]),
                                       ot % 2)
                            dsL4.issue(sy.dma_start(out=UI[:, ot % 2, :],
                                                    in_=u16_d[ot, :, sl]),
                                       ot % 2)
                        blk.sync(s_l4)

                        def v_ob(v, ot=ot, mld=dsL4.n(ot % 2),
                                 th_pgate=th_pgate):
                            v.wait_ge(dsL4.h(ot % 2), mld)
                            v.wait_ge(pD.h, th_pgate)
                            v.tensor_tensor(PR[:, ot % 2, :], G2[:, ot % 2, :],
                                            UI[:, ot % 2, :], op=ALU.mult)
                            v.tensor_scalar(PR[:, ot % 2, :], PR[:, ot % 2, :],
                                            COLS[:, iK3:iK3 + 1], MAGIC,
                                            op0=ALU.mult, op1=ALU.add)
                            v.tensor_scalar(
                                OB[:, ot, :], PR[:, ot % 2, :],
                                MAGIC, COLS[:, iOS:iOS + 1],
                                op0=ALU.subtract, op1=ALU.mult)
                        EW(blk, v_ob, pool)
                        ob_done.append(ewm(pool))
                    m_ob = vD.n
                    m_obg = gD.n

                    ev_done = {}
                    for dt in range(CT):
                        for qq in range(2):
                            gk = dt * 2 + qq
                            def t_dn(te, th=th, dt=dt, qq=qq, gk=gk,
                                     m_ob=m_ob, m_obg=m_obg):
                                if gk == 0:
                                    te.wait_ge(qS.h, m_wd)
                                    te.wait_ge(vD.h, m_ob)
                                    te.wait_ge(gD.h, m_obg)
                                if gk >= 2:
                                    te.wait_ge(vD.h, ev_done[gk - 2])
                                tsl = slice(qq * 512, (qq + 1) * 512)
                                for o in range(OT):
                                    mm = te.matmul(
                                        PB[gk % 2][:, :],
                                        WT[:, o, dt * 128:(dt + 1) * 128],
                                        OB[:, o, tsl],
                                        start=(o == 0), stop=(o == OT - 1))
                                pD.inc(mm)
                            blk.tensor(t_dn)
                            mp = pD.n

                            def v_rs(v, th=th, dt=dt, qq=qq, gk=gk, mp=mp):
                                v.wait_ge(pD.h, mp)
                                sl = slice(th * 1024 + qq * 512,
                                           th * 1024 + (qq + 1) * 512)
                                v.tensor_tensor(
                                    XH[:, dt, sl], XH[:, dt, sl],
                                    PB[gk % 2][:, :], op=ALU.add)
                            VEC(blk, v_rs)
                            ev_done[gk] = vD.n
        esF.close()

    with es:
        esXA = contextlib.ExitStack()
        XA = esXA.enter_context(nc.sbuf_tensor("XA", [128, CT, T], F32))
        esW = contextlib.ExitStack()
        WRAW = esW.enter_context(nc.sbuf_tensor("WRAW", [128, 2, I3], F32))
        WQB = esW.enter_context(nc.sbuf_tensor("WQB", [128, 2, I3], BF16))
        WST = esW.enter_context(nc.sbuf_tensor("WST", [128, 2, I3], F32))
        SCM = esW.enter_context(nc.sbuf_tensor("SCM", [128, 2, 96], F32))
        SCR = esW.enter_context(nc.sbuf_tensor("SCR", [128, 2, 96], F32))

        with nc.Block() as blk:
            def s_0(sy):
                qS.inc(sy.dma_start(out=IDN[:], in_=ident_i), 16)
                qS.inc(sy.dma_start(out=MSKC[:], in_=maskS_i), 16)
                qS.inc(sy.dma_start(out=XA[:],
                                    in_=x_own.rearrange("c p t -> p c t")), 16)
            blk.sync(s_0)

            def v_0(v):
                v.memset(ones_col[:], 1.0)
                v.memset(ones_row[:], 1.0)
                v.memset(SSC[:], 0.0)
                v.memset(SSC[0:1, jEPS:jEPS + 1], EPS)
                v.memset(MXC[:], 0.0)
            VEC(blk, v_0)

        for wsrc, wdst, osz, rl in ((w_gate, wg_q, OT, D), (w_up, wu_q, OT, D),
                                    (w_down, wd_q, CT, I3)):
            nb = rl // 32
            with nc.Block() as blk:
                dq_done = []
                mv_blk = vD.n
                mg_blk = gD.n
                wsv = wsrc.rearrange("(k p) i -> k p i", p=128)
                wdv = wdst.rearrange("(k p) i -> k p i", p=128)
                for k in range(osz):
                    kp = k % 2
                    pool = (kp == 1)

                    def s_lw(sy, k=k, rl=rl, kp=kp):
                        if k >= 2:
                            sy.wait_ge(*dq_done[k - 2])
                        else:
                            sy.wait_ge(vD.h, mv_blk)
                            sy.wait_ge(gD.h, mg_blk)
                        dsWR.issue(sy.dma_start(out=WRAW[:, kp, 0:rl],
                                                in_=wsv[k]), kp)
                    blk.sync(s_lw)

                    def v_sc(v, k=k, rl=rl, nb=nb, kp=kp,
                             mwr=dsWR.n(kp)):
                        v.wait_ge(dsWR.h(kp), mwr)
                        wv = WRAW[:, kp, 0:rl].rearrange(
                            "p (b j) -> p b j", j=32)
                        v.tensor_reduce(SCM[:, kp, 0:nb], wv, op=ALU.max,
                                        axis=AX.X, apply_absolute_value=True)
                        v.tensor_scalar(SCM[:, kp, 0:nb], SCM[:, kp, 0:nb],
                                        1.0 / 7.0, 1e-12, op0=ALU.mult,
                                        op1=ALU.max)
                        v.reciprocal(SCR[:, kp, 0:nb], SCM[:, kp, 0:nb])
                    VEC(blk, v_sc)
                    mv_sc = vD.n

                    def v_dq(v, k=k, rl=rl, nb=nb, kp=kp, mv_sc=mv_sc,
                             mwq=dsWQ.n(kp)):
                        v.wait_ge(vD.h, mv_sc)
                        v.wait_ge(dsWQ.h(kp), mwq)
                        wv = WRAW[:, kp, 0:rl].rearrange(
                            "p (b j) -> p b j", j=32)
                        rb = SCR[:, kp, 0:nb].unsqueeze(2).broadcast_to(
                            [128, nb, 32])
                        sb = SCM[:, kp, 0:nb].unsqueeze(2).broadcast_to(
                            [128, nb, 32])
                        sv = WST[:, kp, 0:rl].rearrange(
                            "p (b j) -> p b j", j=32)
                        v.tensor_tensor(sv, wv, rb, op=ALU.mult)
                        v.tensor_scalar(WST[:, kp, 0:rl], WST[:, kp, 0:rl],
                                        MAGIC, MAGIC,
                                        op0=ALU.add, op1=ALU.subtract)
                        qv = WQB[:, kp, 0:rl].rearrange(
                            "p (b j) -> p b j", j=32)
                        v.tensor_tensor(qv, sv, sb, op=ALU.mult)
                    EW(blk, v_dq, pool)
                    dq_done.append(ewm(pool))

                    def g_st(g, k=k, rl=rl, kp=kp, mdq=dq_done[k]):
                        g.wait_ge(*mdq)
                        dsWQ.issue(g.dma_start(out=wdv[k],
                                               in_=WQB[:, kp, 0:rl]),
                                   kp)
                    blk.scalar(g_st)

        esW.close()

        emit_ffn(XA)

        if STAGE == 1:
            if os.environ.get("KTAP"):
                dbg_ssc = nc.dram_tensor("dbg_ssc", [1, 64], F32,
                                         kind="ExternalOutput").ap()
                dbg_cols = nc.dram_tensor("dbg_cols", [128, 16], F32,
                                          kind="ExternalOutput").ap()
                dbg_g = nc.dram_tensor("dbg_g", [128, T], F32,
                                       kind="ExternalOutput").ap()
                dbg_u = nc.dram_tensor("dbg_u", [128, T], F32,
                                       kind="ExternalOutput").ap()
                dbg_g16 = nc.dram_tensor("dbg_g16", [128, T], I16,
                                         kind="ExternalOutput").ap()
                dbg_g216 = nc.dram_tensor("dbg_g216", [128, T], I16,
                                          kind="ExternalOutput").ap()
                dbg_u16 = nc.dram_tensor("dbg_u16", [128, T], I16,
                                         kind="ExternalOutput").ap()
                dbg_wgq = nc.dram_tensor("dbg_wgq", [256, D], BF16,
                                         kind="ExternalOutput").ap()
                with nc.Block() as blk:
                    def s_tap(sy, mv=vD.n):
                        sy.wait_ge(vD.h, mv)
                        qS.inc(sy.dma_start(out=dbg_wgq, in_=wg_q[0:256, :]), 16)
                        qS.inc(sy.dma_start(out=dbg_ssc, in_=SSC[:]), 16)
                        qS.inc(sy.dma_start(out=dbg_cols, in_=COLS[:]), 16)
                        qS.inc(sy.dma_start(out=dbg_g, in_=G_d[0]), 16)
                        qS.inc(sy.dma_start(out=dbg_u, in_=U_d[0]), 16)
                        qS.inc(sy.dma_start(out=dbg_g16, in_=g16_d[0]), 16)
                        qS.inc(sy.dma_start(out=dbg_g216, in_=g216_d[0]), 16)
                        qS.inc(sy.dma_start(out=dbg_u16, in_=u16_d[0]), 16)
                    blk.sync(s_tap)
            with nc.Block() as blk:
                def s_out(sy, mv=vD.n):
                    sy.wait_ge(vD.h, mv)
                    qS.inc(sy.dma_start(
                        out=y.rearrange("c p t -> p c t"), in_=XA[:]), 16)
                    sy.wait_ge(qS.h, qS.n)
                blk.sync(s_out)
            esXA.close()
            return nc

        xaow_d = dsc("xaow_d", [CT, 128, T], BF16)
        with contextlib.ExitStack() as esL2:
            XAOW = esL2.enter_context(nc.sbuf_tensor("XAOW", [128, CT, T],
                                                     BF16))
            SQ2 = esL2.enter_context(nc.sbuf_tensor("SQ2", [128, 2, T], F32))
            emit_ln(XA, SQ2, False, XAOW)
            with nc.Block() as blk:
                def s_sp(sy, mv=vD.n):
                    sy.wait_ge(vD.h, mv)
                    qS.inc(sy.dma_start(out=x1_d.rearrange("c p t -> p c t"),
                                        in_=XA[:]), 16)
                    qS.inc(sy.dma_start(out=xaow_d.rearrange("c p t -> p c t"),
                                        in_=XAOW[:]), 16)
                blk.sync(s_sp)
        esXA.close()

        with contextlib.ExitStack() as esB:
            XB = esB.enter_context(nc.sbuf_tensor("XB", [128, CT, T], F32))
            with nc.Block() as blk:
                def s_lb(sy, mq_sp=qS.n):
                    sy.wait_ge(qS.h, mq_sp)
                    qS.inc(sy.dma_start(
                        out=XB[:], in_=x_oth.rearrange("c p t -> p c t")), 16)
                blk.sync(s_lb)
            emit_ffn(XB)
            XAB = esB.enter_context(nc.sbuf_tensor("XAB", [128, CT, T], BF16))
            SQ3 = esB.enter_context(nc.sbuf_tensor("SQ3", [128, 2, T], F32))
            emit_ln(XB, SQ3, False, XAB)

            WK = esB.enter_context(nc.sbuf_tensor("WK", [128, CT, D], BF16))
            WV = esB.enter_context(nc.sbuf_tensor("WV", [128, CT, D], BF16))
            KTMP = esB.enter_context(nc.sbuf_tensor("KTMP", [128, D], BF16))
            ROT = esB.enter_context(nc.sbuf_tensor("ROT", [128, D], BF16))
            VTMP = esB.enter_context(nc.sbuf_tensor("VTMP", [128, D], BF16))
            T0 = esB.enter_context(nc.sbuf_tensor("T0", [128, D], F32))
            T1 = esB.enter_context(nc.sbuf_tensor("T1", [128, D], F32))
            KS = esB.enter_context(nc.sbuf_tensor("KS", [128, D], BF16))
            COSB = esB.enter_context(nc.sbuf_tensor("COSB", [128, NCH, DH],
                                                    F32))
            SINB = esB.enter_context(nc.sbuf_tensor("SINB", [128, NCH, DH],
                                                    F32))
            GCT = esB.enter_context(nc.sbuf_tensor("GCT", [128, 512], F32))
            KSC = esB.enter_context(nc.sbuf_tensor("KSC", [128, D], F32))
            with nc.Block() as blk:
                def s_lw2(sy):
                    qS.inc(sy.dma_start(
                        out=COSB[:],
                        in_=cos_oth.rearrange("(n p) d -> p n d", p=128)), 16)
                    qS.inc(sy.dma_start(
                        out=SINB[:],
                        in_=sinS_oth.rearrange("(n p) d -> p n d", p=128)), 16)
                    qS.inc(sy.dma_start(out=GCT[:], in_=gctab_i), 16)
                    qS.inc(sy.dma_start(out=KSC[:], in_=kscT_i), 16)
                blk.sync(s_lw2)

                def g_lw2(g):
                    qG.inc(g.dma_start(
                        out=WK[:],
                        in_=wkT.rearrange("(c p) d -> p c d", p=128)), 16)
                    qG.inc(g.dma_start(
                        out=WV[:],
                        in_=wvT.rearrange("(c p) d -> p c d", p=128)), 16)
                blk.gpsimd(g_lw2)
                m_w2 = qG.n
                m_c2 = qS.n
                mv_xab = vD.n

                def v_z(v):
                    v.memset(SST[:], 0.0)
                blk.vector(v_z)

                ev_k = []
                for tt in range(NCH):
                    def t_kv(te, tt=tt):
                        if tt == 0:
                            te.wait_ge(qG.h, m_w2)
                            te.wait_ge(qS.h, m_c2)
                            te.wait_ge(vD.h, mv_xab)
                        else:
                            te.wait_ge(aD.h, ev_k[tt - 1][0])
                            te.wait_ge(vD.h, ev_k[tt - 1][1])
                        for dq in range(2):
                            for c in range(CT):
                                mmk = te.matmul(
                                    PB[dq][:, :],
                                    XAB[:, c, tt * 128:(tt + 1) * 128],
                                    WK[:, c, dq * 512:(dq + 1) * 512],
                                    start=(c == 0), stop=(c == CT - 1))
                        for dq in range(2):
                            for c in range(CT):
                                mmv = te.matmul(
                                    PB[2 + dq][:, :],
                                    XAB[:, c, tt * 128:(tt + 1) * 128],
                                    WV[:, c, dq * 512:(dq + 1) * 512],
                                    start=(c == 0), stop=(c == CT - 1))
                        pD.inc(mmv)
                    blk.tensor(t_kv)
                    mpk = pD.n

                    def s_ev(sc, tt=tt, mpk=mpk):
                        sc.wait_ge(pD.h, mpk)
                        sc.copy(KTMP[:, 0:512], PB[0][:, :])
                        sc.copy(KTMP[:, 512:1024], PB[1][:, :])
                        sc.copy(VTMP[:, 0:512], PB[2][:, :])
                        sc.copy(VTMP[:, 512:1024], PB[3][:, :])
                        kv = KTMP[:].rearrange("p (h g j) -> p h g j", g=2,
                                               j=32)
                        rv = ROT[:].rearrange("p (h g j) -> p h g j", g=2,
                                              j=32)
                        sc.copy(rv[:, :, 0, :], kv[:, :, 1, :])
                        sc.copy(rv[:, :, 1, :], kv[:, :, 0, :])
                    SCA(blk, s_ev)
                    mak = aD.n

                    def v_rp(v, tt=tt, mak=mak):
                        v.wait_ge(aD.h, mak)
                        cb = COSB[:, tt, :].unsqueeze(1).broadcast_to(
                            [128, NH, DH])
                        sb = SINB[:, tt, :].unsqueeze(1).broadcast_to(
                            [128, NH, DH])
                        kv = KTMP[:].rearrange("p (h d) -> p h d", d=DH)
                        rv = ROT[:].rearrange("p (h d) -> p h d", d=DH)
                        v.tensor_tensor(T0[:].rearrange("p (h d) -> p h d",
                                                        d=DH), kv, cb,
                                        op=ALU.mult)
                        v.tensor_tensor(T1[:].rearrange("p (h d) -> p h d",
                                                        d=DH), rv, sb,
                                        op=ALU.mult)
                        v.tensor_tensor(KTMP[:], T0[:], T1[:], op=ALU.add)
                        v.tensor_tensor(KS[:], KTMP[:], KSC[:], op=ALU.mult)
                    VEC(blk, v_rp)
                    mvk = vD.n

                    def t_st(te, tt=tt, mvk=mvk):
                        te.wait_ge(vD.h, mvk)
                        for h in range(NH):
                            base = 64 * (h % 2)
                            col = (h // 2) * DH
                            mm = te.matmul(
                                PB[4][base:base + 64, col:col + DH],
                                KS[:, h * DH:(h + 1) * DH],
                                VTMP[:, h * DH:(h + 1) * DH],
                                start=True, stop=True)
                        pD.inc(mm)
                    blk.tensor(t_st)
                    mps = pD.n

                    def v_su(v, tt=tt, mps=mps):
                        v.wait_ge(pD.h, mps)
                        v.tensor_tensor(SST[:], SST[:], GCT[:], op=ALU.mult)
                        v.tensor_tensor(SST[:], SST[:], PB[4][:, :],
                                        op=ALU.add)
                    VEC(blk, v_su)
                    ev_k.append((aD.n, vD.n))

            with nc.Block() as blk:
                def v_mk(v):
                    v.wait_ge(vD.h, vD.n)
                    v.tensor_scalar(SST[:], SST[:], MSKC[:, 0:1], None,
                                    op0=ALU.mult)
                VEC(blk, v_mk)

                def s_sb(sc, mv=vD.n):
                    sc.wait_ge(vD.h, mv)
                    sc.copy(SSB[:], SST[:])
                SCA(blk, s_sb)

            if STAGE < 1.7:
                with nc.Block() as blk:
                    def s_o15(sy, mv=vD.n, ma=aD.n):
                        sy.wait_ge(vD.h, mv)
                        sy.wait_ge(aD.h, ma)
                        qS.inc(sy.dma_start(
                            out=y.rearrange("c p t -> p c t"), in_=XB[:]), 16)
                        sy.wait_ge(qS.h, qS.n)
                    blk.sync(s_o15)
                return nc

        esAT = contextlib.ExitStack()
        KT = esAT.enter_context(nc.sbuf_tensor("KT", [128, NCH, D], BF16))
        VT = esAT.enter_context(nc.sbuf_tensor("VT", [128, NCH, D], BF16))
        GSIL = esAT.enter_context(nc.sbuf_tensor("GSIL", [128, CT, T], BF16))
        with contextlib.ExitStack() as esA2:
            XAP = esA2.enter_context(nc.sbuf_tensor("XAP", [128, CT, T],
                                                    BF16))
            WB1 = esA2.enter_context(nc.sbuf_tensor("WB1", [128, CT, D],
                                                    BF16))
            KTMP = esA2.enter_context(nc.sbuf_tensor("KTMP2", [128, D], BF16))
            ROT = esA2.enter_context(nc.sbuf_tensor("ROT2", [128, D], BF16))
            T0 = esA2.enter_context(nc.sbuf_tensor("T02", [128, D], F32))
            T1 = esA2.enter_context(nc.sbuf_tensor("T12", [128, D], F32))
            COSB = esA2.enter_context(nc.sbuf_tensor("COSB2", [128, NCH, DH],
                                                     F32))
            SINB = esA2.enter_context(nc.sbuf_tensor("SINB2", [128, NCH, DH],
                                                     F32))

            with nc.Block() as blk:
                def s_lo(sy):
                    qS.inc(sy.dma_start(
                        out=XAP[:], in_=xaow_d.rearrange("c p t -> p c t")),
                        16)
                    qS.inc(sy.dma_start(
                        out=COSB[:],
                        in_=cos_own.rearrange("(n p) d -> p n d", p=128)), 16)
                    qS.inc(sy.dma_start(
                        out=SINB[:],
                        in_=sinS_own.rearrange("(n p) d -> p n d", p=128)), 16)
                blk.sync(s_lo)

            def proj_tm(wsrc, dst, rope):
                with nc.Block() as blk:
                    mp_prev = pD.n

                    def g_w(g, wsrc=wsrc, mp_prev=mp_prev):
                        g.wait_ge(pD.h, mp_prev)
                        qG.inc(g.dma_start(
                            out=WB1[:],
                            in_=wsrc.rearrange("(c p) d -> p c d", p=128)),
                            16)
                    blk.gpsimd(g_w)
                    m_w = qG.n
                    m_ld = qS.n
                    ev_m = []
                    for tt in range(NCH):
                        def t_p(te, tt=tt):
                            if tt == 0:
                                te.wait_ge(qG.h, m_w)
                                te.wait_ge(qS.h, m_ld)
                            if tt >= 2:
                                te.wait_ge(vD.h, ev_m[tt - 2][1])
                                te.wait_ge(aD.h, ev_m[tt - 2][0])
                            for dq in range(2):
                                for c in range(CT):
                                    mm = te.matmul(
                                        PB[2 * (tt % 2) + dq][:, :],
                                        XAP[:, c, tt * 128:(tt + 1) * 128],
                                        WB1[:, c, dq * 512:(dq + 1) * 512],
                                        start=(c == 0), stop=(c == CT - 1))
                            pD.inc(mm)
                        blk.tensor(t_p)
                        mpk = pD.n

                        def s_e(sc, tt=tt, mpk=mpk, rope=rope, dst=dst,
                                m_ld=m_ld):
                            sc.wait_ge(pD.h, mpk)
                            if tt == 0:
                                sc.wait_ge(qS.h, m_ld)
                            if rope and tt >= 1:
                                sc.wait_ge(vD.h, ev_m[tt - 1][1])
                            tgt = KTMP if rope else dst
                            off = 0 if rope else tt
                            if rope:
                                sc.copy(KTMP[:, 0:512],
                                        PB[2 * (tt % 2)][:, :])
                                sc.copy(KTMP[:, 512:1024],
                                        PB[2 * (tt % 2) + 1][:, :])
                                kv = KTMP[:].rearrange(
                                    "p (h g j) -> p h g j", g=2, j=32)
                                rv = ROT[:].rearrange(
                                    "p (h g j) -> p h g j", g=2, j=32)
                                sc.copy(rv[:, :, 0, :], kv[:, :, 1, :])
                                sc.copy(rv[:, :, 1, :], kv[:, :, 0, :])
                            else:
                                sc.copy(dst[:, tt, 0:512],
                                        PB[2 * (tt % 2)][:, :])
                                sc.copy(dst[:, tt, 512:1024],
                                        PB[2 * (tt % 2) + 1][:, :])
                        SCA(blk, s_e)
                        mak = aD.n

                        def v_r(v, tt=tt, mak=mak, rope=rope, dst=dst):
                            v.wait_ge(aD.h, mak)
                            if rope:
                                cb = COSB[:, tt, :].unsqueeze(1).broadcast_to(
                                    [128, NH, DH])
                                sb = SINB[:, tt, :].unsqueeze(1).broadcast_to(
                                    [128, NH, DH])
                                kv = KTMP[:].rearrange("p (h d) -> p h d",
                                                       d=DH)
                                rv = ROT[:].rearrange("p (h d) -> p h d",
                                                      d=DH)
                                v.tensor_tensor(
                                    T0[:].rearrange("p (h d) -> p h d", d=DH),
                                    kv, cb, op=ALU.mult)
                                v.tensor_tensor(
                                    T1[:].rearrange("p (h d) -> p h d", d=DH),
                                    rv, sb, op=ALU.mult)
                                v.tensor_tensor(dst[:, tt, :], T0[:], T1[:],
                                                op=ALU.add)
                        VEC(blk, v_r)
                        ev_m.append((aD.n, vD.n))

            proj_tm(wqT, KT, True)
            with nc.Block() as blk:
                def s_bq(sy, mv=vD.n):
                    sy.wait_ge(vD.h, mv)
                    qS.inc(sy.dma_start(
                        out=trb_d.rearrange("(n p) d -> p n d", p=128),
                        in_=KT[:]), 16)
                blk.sync(s_bq)
            proj_tm(wkT, KT, True)
            proj_tm(wvT, VT, False)

            with nc.Block() as blk:
                mp_prev = pD.n

                def g_wg(g, mp_prev=mp_prev):
                    g.wait_ge(pD.h, mp_prev)
                    qG.inc(g.dma_start(
                        out=WB1[:],
                        in_=wgTT.rearrange("(c p) d -> p c d", p=128)), 16)
                blk.gpsimd(g_wg)
                m_w = qG.n
                ev_g = []
                gk = 0
                for dg in range(CT):
                    for q in range(4):
                        def t_gp(te, dg=dg, q=q, gk=gk):
                            if gk == 0:
                                te.wait_ge(qG.h, m_w)
                            if gk >= 2:
                                te.wait_ge(aD.h, ev_g[gk - 2])
                            for c in range(CT):
                                mm = te.matmul(
                                    PB[4 + gk % 2][:, :],
                                    WB1[:, c, dg * 128:(dg + 1) * 128],
                                    XAP[:, c, q * 512:(q + 1) * 512],
                                    start=(c == 0), stop=(c == CT - 1))
                            pD.inc(mm)
                        blk.tensor(t_gp)
                        mpg = pD.n

                        def s_gs(sc, dg=dg, q=q, gk=gk, mpg=mpg):
                            sc.wait_ge(pD.h, mpg)
                            sc.activation(
                                GSIL[:, dg, q * 512:(q + 1) * 512],
                                PB[4 + gk % 2][:, :], AF.Silu)
                        SCA(blk, s_gs)
                        ev_g.append(aD.n)
                        gk += 1

            with nc.Block() as blk:
                def s_bn(sy, mv=vD.n, ma=aD.n):
                    sy.wait_ge(vD.h, mv)
                    sy.wait_ge(aD.h, ma)
                    qS.inc(sy.dma_start(
                        out=trb2_d.rearrange("(n p) d -> p n d", p=128),
                        in_=KT[:]), 16)
                blk.sync(s_bn)

            if STAGE < 1.8:
                with nc.Block() as blk:
                    def g_o17(g, mv=vD.n, ma=aD.n):
                        g.wait_ge(vD.h, mv)
                        g.wait_ge(aD.h, ma)
                        qG.inc(g.dma_start(
                            out=y.rearrange("c p t -> p c t"),
                            in_=GSIL[:]), 16)
                        g.wait_ge(qG.h, qG.n)
                    blk.gpsimd(g_o17)
                esA2.close()
                esAT.close()
                return nc

        if STAGE == 1.85:
            with contextlib.ExitStack() as esT:
                QF85 = esT.enter_context(
                    nc.sbuf_tensor("QF85", [128, CT, T], BF16))
                KF85 = esT.enter_context(
                    nc.sbuf_tensor("KF85", [128, CT, T], BF16))
                YF85 = esT.enter_context(
                    nc.sbuf_tensor("YF85", [128, T], F32))
                with nc.Block() as blk:
                    def s_tr85(sy):
                        sy.wait_ge(qS.h, qS.n)
                        for i in range(CT):
                            qS.inc(sy.dma_start_transpose(
                                QF85[:, i, :],
                                trb_d[:, i * 128:(i + 1) * 128]), 16)
                        for i in range(CT):
                            qS.inc(sy.dma_start_transpose(
                                KF85[:, i, :],
                                trb2_d[:, i * 128:(i + 1) * 128]), 16)
                    blk.sync(s_tr85)
                    m_tr85 = qS.n
                    srcs = [(QF85[:, 0, :], T), (QF85[:, 7, :], T),
                            (KF85[:, 0, :], T), (KF85[:, 7, :], T),
                            (SST[:], 512), (SSB[:], 512),
                            (KT[:, 0, :], D), (VT[:, 0, :], D)]
                    for i85, (src, w85) in enumerate(srcs):
                        def v_c85(v, i85=i85, src=src, w85=w85,
                                  m_tr85=m_tr85, mq=qS.n, ma=aD.n):
                            v.wait_ge(qS.h, m_tr85 if i85 == 0 else mq)
                            if i85 == 0:
                                v.wait_ge(aD.h, ma)
                            if w85 < T:
                                v.memset(YF85[:], 0.0)
                            v.tensor_copy(YF85[:, 0:w85], src)
                        VEC(blk, v_c85)
                        mv85 = vD.n

                        def s_s85(sy, i85=i85, mv85=mv85):
                            sy.wait_ge(vD.h, mv85)
                            qS.inc(sy.dma_start(out=y[i85], in_=YF85[:]), 16)
                            if i85 == 7:
                                sy.wait_ge(qS.h, qS.n)
                        blk.sync(s_s85)
            esAT.close()
            return nc

        with contextlib.ExitStack() as esA3:
            QF = esA3.enter_context(nc.sbuf_tensor("QF", [128, CT, T], BF16))
            KF = esA3.enter_context(nc.sbuf_tensor("KF", [128, CT, T], BF16))
            DCT = esA3.enter_context(nc.sbuf_tensor("DCT", [128, NH * CK],
                                                    F32))
            QSC = esA3.enter_context(nc.sbuf_tensor("QSC", [128, CT * CK],
                                                    F32))
            KSC2 = esA3.enter_context(nc.sbuf_tensor("KSC2", [128, D], F32))
            GCT2 = esA3.enter_context(nc.sbuf_tensor("GCT2", [128, 512], F32))
            QS = esA3.enter_context(nc.sbuf_tensor("QS", [128, CT, CK], BF16))
            KS2 = esA3.enter_context(nc.sbuf_tensor("KS2", [128, D], BF16))
            ATS = esA3.enter_context(nc.sbuf_tensor("ATS", [128, 4, 512],
                                                    BF16))
            GTD = esA3.enter_context(nc.sbuf_tensor("GTD", [128, 2, 512],
                                                    BF16))

            with nc.Block() as blk:
                def s_tr(sy):
                    sy.wait_ge(qS.h, qS.n)
                    for i in range(CT):
                        qS.inc(sy.dma_start_transpose(
                            QF[:, i, :], trb_d[:, i * 128:(i + 1) * 128]), 16)
                    for i in range(CT):
                        qS.inc(sy.dma_start_transpose(
                            KF[:, i, :], trb2_d[:, i * 128:(i + 1) * 128]),
                            16)
                    qS.inc(sy.dma_start(out=DCT[:], in_=decayT_i), 16)
                    qS.inc(sy.dma_start(out=QSC[:], in_=qscT_i), 16)
                    qS.inc(sy.dma_start(out=KSC2[:], in_=kscT_i), 16)
                    qS.inc(sy.dma_start(out=GCT2[:], in_=gctab_i), 16)
                blk.sync(s_tr)
                m_tr = qS.n

                prev = {(-1, k): 0 for k in
                        ("mm", "ats", "su", "ssb", "gated", "gst")}
                QSCv = QSC[:].rearrange("p (c k) -> p c k", c=CT)
                GSILv = GSIL
                NSC = int(os.environ.get("KNSC", str(NCH)))
                SCP = int(os.environ.get("KSCPART", "5"))
                for n_ in range(NSC):
                    def v_qk(v, n_=n_):
                        if n_ == 0:
                            v.wait_ge(qS.h, m_tr)
                        v.wait_ge(pD.h, prev[(n_ - 1, "mm")])
                        v.tensor_tensor(QS[:],
                                        QF[:, :, n_ * 128:(n_ + 1) * 128],
                                        QSCv, op=ALU.mult)
                        v.tensor_tensor(KS2[:], KT[:, n_, :], KSC2[:],
                                        op=ALU.mult)
                    VEC(blk, v_qk)
                    prev[(n_, "qk")] = vD.n

                    if SCP < 2:
                        prev[(n_, "ats")] = vD.n
                        prev[(n_, "mm")] = pD.n
                        prev[(n_, "su")] = vD.n
                        prev[(n_, "ssb")] = aD.n
                        prev[(n_, "gated")] = vD.n
                        continue
                    at_m = {}
                    NBG = int(os.environ.get("KNBG", "4"))
                    for bg in range(NBG):
                        def t_sc(te, n_=n_, bg=bg):
                            if bg == 0:
                                te.wait_ge(vD.h, prev[(n_, "qk")])
                                te.wait_ge(pD.h, prev[(n_ - 1, "mm")])
                            if bg >= 2:
                                te.wait_ge(vD.h, at_m[bg - 2])
                            for hh in range(4):
                                h = bg * 4 + hh
                                base = 64 * (h % 2)
                                for _ in range(4):
                                    te.nop()
                                mm = te.matmul(
                                    PB[bg % 2][:, hh * 128:(hh + 1) * 128],
                                    KF[base:base + 64, h // 2,
                                       n_ * 128:(n_ + 1) * 128],
                                    QF[base:base + 64, h // 2,
                                       n_ * 128:(n_ + 1) * 128],
                                    start=True, stop=True)
                            pD.inc(mm)
                        blk.tensor(t_sc)
                        msc = pD.n

                        def v_at(v, n_=n_, bg=bg, msc=msc):
                            v.wait_ge(pD.h, msc)
                            v.tensor_tensor(
                                ATS[:, bg, :], PB[bg % 2][:, :],
                                DCT[:, bg * 512:(bg + 1) * 512], op=ALU.mult)
                        VEC(blk, v_at)
                        at_m[bg] = vD.n
                    prev[(n_, "ats")] = vD.n
                    if NBG < 4:
                        prev[(n_, "mm")] = pD.n
                        prev[(n_, "su")] = vD.n
                        prev[(n_, "ssb")] = aD.n
                        prev[(n_, "gated")] = vD.n
                        continue

                    if SCP < 3:
                        prev[(n_, "mm")] = pD.n
                        prev[(n_, "su")] = vD.n
                        prev[(n_, "ssb")] = aD.n
                        prev[(n_, "gated")] = vD.n
                        continue

                    def t_ov(te, n_=n_):
                        te.wait_ge(vD.h, prev[(n_, "ats")])
                        te.wait_ge(vD.h, prev[(n_ - 1, "gated")])
                        te.wait_ge(vD.h, prev[(n_ - 1, "su")])
                        te.wait_ge(aD.h, prev[(n_ - 1, "ssb")])
                        for _ in range(4):
                            te.nop()
                        for h in range(NH):
                            base = 64 * (h % 2)
                            bank = 2 + (h // 8)
                            col = 128 * ((h // 2) % 4)
                            te.matmul(
                                PB[bank][base:base + 64, col:col + 128],
                                VT[:, n_, h * DH:(h + 1) * DH],
                                ATS[:, h // 4,
                                    (h % 4) * 128:(h % 4 + 1) * 128],
                                start=True, stop=False)
                            for _ in range(4 if base else 0):
                                te.nop()
                            te.matmul(
                                PB[bank][base:base + 64, col:col + 128],
                                SSB[base:base + 64,
                                    (h // 2) * DH:(h // 2) * DH + DH],
                                QS[base:base + 64, h // 2, :],
                                start=False, stop=True)
                            for _ in range(4 if base else 0):
                                te.nop()
                        for h in range(NH):
                            base = 64 * (h % 2)
                            col = (h // 2) * DH
                            mm = te.matmul(
                                PB[4][base:base + 64, col:col + DH],
                                KS2[:, h * DH:(h + 1) * DH],
                                VT[:, n_, h * DH:(h + 1) * DH],
                                start=True, stop=True)
                        pD.inc(mm)
                    blk.tensor(t_ov)
                    prev[(n_, "mm")] = pD.n

                    if SCP < 4:
                        prev[(n_, "su")] = vD.n
                        prev[(n_, "ssb")] = aD.n
                        prev[(n_, "gated")] = vD.n
                        continue

                    def v_su(v, n_=n_):
                        v.wait_ge(pD.h, prev[(n_, "mm")])
                        v.tensor_tensor(SST[:], SST[:], GCT2[:], op=ALU.mult)
                        v.tensor_tensor(SST[:], SST[:], PB[4][:, :],
                                        op=ALU.add)
                    VEC(blk, v_su)
                    prev[(n_, "su")] = vD.n

                    def s_sb2(sc, n_=n_):
                        sc.wait_ge(vD.h, prev[(n_, "su")])
                        sc.copy(SSB[:], SST[:])
                    SCA(blk, s_sb2)
                    prev[(n_, "ssb")] = aD.n

                    if SCP < 5:
                        prev[(n_, "gated")] = vD.n
                        continue

                    gt_m = {}
                    for b2 in range(2):
                        def v_gt(v, n_=n_, b2=b2, mgt=dsGT.n(b2)):
                            v.wait_ge(pD.h, prev[(n_, "mm")])
                            v.wait_ge(dsGT.h(b2), mgt)
                            v.tensor_tensor(
                                GTD[:, b2, :].rearrange("p (c k) -> p c k",
                                                        k=128),
                                PB[2 + b2][:, :].rearrange(
                                    "p (c k) -> p c k", k=128),
                                GSILv[:, 4 * b2:4 * b2 + 4,
                                      n_ * 128:(n_ + 1) * 128], op=ALU.mult)
                        VEC(blk, v_gt)
                        gt_m[b2] = vD.n
                    prev[(n_, "gated")] = vD.n

                    def g_gst(g, n_=n_, gt_m=dict(gt_m)):
                        for b2 in range(2):
                            g.wait_ge(vD.h, gt_m[b2])
                            dsGT.issue(g.dma_start(
                                out=gated_d[4 * b2:4 * b2 + 4, :,
                                            n_ * 128:(n_ + 1) * 128
                                            ].rearrange("c p k -> p c k"),
                                in_=GTD[:, b2, :].rearrange(
                                    "p (c k) -> p c k", k=128)), b2)
                    blk.gpsimd(g_gst)
                    prev[(n_, "gst")] = 0
        esAT.close()

        if STAGE < 2:
            dbg18 = nc.dram_tensor("dbg18", [CT, 128, T], I16,
                                   kind="ExternalOutput").ap()
            with (
                nc.sbuf_tensor("Y18F", [128, T], F32) as Y18F,
                nc.sbuf_tensor("G18B", [128, T], BF16) as G18B,
                nc.sbuf_tensor("N18I", [128, T], I16) as N18I,
                nc.Block() as blk,
            ):
                def v_z18(v):
                    v.memset(Y18F[:], 0.0)
                VEC(blk, v_z18)
                mvz = vD.n

                def s_y18(sy, mvz=mvz):
                    sy.wait_ge(vD.h, mvz)
                    for c18 in range(CT):
                        qS.inc(sy.dma_start(out=y[c18], in_=Y18F[:]), 16)
                blk.sync(s_y18)
                for c18 in range(CT):
                    def s_l18(sy, c18=c18, mg0=dsGT.n(0), mg1=dsGT.n(1),
                              mvp=vD.n):
                        if c18 == 0:
                            sy.wait_ge(dsGT.h(0), mg0)
                            sy.wait_ge(dsGT.h(1), mg1)
                        sy.wait_ge(vD.h, mvp)
                        qS.inc(sy.dma_start(out=G18B[:], in_=gated_d[c18]),
                               16)
                    blk.sync(s_l18)
                    mq18 = qS.n

                    def v_n18(v, mq18=mq18):
                        v.wait_ge(qS.h, mq18)
                        v.tensor_tensor(N18I[:], G18B[:], G18B[:],
                                        op=ALU.is_equal)
                    VEC(blk, v_n18)
                    mv18 = vD.n

                    def s_s18(sy, c18=c18, mv18=mv18):
                        sy.wait_ge(vD.h, mv18)
                        qS.inc(sy.dma_start(out=dbg18[c18], in_=N18I[:]), 16)
                        if c18 == CT - 1:
                            sy.wait_ge(qS.h, qS.n)
                    blk.sync(s_s18)
            return nc

        esX2 = contextlib.ExitStack()
        X2 = esX2.enter_context(nc.sbuf_tensor("X2", [128, CT, T], F32))
        with contextlib.ExitStack() as esA4:
            GSTF = esA4.enter_context(nc.sbuf_tensor("GSTF", [128, CT, T],
                                                     BF16))
            WO = esA4.enter_context(nc.sbuf_tensor("WO", [128, CT, D], BF16))
            X1S = esA4.enter_context(nc.sbuf_tensor("X1S", [128, 2, T], F32))
            with nc.Block() as blk:
                gt_gates = [dsGT.n(0), dsGT.n(1)]
                mq_x1 = qS.n

                def s_lg(sy, gt_gates=gt_gates, mq_x1=mq_x1):
                    sy.wait_ge(dsGT.h(0), gt_gates[0])
                    sy.wait_ge(dsGT.h(1), gt_gates[1])
                    sy.wait_ge(qS.h, mq_x1)
                    qS.inc(sy.dma_start(
                        out=GSTF[:],
                        in_=gated_d.rearrange("c p t -> p c t")), 16)
                blk.sync(s_lg)

                def g_lw(g):
                    qG.inc(g.dma_start(
                        out=WO[:],
                        in_=woT.rearrange("(c p) d -> p c d", p=128)), 16)
                blk.gpsimd(g_lw)
                m_lg = qS.n
                m_lw = qG.n
                ev_w = {-1: vD.n, -2: vD.n}
                for dt in range(CT):
                    def s_lx(sy, dt=dt):
                        sy.wait_ge(vD.h, ev_w[dt - 2])
                        dsX1.issue(sy.dma_start(out=X1S[:, dt % 2, :],
                                                in_=x1_d[dt]), dt % 2)
                    blk.sync(s_lx)
                    m_lx = dsX1.n(dt % 2)
                    for q in range(4):
                        gk = dt * 4 + q

                        def t_wo(te, dt=dt, q=q, gk=gk, m_lx=m_lx):
                            if gk == 0:
                                te.wait_ge(qS.h, m_lg)
                                te.wait_ge(qG.h, m_lw)
                            if gk >= 2:
                                te.wait_ge(vD.h, ev_w[-2] if gk < 2 else
                                           ev_w.get(("g", gk - 2), 0))
                            for c in range(CT):
                                mm = te.matmul(
                                    PB[gk % 2][:, :],
                                    WO[:, c, dt * 128:(dt + 1) * 128],
                                    GSTF[:, c, q * 512:(q + 1) * 512],
                                    start=(c == 0), stop=(c == CT - 1))
                            pD.inc(mm)
                        blk.tensor(t_wo)
                        mpw = pD.n

                        def v_x2(v, dt=dt, q=q, mpw=mpw, m_lx=m_lx):
                            v.wait_ge(pD.h, mpw)
                            v.wait_ge(dsX1.h(dt % 2), m_lx)
                            v.tensor_tensor(
                                X2[:, dt, q * 512:(q + 1) * 512],
                                X1S[:, dt % 2, q * 512:(q + 1) * 512],
                                PB[gk % 2][:, :], op=ALU.add)
                        VEC(blk, v_x2)
                        ev_w[("g", gk)] = vD.n
                    ev_w[dt] = vD.n

        if STAGE >= 3:
            emit_ffn(X2)

        with nc.Block() as blk:
            def s_out(sy, mv=vD.n):
                sy.wait_ge(vD.h, mv)
                qS.inc(sy.dma_start(
                    out=y.rearrange("c p t -> p c t"), in_=X2[:]), 16)
                sy.wait_ge(qS.h, qS.n)
            blk.sync(s_out)
        esX2.close()



    return nc


_CONSTS = None


def kernel(**inputs):
    global _CONSTS
    if _CONSTS is None:
        _CONSTS = _host_consts()
    cns = _CONSTS
    x = np.ascontiguousarray(inputs["L0_attn_x"])
    xT = np.ascontiguousarray(np.swapaxes(x, 1, 2))

    def wT(a):
        return np.ascontiguousarray(np.asarray(a).T)

    base = dict(
        wqT=wT(inputs["L0_attn_wq"]), wkT=wT(inputs["L0_attn_wk"]),
        wvT=wT(inputs["L0_attn_wv"]), wgTT=wT(inputs["L0_attn_wg"]),
        woT=wT(inputs["L0_attn_wo"]),
        w_gate=np.ascontiguousarray(inputs["L0_ffn_w_gate"]),
        w_up=np.ascontiguousarray(inputs["L0_ffn_w_up"]),
        w_down=np.ascontiguousarray(inputs["L0_ffn_w_down"]),
        decayT=cns["decayT"], qscT=cns["qscT"], kscT=cns["kscT"],
        gctab=cns["gctab"], ident=cns["ident"],
    )
    in_maps = []
    for c in range(8):
        b, h = c // 2, c % 2
        own = slice(h * T, (h + 1) * T)
        oth = slice((1 - h) * T, (2 - h) * T)
        m = dict(base)
        m["x_own"] = np.ascontiguousarray(
            xT[b, :, own].reshape(CT, 128, T))
        m["x_oth"] = np.ascontiguousarray(
            xT[b, :, oth].reshape(CT, 128, T))
        m["cos_own"] = np.ascontiguousarray(cns["cos_t"][own])
        m["sinS_own"] = np.ascontiguousarray(cns["sinS_t"][own])
        m["cos_oth"] = np.ascontiguousarray(cns["cos_t"][oth])
        m["sinS_oth"] = np.ascontiguousarray(cns["sinS_t"][oth])
        m["maskS"] = np.full((128, 1), float(h), np.float32)
        in_maps.append(m)

    nc = build_nc()
    res = run_bass_kernel_spmd(nc, in_maps, core_ids=list(range(8)))
    out = np.empty((B, S, D), np.float32)
    for c in range(8):
        b, h = c // 2, c % 2
        yv = res.results[c]["y"].reshape(D, T)
        out[b, h * T:(h + 1) * T, :] = yv.T
    return out


# version 09
# speedup vs baseline: 1.6966x; 1.6870x over previous
import os
import contextlib
import numpy as np
import ml_dtypes
import concourse.bass as bass
import concourse.mybir as mybir
from concourse.bass_utils import run_bass_kernel_spmd

F32 = mybir.dt.float32
BF16 = mybir.dt.bfloat16
I16 = mybir.dt.int16
AF = mybir.ActivationFunctionType
ALU = mybir.AluOpType
AX = mybir.AxisListType

MAGIC = float(3 * 2 ** 22)
B, S, D, I3 = 4, 4096, 1024, 3072
T = S // 2
NH, DH, CK = 16, 64, 128
NCH = T // CK
CT = D // 128
OT = I3 // 128
QMAX = 32767.0
EPS = 1e-5
STAGE = float(os.environ.get("KSTAGE", "3"))


def _host_consts():
    f64 = np.float64
    inv = 1.0 / (10000.0 ** (np.arange(0, DH, 2, dtype=f64) / DH))
    t = np.arange(S, dtype=f64)
    fr = np.outer(t, inv)
    cos_t = np.concatenate([np.cos(fr), np.cos(fr)], axis=1).astype(np.float32)
    sn = np.sin(fr)
    sinS_t = np.concatenate([-sn, sn], axis=1).astype(np.float32)
    gam = 1.0 - np.exp2(-5.0 - np.arange(NH, dtype=f64))
    i = np.arange(CK)
    dm = i[None, :] - i[:, None]
    low = (dm >= 0).astype(f64)
    decayT = np.zeros((CK, NH * CK), np.float32)
    for h in range(NH):
        decayT[:, h * CK:(h + 1) * CK] = (
            np.power(gam[h], np.maximum(dm, 0)) * low / 8.0).astype(np.float32)
    qscT = np.zeros((128, CT * CK), np.float32)
    for m in range(CT):
        for hf in range(2):
            qscT[hf * 64:(hf + 1) * 64, m * CK:(m + 1) * CK] = (
                np.power(gam[2 * m + hf], i + 1) / 8.0).astype(np.float32)[None, :]
    kscT = np.zeros((CK, D), np.float32)
    for h in range(NH):
        kscT[:, h * DH:(h + 1) * DH] = np.power(
            gam[h], CK - 1 - i).astype(np.float32)[:, None]
    gctab = np.zeros((128, 512), np.float32)
    for m in range(CT):
        for hf in range(2):
            gctab[hf * 64:(hf + 1) * 64, m * DH:(m + 1) * DH] = float(
                np.power(gam[2 * m + hf], CK))
    ident = np.eye(128, dtype=np.float32)
    return dict(cos_t=cos_t, sinS_t=sinS_t, decayT=decayT, qscT=qscT,
                kscT=kscT, gctab=gctab, ident=ident)


class Sem:
    def __init__(self, es, nc, name):
        self.h = es.enter_context(nc.semaphore(name))
        self.n = 0

    def inc(self, instr, k=1):
        instr.then_inc(self.h, k)
        self.n += k
        return self.n


_FENCED = {"tensor_tensor", "tensor_scalar", "tensor_reduce", "reciprocal",
           "tensor_copy", "memset", "activation", "copy",
           "scalar_tensor_tensor", "mul", "add"}


class DSlot:

    def __init__(self, es, nc, name, nslots):
        self.sems = [Sem(es, nc, "%s%d" % (name, i)) for i in range(nslots)]

    def issue(self, instr, slot):
        return self.sems[slot].inc(instr, 16)

    def wait(self, eng, slot):
        eng.wait_ge(self.sems[slot].h, self.sems[slot].n)

    def n(self, slot):
        return self.sems[slot].n

    def h(self, slot):
        return self.sems[slot].h


class FE:

    def __init__(self, eng, sem):
        self._e = eng
        self._s = sem

    def wait_ge(self, h, n):
        self._e.wait_ge(h, n)

    def dma_start(self, *a, **k):
        return self._e.dma_start(*a, **k)

    def dma_start_transpose(self, *a, **k):
        return self._e.dma_start_transpose(*a, **k)

    def __getattr__(self, name):
        fn = getattr(self._e, name)
        if name in _FENCED:
            s = self._s

            def wrapped(*a, **k):
                ins = fn(*a, **k)
                s.n += 1
                ins.then_inc(s.h, 1)
                self._e.wait_ge(s.h, s.n)
                return ins
            return wrapped
        return fn


def build_nc():
    nc = bass.Bass(target_bir_lowering=False)

    def din(name, shape, dt=F32):
        return nc.dram_tensor(name, shape, dt, kind="ExternalInput").ap()

    x_own = din("x_own", [CT, 128, T])
    x_oth = din("x_oth", [CT, 128, T])
    wqT = din("wqT", [D, D]); wkT = din("wkT", [D, D]); wvT = din("wvT", [D, D])
    wgTT = din("wgTT", [D, D]); woT = din("woT", [D, D])
    w_gate = din("w_gate", [I3, D]); w_up = din("w_up", [I3, D])
    w_down = din("w_down", [D, I3])
    cos_own = din("cos_own", [T, DH]); sinS_own = din("sinS_own", [T, DH])
    cos_oth = din("cos_oth", [T, DH]); sinS_oth = din("sinS_oth", [T, DH])
    decayT_i = din("decayT", [CK, NH * CK])
    qscT_i = din("qscT", [128, CT * CK])
    kscT_i = din("kscT", [CK, D])
    gctab_i = din("gctab", [128, 512])
    maskS_i = din("maskS", [128, 1])
    ident_i = din("ident", [128, 128])

    y = nc.dram_tensor("y", [CT, 128, T], F32, kind="ExternalOutput").ap()

    def dsc(name, shape, dt):
        return nc.dram_tensor(name, shape, dt).ap()

    wg_q = dsc("wg_q", [I3, D], BF16)
    wu_q = dsc("wu_q", [I3, D], BF16)
    wd_q = dsc("wd_q", [D, I3], BF16)
    U_d = dsc("U_d", [OT, 128, T], F32)
    G_d = dsc("G_d", [OT, 128, T], F32)
    g16_d = dsc("g16_d", [OT, 128, T], I16)
    g216_d = dsc("g216_d", [OT, 128, T], I16)
    u16_d = dsc("u16_d", [OT, 128, T], I16)
    x1_d = dsc("x1_d", [CT, 128, T], F32)
    gated_d = dsc("gated_d", [CT, 128, T], BF16)
    trb_d = dsc("trb_d", [T, D], BF16)
    trb2_d = dsc("trb2_d", [T, D], BF16)

    es = contextlib.ExitStack()
    ones_col = es.enter_context(nc.sbuf_tensor("ones_col", [128, 1], F32))
    ones_row = es.enter_context(nc.sbuf_tensor("ones_row", [1, 128], F32))
    COLS = es.enter_context(nc.sbuf_tensor("COLS", [128, 16], F32))
    SSC = es.enter_context(nc.sbuf_tensor("SSC", [1, 64], F32))
    MXC = es.enter_context(nc.sbuf_tensor("MXC", [128, 128], F32))
    IDN = es.enter_context(nc.sbuf_tensor("IDN", [128, 128], F32))
    MSKC = es.enter_context(nc.sbuf_tensor("MSKC", [128, 1], F32))
    SST = es.enter_context(nc.sbuf_tensor("SST", [128, 512], F32))
    SSB = es.enter_context(nc.sbuf_tensor("SSB", [128, 512], BF16))
    PB = [es.enter_context(nc.psum_tensor("pb%d" % k, [128, 512], F32))
          for k in range(8)]

    qS = Sem(es, nc, "qS")
    gD = Sem(es, nc, "gD")
    qA = Sem(es, nc, "qA")
    qV = Sem(es, nc, "qV")
    qG = Sem(es, nc, "qG")
    vD = Sem(es, nc, "vD")
    aD = Sem(es, nc, "aD")
    pD = Sem(es, nc, "pD")
    dsWR = DSlot(es, nc, "dsWR", 2)
    dsWQ = DSlot(es, nc, "dsWQ", 2)
    dsEV = DSlot(es, nc, "dsEV", 2)
    dsGS = DSlot(es, nc, "dsGS", 2)
    dsGI = DSlot(es, nc, "dsGI", 3)
    dsUS = DSlot(es, nc, "dsUS", 2)
    dsUI = DSlot(es, nc, "dsUI", 2)
    dsG2 = DSlot(es, nc, "dsG2", 2)
    dsL4 = DSlot(es, nc, "dsL4", 2)
    dsGT = DSlot(es, nc, "dsGT", 2)
    dsX1 = DSlot(es, nc, "dsX1", 2)

    _uid = [0]

    def un(base):
        _uid[0] += 1
        return "%s_%d" % (base, _uid[0])

    def VEC(blk, fn):
        blk.vector(lambda e: fn(FE(e, vD)))

    def SCA(blk, fn):
        blk.scalar(lambda e: fn(FE(e, aD)))

    def PLC(blk, fn):
        blk.gpsimd(lambda e: fn(FE(e, gD)))

    def EW(blk, fn, pool):
        (PLC if pool else VEC)(blk, fn)

    def ewm(pool):
        s = gD if pool else vD
        return (s.h, s.n)

    iRX, iSX, iRUP, iSUP, iRG, iSG, iK2, iK3, iOS, iSO = range(10)
    (jEPS, jMX, jSX, jRX, jT0, jT1, jSG, jSUP, jSG2, jRG2, jK2, jSO, jRO,
     jK3, jMG, jMUP, jMP, jMP2, jRG, jRUP) = range(20)

    _strbox = [None]

    def STRr(r):
        return _strbox[0][0:1, r * T:(r + 1) * T]

    def final_max(blk, ncols, out_j):
        def v1(v):
            v.tensor_reduce(MXC[:, 127:128], MXC[:, 0:ncols],
                                   op=ALU.max, axis=AX.X)
        VEC(blk, v1)
        mv = vD.n

        def t1(te):
            te.wait_ge(vD.h, mv)
            pD.inc(te.transpose(PB[7][0:1, 0:128], MXC[:, 127:128], IDN[:]))
        blk.tensor(t1)
        mp = pD.n

        def v2(v):
            v.wait_ge(pD.h, mp)
            v.tensor_reduce(SSC[0:1, out_j:out_j + 1], PB[7][0:1, 0:128],
                            op=ALU.max, axis=AX.X)
            v.memset(MXC[:], 0.0)
        VEC(blk, v2)

    def bcast_scalar(blk, src_j, dst_i):
        mv = vD.n
        ma = aD.n

        def t1(te):
            te.wait_ge(vD.h, mv)
            te.wait_ge(aD.h, ma)
            pD.inc(te.matmul(PB[7][:, 0:1], ones_row[:],
                             SSC[0:1, src_j:src_j + 1], start=True, stop=True))
        blk.tensor(t1)
        mp = pD.n

        def s1(sc):
            sc.wait_ge(pD.h, mp)
            sc.copy(COLS[:, dst_i:dst_i + 1], PB[7][:, 0:1])
        SCA(blk, s1)

    def qdq_scale(blk, m_j, s_j, r_j):
        def v1(v):
            v.wait_ge(vD.h, vD.n)
            v.tensor_scalar(SSC[0:1, s_j:s_j + 1], SSC[0:1, m_j:m_j + 1],
                            1.0 / QMAX, 1e-12, op0=ALU.mult, op1=ALU.max)
            v.reciprocal(SSC[0:1, r_j:r_j + 1], SSC[0:1, s_j:s_j + 1])
        VEC(blk, v1)

    def emit_ln(XH, SQ, qdq, XQ):
        esL = contextlib.ExitStack()
        _strbox[0] = esL.enter_context(nc.sbuf_tensor(un("STR"), [1, 2 * T],
                                                      F32))
        mq0 = qS.n
        with nc.Block() as blk:
            pe_done = []
            for ct in range(CT):
                ma_prev = aD.n

                def s_sq(sc, ct=ct, mq0=mq0):
                    if ct == 0:
                        sc.wait_ge(qS.h, mq0)
                    if ct >= 2:
                        sc.wait_ge(pD.h, pe_done[ct - 2])
                    sc.activation(SQ[:, ct % 2, :], XH[:, ct, :],
                                         AF.Square)
                SCA(blk, s_sq)
                ma = aD.n

                def t_mm(te, ct=ct, ma=ma):
                    te.wait_ge(aD.h, ma)
                    for q in range(4):
                        st = (ct == 0)
                        sp = (ct == CT - 1)
                        te.matmul(PB[q][0:1, :], ones_col[:],
                                  XH[:, ct, q * 512:(q + 1) * 512],
                                  start=st, stop=sp)
                        mm = te.matmul(PB[4 + q][0:1, :], ones_col[:],
                                       SQ[:, ct % 2, q * 512:(q + 1) * 512],
                                       start=st, stop=sp)
                    pD.inc(mm)
                blk.tensor(t_mm)
                pe_done.append(pD.n)

            mp = pD.n

            def s_ev(sc):
                sc.wait_ge(pD.h, mp)
                for q in range(4):
                    sc.copy(STRr(0)[0:1, q * 512:(q + 1) * 512], PB[q][0:1, :])
                    sc.copy(STRr(1)[0:1, q * 512:(q + 1) * 512],
                                   PB[4 + q][0:1, :])
            SCA(blk, s_ev)
            ma = aD.n

            def v_st(v):
                v.wait_ge(aD.h, ma)
                v.tensor_scalar(STRr(0), STRr(0), 1.0 / D, None, op0=ALU.mult)
                v.tensor_scalar(STRr(1), STRr(1), 1.0 / D, None, op0=ALU.mult)
                v.tensor_tensor(SQ[0:1, 0, :], STRr(0), STRr(0), op=ALU.mult)
                v.tensor_tensor(STRr(1), STRr(1), SQ[0:1, 0, :],
                                       op=ALU.subtract)
            VEC(blk, v_st)
            mv = vD.n

            def s_sd(sc):
                sc.wait_ge(vD.h, mv)
                sc.activation(STRr(1), STRr(1), AF.Sqrt,
                                     bias=SSC[0:1, jEPS:jEPS + 1])
            SCA(blk, s_sd)
            ma2 = aD.n

            def v_rs(v):
                v.wait_ge(aD.h, ma2)
                v.reciprocal(STRr(1), STRr(1))
            VEC(blk, v_rs)

        if not qdq:
            with nc.Block() as blk:
                last_reads = [vD.n, vD.n]
                for q in range(4):
                    mv0 = last_reads[0]
                    mv1 = last_reads[1]

                    def t_bc(te, q=q, mv0=mv0, mv1=mv1):
                        te.wait_ge(vD.h, max(mv0, mv1, 0))
                        te.matmul(PB[4][:, :], ones_row[:],
                                  STRr(0)[0:1, q * 512:(q + 1) * 512],
                                  start=True, stop=True)
                        pD.inc(te.matmul(PB[5][:, :], ones_row[:],
                                         STRr(1)[0:1, q * 512:(q + 1) * 512],
                                         start=True, stop=True))
                    blk.tensor(t_bc)
                    mp = pD.n

                    def v_nm(v, q=q, mp=mp):
                        v.wait_ge(pD.h, mp)
                        for ct in range(CT):
                            sl = slice(q * 512, (q + 1) * 512)
                            v.tensor_tensor(SQ[:, 0, 0:512], XH[:, ct, sl],
                                            PB[4][:, :], op=ALU.subtract)
                            mm = v.tensor_tensor(XQ[:, ct, sl],
                                                 SQ[:, 0, 0:512],
                                                 PB[5][:, :], op=ALU.mult)
                        mm
                    VEC(blk, v_nm)
                    last_reads = [vD.n, vD.n]
            esL.close()
            return

        with nc.Block() as blk:
            for q in range(4):
                mv = vD.n

                def t_bc(te, q=q, mv=mv):
                    te.wait_ge(vD.h, mv)
                    te.matmul(PB[4][:, :], ones_row[:],
                              STRr(0)[0:1, q * 512:(q + 1) * 512],
                              start=True, stop=True)
                    pD.inc(te.matmul(PB[5][:, :], ones_row[:],
                                     STRr(1)[0:1, q * 512:(q + 1) * 512],
                                     start=True, stop=True))
                blk.tensor(t_bc)
                mp = pD.n

                def v_nm(v, q=q, mp=mp):
                    v.wait_ge(pD.h, mp)
                    for ct in range(CT):
                        sl = slice(q * 512, (q + 1) * 512)
                        v.tensor_tensor(SQ[:, 0, 0:512], XH[:, ct, sl],
                                        PB[4][:, :], op=ALU.subtract)
                        v.tensor_tensor(SQ[:, 1, 0:512], SQ[:, 0, 0:512],
                                        PB[5][:, :], op=ALU.mult)
                        mm = v.tensor_reduce(
                            MXC[:, q * 8 + ct:q * 8 + ct + 1],
                            SQ[:, 1, 0:512], op=ALU.max, axis=AX.X,
                            apply_absolute_value=True)
                    mm
                VEC(blk, v_nm)

        with nc.Block() as blk:
            final_max(blk, 32, jMX)
            qdq_scale(blk, jMX, jSX, jRX)
            bcast_scalar(blk, jSX, iSX)

            def v_rr(v):
                v.wait_ge(vD.h, vD.n)
                v.tensor_scalar(STRr(1), STRr(1), SSC[0:1, jRX:jRX + 1], None,
                                op0=ALU.mult)
                v.tensor_tensor(STRr(0), STRr(0), STRr(1), op=ALU.mult)
            VEC(blk, v_rr)

        with nc.Block() as blk:
            ma0 = aD.n
            for q in range(4):
                mv = vD.n

                def t_bc(te, q=q, mv=mv):
                    te.wait_ge(vD.h, mv)
                    te.wait_ge(aD.h, ma0)
                    te.matmul(PB[4][:, :], ones_row[:],
                              STRr(1)[0:1, q * 512:(q + 1) * 512],
                              start=True, stop=True)
                    pD.inc(te.matmul(PB[5][:, :], ones_row[:],
                                     STRr(0)[0:1, q * 512:(q + 1) * 512],
                                     start=True, stop=True))
                blk.tensor(t_bc)
                mp = pD.n

                def v_qz(v, q=q, mp=mp):
                    v.wait_ge(pD.h, mp)
                    for ct in range(CT):
                        sl = slice(q * 512, (q + 1) * 512)
                        v.tensor_tensor(SQ[:, 0, 0:512], XH[:, ct, sl],
                                        PB[4][:, :], op=ALU.mult)
                        v.tensor_tensor(SQ[:, 1, 0:512], SQ[:, 0, 0:512],
                                        PB[5][:, :], op=ALU.subtract)
                        v.tensor_scalar(SQ[:, 0, 0:512], SQ[:, 1, 0:512],
                                        MAGIC, MAGIC, op0=ALU.add,
                                        op1=ALU.subtract)
                        mm = v.tensor_scalar(XQ[:, ct, sl], SQ[:, 0, 0:512],
                                             COLS[:, iSX:iSX + 1], None,
                                             op0=ALU.mult)
                    mm
                VEC(blk, v_qz)
        esL.close()

    def emit_ffn(XH):
        esF = contextlib.ExitStack()
        WT = esF.enter_context(nc.sbuf_tensor(un("WT_f"), [128, OT, D], BF16))
        WTg = WT[:].rearrange("p o d -> p (o d)").rearrange(
            "p (i o) -> p i o", i=CT)

        with contextlib.ExitStack() as esQ:
            XQ = esQ.enter_context(nc.sbuf_tensor(un("XQ_f"), [128, CT, T], BF16))
            EV = esQ.enter_context(nc.sbuf_tensor(un("EV_f"), [128, 2, 512], F32))
            SQ = esQ.enter_context(nc.sbuf_tensor(un("SQ_f"), [128, 2, T], F32))
            emit_ln(XH, SQ, True, XQ)

            for wsta, wdst, mj, sj, rj, ci in (
                    (wg_q, G_d, jMG, jSG, jRG, iRG),
                    (wu_q, U_d, jMUP, jSUP, jRUP, iRUP)):
                with nc.Block() as blk:
                    mp_wt = pD.n
                    mq0 = dsWQ.n(0)
                    mq1 = dsWQ.n(1)

                    def s_ld(sy, wsta=wsta, mp_wt=mp_wt, mq0=mq0, mq1=mq1):
                        sy.wait_ge(pD.h, mp_wt)
                        sy.wait_ge(qS.h, qS.n)
                        sy.wait_ge(dsWQ.h(0), mq0)
                        sy.wait_ge(dsWQ.h(1), mq1)
                        for i in range(CT):
                            qS.inc(sy.dma_start_transpose(
                                WTg[:, i, :],
                                wsta[:, i * 128:(i + 1) * 128]), 16)
                    blk.sync(s_ld)
                    m_ld = qS.n
                    mv_xq = vD.n

                    evac_done = []
                    red_done = []
                    dma_done = []
                    g = 0
                    for ot in range(OT):
                        for q in range(4):
                            def t_g(te, ot=ot, q=q, g=g):
                                if g == 0:
                                    te.wait_ge(qS.h, m_ld)
                                    te.wait_ge(vD.h, mv_xq)
                                if g >= 2:
                                    te.wait_ge(aD.h, evac_done[g - 2])
                                for i in range(CT):
                                    mm = te.matmul(
                                        PB[g % 2][:, :],
                                        WTg[:, i, ot * 128:(ot + 1) * 128],
                                        XQ[:, i, q * 512:(q + 1) * 512],
                                        start=(i == 0), stop=(i == CT - 1))
                                pD.inc(mm)
                            blk.tensor(t_g)
                            mp = pD.n

                            def s_g(sc, ot=ot, q=q, g=g, mp=mp, wdst=wdst,
                                    mev=dsEV.n(g % 2), mv_xq=mv_xq):
                                sc.wait_ge(pD.h, mp)
                                sc.wait_ge(dsEV.h(g % 2), mev)
                                sc.wait_ge(vD.h, red_done[g - 2]
                                           if g >= 2 else mv_xq)
                                sc.copy(EV[:, g % 2, :],
                                               PB[g % 2][:, :])
                                dsEV.issue(sc.dma_start(
                                    out=wdst[ot, :, q * 512:(q + 1) * 512],
                                    in_=EV[:, g % 2, :]), g % 2)
                            SCA(blk, s_g)
                            evac_done.append(aD.n)
                            dma_done.append(0)

                            def v_g(v, g=g, ma=aD.n):
                                v.wait_ge(aD.h, ma)
                                v.tensor_reduce(
                                    MXC[:, g:g + 1], EV[:, g % 2, :],
                                    op=ALU.max, axis=AX.X,
                                    apply_absolute_value=True)
                            VEC(blk, v_g)
                            red_done.append(vD.n)
                            g += 1

                with nc.Block() as blk:
                    final_max(blk, 96, mj)
                    qdq_scale(blk, mj, sj, rj)
                    bcast_scalar(blk, rj, ci)
                    if sj == jSG:
                        bcast_scalar(blk, sj, iSG)

        with contextlib.ExitStack() as es2:
            GS = es2.enter_context(nc.sbuf_tensor(un("GS_f"), [128, 2, T], F32))
            GI = es2.enter_context(nc.sbuf_tensor(un("GI_f"), [128, 3, T], I16))
            SIG = es2.enter_context(nc.sbuf_tensor(un("SIG_f"), [128, 2, T], F32))
            U16 = es2.enter_context(nc.sbuf_tensor(un("U16_f"), [128, 2, T], F32))
            with nc.Block() as blk:
                ts_done = []
                sig_done = []
                fin_done = []
                red_done = []
                ev0, ev1 = dsEV.n(0), dsEV.n(1)
                for ot in range(OT):
                    pool = False

                    def s_ld(sy, ot=ot, ev0=ev0, ev1=ev1):
                        if ot == 0:
                            sy.wait_ge(dsEV.h(0), ev0)
                            sy.wait_ge(dsEV.h(1), ev1)
                        if ot >= 2:
                            sy.wait_ge(*ts_done[ot - 2])
                        dsGS.issue(sy.dma_start(out=GS[:, ot % 2, :],
                                                in_=G_d[ot]), ot % 2)
                    blk.sync(s_ld)

                    def v_a(v, ot=ot, mld=dsGS.n(ot % 2),
                            mgi=dsGI.n(ot % 3)):
                        v.wait_ge(dsGS.h(ot % 2), mld)
                        if ot >= 2:
                            v.wait_ge(vD.h, red_done[ot - 2])
                        if ot >= 3:
                            v.wait_ge(aD.h, sig_done[ot - 3])
                            v.wait_ge(*fin_done[ot - 3])
                            v.wait_ge(dsGI.h(ot % 3), mgi)
                        v.tensor_scalar(SIG[:, ot % 2, :], GS[:, ot % 2, :],
                                        COLS[:, iRG:iRG + 1], MAGIC,
                                        op0=ALU.mult, op1=ALU.add)
                        v.tensor_scalar(GI[:, ot % 3, :],
                                               SIG[:, ot % 2, :], MAGIC, None,
                                               op0=ALU.subtract)
                    EW(blk, v_a, pool)
                    ts_done.append(ewm(pool))

                    def s_sig(sc, ot=ot, mts=ts_done[ot]):
                        sc.wait_ge(*mts)
                        sc.activation(SIG[:, ot % 2, :],
                                             GI[:, ot % 3, :], AF.Sigmoid,
                                             scale=COLS[:, iSG:iSG + 1])
                    SCA(blk, s_sig)
                    sig_done.append(aD.n)

                    def g_st(g, ot=ot, mts=ts_done[ot]):
                        g.wait_ge(*mts)
                        dsGI.issue(g.dma_start(out=g16_d[ot],
                                               in_=GI[:, ot % 3, :]), ot % 3)
                    blk.scalar(g_st)

                    def v_b(v, ot=ot, ma=aD.n):
                        v.wait_ge(aD.h, ma)
                        v.tensor_scalar(U16[:, ot % 2, :], SIG[:, ot % 2, :],
                                        65536.0, MAGIC, op0=ALU.mult,
                                        op1=ALU.add)
                        v.tensor_scalar(U16[:, ot % 2, :], U16[:, ot % 2, :],
                                        MAGIC, 65535.0, op0=ALU.subtract,
                                        op1=ALU.min)
                        v.tensor_tensor(SIG[:, ot % 2, :], GI[:, ot % 3, :],
                                        U16[:, ot % 2, :], op=ALU.mult)
                    EW(blk, v_b, pool)
                    fin_done.append(ewm(pool))

                    def v_red(v, ot=ot, mfb=fin_done[ot]):
                        v.wait_ge(*mfb)
                        v.tensor_reduce(MXC[:, ot:ot + 1],
                                               SIG[:, ot % 2, :],
                                               op=ALU.max, axis=AX.X,
                                               apply_absolute_value=True)
                    VEC(blk, v_red)
                    red_done.append(vD.n)

            with nc.Block() as blk:
                final_max(blk, OT, jMP)

                def v_s2(v):
                    v.wait_ge(vD.h, vD.n)
                    v.tensor_tensor(SSC[0:1, jT0:jT0 + 1],
                                    SSC[0:1, jMP:jMP + 1],
                                    SSC[0:1, jSG:jSG + 1], op=ALU.mult)
                    v.tensor_scalar(SSC[0:1, jSG2:jSG2 + 1],
                                    SSC[0:1, jT0:jT0 + 1],
                                    float(2.0 ** -16 / QMAX), 1e-12,
                                    op0=ALU.mult, op1=ALU.max)
                    v.reciprocal(SSC[0:1, jRG2:jRG2 + 1],
                                 SSC[0:1, jSG2:jSG2 + 1])
                    v.tensor_tensor(SSC[0:1, jT0:jT0 + 1],
                                    SSC[0:1, jSG:jSG + 1],
                                    SSC[0:1, jRG2:jRG2 + 1], op=ALU.mult)
                    v.tensor_scalar(SSC[0:1, jK2:jK2 + 1],
                                           SSC[0:1, jT0:jT0 + 1],
                                           float(2.0 ** -16), None,
                                           op0=ALU.mult)
                VEC(blk, v_s2)
                bcast_scalar(blk, jK2, iK2)

        with contextlib.ExitStack() as es3:
            US = es3.enter_context(nc.sbuf_tensor(un("US_f"), [128, 2, T], F32))
            GI = es3.enter_context(nc.sbuf_tensor(un("GI3_f"), [128, 2, T], I16))
            UI = es3.enter_context(nc.sbuf_tensor(un("UI3_f"), [128, 2, T], I16))
            G2 = es3.enter_context(nc.sbuf_tensor(un("G23_f"), [128, 2, T], I16))
            SG3 = es3.enter_context(nc.sbuf_tensor(un("SG3_f"), [128, 2, T], F32))
            with nc.Block() as blk:
                a_done = []; b_done = []; sg_done = []; red_done = []
                gi_gates = [dsGI.n(i) for i in range(3)]
                ev_gates = [dsEV.n(0), dsEV.n(1)]
                for ot in range(OT):
                    pool = False

                    def s_ld(sy, ot=ot):
                        if ot == 0:
                            for i in range(3):
                                sy.wait_ge(dsGI.h(i), gi_gates[i])
                            sy.wait_ge(dsEV.h(0), ev_gates[0])
                            sy.wait_ge(dsEV.h(1), ev_gates[1])
                        if ot >= 2:
                            sy.wait_ge(*b_done[ot - 2])
                            sy.wait_ge(aD.h, sg_done[ot - 2])
                        dsUS.issue(sy.dma_start(out=US[:, ot % 2, :],
                                                in_=U_d[ot]), ot % 2)
                        dsUS.issue(sy.dma_start(out=GI[:, ot % 2, :],
                                                in_=g16_d[ot]), ot % 2)
                    blk.sync(s_ld)

                    def v_a(v, ot=ot, mld=dsUS.n(ot % 2),
                            mui=dsUI.n(ot % 2)):
                        v.wait_ge(dsUS.h(ot % 2), mld)
                        if ot >= 2:
                            v.wait_ge(*b_done[ot - 2])
                            v.wait_ge(vD.h, red_done[ot - 2])
                            v.wait_ge(dsUI.h(ot % 2), mui)
                        v.tensor_scalar(SG3[:, ot % 2, :], US[:, ot % 2, :],
                                        COLS[:, iRUP:iRUP + 1], MAGIC,
                                        op0=ALU.mult, op1=ALU.add)
                        v.tensor_scalar(UI[:, ot % 2, :],
                                               SG3[:, ot % 2, :],
                                               MAGIC, None, op0=ALU.subtract)
                    EW(blk, v_a, pool)
                    a_done.append(ewm(pool))

                    def g_su(g, ot=ot, mva=a_done[ot]):
                        g.wait_ge(*mva)
                        dsUI.issue(g.dma_start(out=u16_d[ot],
                                               in_=UI[:, ot % 2, :]), ot % 2)
                    blk.scalar(g_su)

                    def s_sig(sc, ot=ot, mva=a_done[ot]):
                        sc.wait_ge(*mva)
                        sc.activation(SG3[:, ot % 2, :],
                                             GI[:, ot % 2, :], AF.Sigmoid,
                                             scale=COLS[:, iSG:iSG + 1])
                    SCA(blk, s_sig)
                    sg_done.append(aD.n)

                    g2m = {}

                    def v_b(v, ot=ot, ma=aD.n, g2m=g2m, pool=pool,
                            mg2=dsG2.n(ot % 2)):
                        v.wait_ge(aD.h, ma)
                        if ot >= 2:
                            v.wait_ge(dsG2.h(ot % 2), mg2)
                        v.tensor_scalar(SG3[:, ot % 2, :], SG3[:, ot % 2, :],
                                        65536.0, MAGIC, op0=ALU.mult,
                                        op1=ALU.add)
                        v.tensor_scalar(SG3[:, ot % 2, :], SG3[:, ot % 2, :],
                                        MAGIC, 65535.0, op0=ALU.subtract,
                                        op1=ALU.min)
                        v.tensor_tensor(SG3[:, ot % 2, :], GI[:, ot % 2, :],
                                        SG3[:, ot % 2, :], op=ALU.mult)
                        v.tensor_scalar(SG3[:, ot % 2, :], SG3[:, ot % 2, :],
                                        COLS[:, iK2:iK2 + 1], MAGIC,
                                        op0=ALU.mult, op1=ALU.add)
                        v.tensor_scalar(G2[:, ot % 2, :],
                                               SG3[:, ot % 2, :],
                                               MAGIC, None, op0=ALU.subtract)
                        g2m[0] = (gD.h, gD.n) if pool else (vD.h, vD.n)
                        v.tensor_tensor(SG3[:, ot % 2, :], G2[:, ot % 2, :],
                                        UI[:, ot % 2, :], op=ALU.mult)
                    EW(blk, v_b, pool)
                    b_done.append(ewm(pool))

                    def v_red(v, ot=ot, mfb=b_done[ot]):
                        v.wait_ge(*mfb)
                        v.tensor_reduce(MXC[:, ot:ot + 1],
                                               SG3[:, ot % 2, :],
                                               op=ALU.max, axis=AX.X,
                                               apply_absolute_value=True)
                    VEC(blk, v_red)
                    red_done.append(vD.n)

                    def g_sg(g, ot=ot, mg=g2m[0]):
                        g.wait_ge(*mg)
                        dsG2.issue(g.dma_start(out=g216_d[ot],
                                               in_=G2[:, ot % 2, :]), ot % 2)
                    blk.scalar(g_sg)

            with nc.Block() as blk:
                final_max(blk, OT, jMP2)

                def v_so(v):
                    v.wait_ge(vD.h, vD.n)
                    v.tensor_tensor(SSC[0:1, jT0:jT0 + 1],
                                    SSC[0:1, jMP2:jMP2 + 1],
                                    SSC[0:1, jSG2:jSG2 + 1], op=ALU.mult)
                    v.tensor_tensor(SSC[0:1, jT0:jT0 + 1],
                                    SSC[0:1, jT0:jT0 + 1],
                                    SSC[0:1, jSUP:jSUP + 1], op=ALU.mult)
                    v.tensor_scalar(SSC[0:1, jSO:jSO + 1],
                                    SSC[0:1, jT0:jT0 + 1],
                                    1.0 / QMAX, 1e-12, op0=ALU.mult,
                                    op1=ALU.max)
                    v.reciprocal(SSC[0:1, jRO:jRO + 1], SSC[0:1, jSO:jSO + 1])
                    v.tensor_tensor(SSC[0:1, jT1:jT1 + 1],
                                    SSC[0:1, jSG2:jSG2 + 1],
                                    SSC[0:1, jSUP:jSUP + 1], op=ALU.mult)
                    v.tensor_tensor(SSC[0:1, jK3:jK3 + 1],
                                    SSC[0:1, jT1:jT1 + 1],
                                    SSC[0:1, jRO:jRO + 1], op=ALU.mult)
                    v.tensor_scalar(SSC[0:1, jT0:jT0 + 1],
                                           SSC[0:1, jSO:jSO + 1], 0.5, None,
                                           op0=ALU.mult)
                VEC(blk, v_so)
                bcast_scalar(blk, jK3, iK3)
                bcast_scalar(blk, jT0, iOS)

        with contextlib.ExitStack() as es4:
            OB = es4.enter_context(nc.sbuf_tensor(un("OB_f"), [128, OT, 1024],
                                                  BF16))
            G2 = es4.enter_context(nc.sbuf_tensor(un("G24_f"), [128, 2, 1024],
                                                  I16))
            UI = es4.enter_context(nc.sbuf_tensor(un("UI4_f"), [128, 2, 1024],
                                                  I16))
            PR = es4.enter_context(nc.sbuf_tensor(un("PR4_f"), [128, 2, 1024],
                                                  F32))
            with nc.Block() as blk:
                mp_wt = pD.n

                mg_wd = qG.n

                def s_wd(sy, mp_wt=mp_wt, mg_wd=mg_wd):
                    sy.wait_ge(pD.h, mp_wt)
                    sy.wait_ge(qS.h, qS.n)
                    sy.wait_ge(qG.h, mg_wd)
                    for o in range(OT):
                        qS.inc(sy.dma_start_transpose(
                            WT[:, o, :], wd_q[:, o * 128:(o + 1) * 128]), 16)
                blk.sync(s_wd)
                m_wd = qS.n

                g4_gates = [dsUI.n(0), dsUI.n(1), dsG2.n(0), dsG2.n(1)]
                ob_done = []
                for th in range(2):
                    th_pgate = pD.n
                    for ot in range(OT):
                        gk2 = th * OT + ot
                        pool = False

                        def s_l4(sy, th=th, ot=ot, gk2=gk2):
                            if th == 0 and ot == 0:
                                sy.wait_ge(dsUI.h(0), g4_gates[0])
                                sy.wait_ge(dsUI.h(1), g4_gates[1])
                                sy.wait_ge(dsG2.h(0), g4_gates[2])
                                sy.wait_ge(dsG2.h(1), g4_gates[3])
                            if gk2 >= 2:
                                sy.wait_ge(*ob_done[gk2 - 2])
                            sl = slice(th * 1024, (th + 1) * 1024)
                            dsL4.issue(sy.dma_start(out=G2[:, ot % 2, :],
                                                    in_=g216_d[ot, :, sl]),
                                       ot % 2)
                            dsL4.issue(sy.dma_start(out=UI[:, ot % 2, :],
                                                    in_=u16_d[ot, :, sl]),
                                       ot % 2)
                        blk.sync(s_l4)

                        def v_ob(v, ot=ot, mld=dsL4.n(ot % 2),
                                 th_pgate=th_pgate):
                            v.wait_ge(dsL4.h(ot % 2), mld)
                            v.wait_ge(pD.h, th_pgate)
                            v.tensor_tensor(PR[:, ot % 2, :], G2[:, ot % 2, :],
                                            UI[:, ot % 2, :], op=ALU.mult)
                            v.tensor_scalar(PR[:, ot % 2, :], PR[:, ot % 2, :],
                                            COLS[:, iK3:iK3 + 1], MAGIC,
                                            op0=ALU.mult, op1=ALU.add)
                            v.tensor_scalar(
                                OB[:, ot, :], PR[:, ot % 2, :],
                                MAGIC, COLS[:, iOS:iOS + 1],
                                op0=ALU.subtract, op1=ALU.mult)
                        EW(blk, v_ob, pool)
                        ob_done.append(ewm(pool))
                    m_ob = vD.n
                    m_obg = gD.n

                    ev_done = {}
                    for dt in range(CT):
                        for qq in range(2):
                            gk = dt * 2 + qq
                            def t_dn(te, th=th, dt=dt, qq=qq, gk=gk,
                                     m_ob=m_ob, m_obg=m_obg):
                                if gk == 0:
                                    te.wait_ge(qS.h, m_wd)
                                    te.wait_ge(vD.h, m_ob)
                                    te.wait_ge(gD.h, m_obg)
                                if gk >= 2:
                                    te.wait_ge(vD.h, ev_done[gk - 2])
                                tsl = slice(qq * 512, (qq + 1) * 512)
                                for o in range(OT):
                                    mm = te.matmul(
                                        PB[gk % 2][:, :],
                                        WT[:, o, dt * 128:(dt + 1) * 128],
                                        OB[:, o, tsl],
                                        start=(o == 0), stop=(o == OT - 1))
                                pD.inc(mm)
                            blk.tensor(t_dn)
                            mp = pD.n

                            def v_rs(v, th=th, dt=dt, qq=qq, gk=gk, mp=mp):
                                v.wait_ge(pD.h, mp)
                                sl = slice(th * 1024 + qq * 512,
                                           th * 1024 + (qq + 1) * 512)
                                v.tensor_tensor(
                                    XH[:, dt, sl], XH[:, dt, sl],
                                    PB[gk % 2][:, :], op=ALU.add)
                            VEC(blk, v_rs)
                            ev_done[gk] = vD.n
        esF.close()

    with es:
        esXA = contextlib.ExitStack()
        XA = esXA.enter_context(nc.sbuf_tensor("XA", [128, CT, T], F32))
        esW = contextlib.ExitStack()
        WRAW = esW.enter_context(nc.sbuf_tensor("WRAW", [128, 2, I3], F32))
        WQB = esW.enter_context(nc.sbuf_tensor("WQB", [128, 2, I3], BF16))
        WST = esW.enter_context(nc.sbuf_tensor("WST", [128, 2, I3], F32))
        SCM = esW.enter_context(nc.sbuf_tensor("SCM", [128, 2, 96], F32))
        SCR = esW.enter_context(nc.sbuf_tensor("SCR", [128, 2, 96], F32))

        with nc.Block() as blk:
            def s_0(sy):
                qS.inc(sy.dma_start(out=IDN[:], in_=ident_i), 16)
                qS.inc(sy.dma_start(out=MSKC[:], in_=maskS_i), 16)
                qS.inc(sy.dma_start(out=XA[:],
                                    in_=x_own.rearrange("c p t -> p c t")), 16)
            blk.sync(s_0)

            def v_0(v):
                v.memset(ones_col[:], 1.0)
                v.memset(ones_row[:], 1.0)
                v.memset(SSC[:], 0.0)
                v.memset(SSC[0:1, jEPS:jEPS + 1], EPS)
                v.memset(MXC[:], 0.0)
            VEC(blk, v_0)

        for wsrc, wdst, osz, rl in ((w_gate, wg_q, OT, D), (w_up, wu_q, OT, D),
                                    (w_down, wd_q, CT, I3)):
            nb = rl // 32
            with nc.Block() as blk:
                dq_done = []
                mv_blk = vD.n
                mg_blk = gD.n
                wsv = wsrc.rearrange("(k p) i -> k p i", p=128)
                wdv = wdst.rearrange("(k p) i -> k p i", p=128)
                for k in range(osz):
                    kp = k % 2
                    pool = False

                    def s_lw(sy, k=k, rl=rl, kp=kp):
                        if k >= 2:
                            sy.wait_ge(*dq_done[k - 2])
                        else:
                            sy.wait_ge(vD.h, mv_blk)
                            sy.wait_ge(gD.h, mg_blk)
                        dsWR.issue(sy.dma_start(out=WRAW[:, kp, 0:rl],
                                                in_=wsv[k]), kp)
                    blk.sync(s_lw)

                    def v_sc(v, k=k, rl=rl, nb=nb, kp=kp,
                             mwr=dsWR.n(kp)):
                        v.wait_ge(dsWR.h(kp), mwr)
                        wv = WRAW[:, kp, 0:rl].rearrange(
                            "p (b j) -> p b j", j=32)
                        v.tensor_reduce(SCM[:, kp, 0:nb], wv, op=ALU.max,
                                        axis=AX.X, apply_absolute_value=True)
                        v.tensor_scalar(SCM[:, kp, 0:nb], SCM[:, kp, 0:nb],
                                        1.0 / 7.0, 1e-12, op0=ALU.mult,
                                        op1=ALU.max)
                        v.reciprocal(SCR[:, kp, 0:nb], SCM[:, kp, 0:nb])
                    VEC(blk, v_sc)
                    mv_sc = vD.n

                    def v_dq(v, k=k, rl=rl, nb=nb, kp=kp, mv_sc=mv_sc,
                             mwq=dsWQ.n(kp)):
                        v.wait_ge(vD.h, mv_sc)
                        v.wait_ge(dsWQ.h(kp), mwq)
                        wv = WRAW[:, kp, 0:rl].rearrange(
                            "p (b j) -> p b j", j=32)
                        rb = SCR[:, kp, 0:nb].unsqueeze(2).broadcast_to(
                            [128, nb, 32])
                        sb = SCM[:, kp, 0:nb].unsqueeze(2).broadcast_to(
                            [128, nb, 32])
                        sv = WST[:, kp, 0:rl].rearrange(
                            "p (b j) -> p b j", j=32)
                        v.tensor_tensor(sv, wv, rb, op=ALU.mult)
                        v.tensor_scalar(WST[:, kp, 0:rl], WST[:, kp, 0:rl],
                                        MAGIC, MAGIC,
                                        op0=ALU.add, op1=ALU.subtract)
                        qv = WQB[:, kp, 0:rl].rearrange(
                            "p (b j) -> p b j", j=32)
                        v.tensor_tensor(qv, sv, sb, op=ALU.mult)
                    EW(blk, v_dq, pool)
                    dq_done.append(ewm(pool))

                    def g_st(g, k=k, rl=rl, kp=kp, mdq=dq_done[k]):
                        g.wait_ge(*mdq)
                        dsWQ.issue(g.dma_start(out=wdv[k],
                                               in_=WQB[:, kp, 0:rl]),
                                   kp)
                    blk.scalar(g_st)

        esW.close()

        emit_ffn(XA)

        if STAGE == 1:
            if os.environ.get("KTAP"):
                dbg_ssc = nc.dram_tensor("dbg_ssc", [1, 64], F32,
                                         kind="ExternalOutput").ap()
                dbg_cols = nc.dram_tensor("dbg_cols", [128, 16], F32,
                                          kind="ExternalOutput").ap()
                dbg_g = nc.dram_tensor("dbg_g", [128, T], F32,
                                       kind="ExternalOutput").ap()
                dbg_u = nc.dram_tensor("dbg_u", [128, T], F32,
                                       kind="ExternalOutput").ap()
                dbg_g16 = nc.dram_tensor("dbg_g16", [128, T], I16,
                                         kind="ExternalOutput").ap()
                dbg_g216 = nc.dram_tensor("dbg_g216", [128, T], I16,
                                          kind="ExternalOutput").ap()
                dbg_u16 = nc.dram_tensor("dbg_u16", [128, T], I16,
                                         kind="ExternalOutput").ap()
                dbg_wgq = nc.dram_tensor("dbg_wgq", [256, D], BF16,
                                         kind="ExternalOutput").ap()
                with nc.Block() as blk:
                    def s_tap(sy, mv=vD.n):
                        sy.wait_ge(vD.h, mv)
                        qS.inc(sy.dma_start(out=dbg_wgq, in_=wg_q[0:256, :]), 16)
                        qS.inc(sy.dma_start(out=dbg_ssc, in_=SSC[:]), 16)
                        qS.inc(sy.dma_start(out=dbg_cols, in_=COLS[:]), 16)
                        qS.inc(sy.dma_start(out=dbg_g, in_=G_d[0]), 16)
                        qS.inc(sy.dma_start(out=dbg_u, in_=U_d[0]), 16)
                        qS.inc(sy.dma_start(out=dbg_g16, in_=g16_d[0]), 16)
                        qS.inc(sy.dma_start(out=dbg_g216, in_=g216_d[0]), 16)
                        qS.inc(sy.dma_start(out=dbg_u16, in_=u16_d[0]), 16)
                    blk.sync(s_tap)
            with nc.Block() as blk:
                def s_out(sy, mv=vD.n):
                    sy.wait_ge(vD.h, mv)
                    qS.inc(sy.dma_start(
                        out=y.rearrange("c p t -> p c t"), in_=XA[:]), 16)
                    sy.wait_ge(qS.h, qS.n)
                blk.sync(s_out)
            esXA.close()
            return nc

        xaow_d = dsc("xaow_d", [CT, 128, T], BF16)
        with contextlib.ExitStack() as esL2:
            XAOW = esL2.enter_context(nc.sbuf_tensor("XAOW", [128, CT, T],
                                                     BF16))
            SQ2 = esL2.enter_context(nc.sbuf_tensor("SQ2", [128, 2, T], F32))
            emit_ln(XA, SQ2, False, XAOW)
            with nc.Block() as blk:
                def s_sp(sy, mv=vD.n):
                    sy.wait_ge(vD.h, mv)
                    qS.inc(sy.dma_start(out=x1_d.rearrange("c p t -> p c t"),
                                        in_=XA[:]), 16)
                    qS.inc(sy.dma_start(out=xaow_d.rearrange("c p t -> p c t"),
                                        in_=XAOW[:]), 16)
                blk.sync(s_sp)
        esXA.close()

        with contextlib.ExitStack() as esB:
            XB = esB.enter_context(nc.sbuf_tensor("XB", [128, CT, T], F32))
            with nc.Block() as blk:
                def s_lb(sy, mq_sp=qS.n):
                    sy.wait_ge(qS.h, mq_sp)
                    qS.inc(sy.dma_start(
                        out=XB[:], in_=x_oth.rearrange("c p t -> p c t")), 16)
                blk.sync(s_lb)
            emit_ffn(XB)
            XAB = esB.enter_context(nc.sbuf_tensor("XAB", [128, CT, T], BF16))
            SQ3 = esB.enter_context(nc.sbuf_tensor("SQ3", [128, 2, T], F32))
            emit_ln(XB, SQ3, False, XAB)

            WK = esB.enter_context(nc.sbuf_tensor("WK", [128, CT, D], BF16))
            WV = esB.enter_context(nc.sbuf_tensor("WV", [128, CT, D], BF16))
            KTMP = esB.enter_context(nc.sbuf_tensor("KTMP", [128, D], BF16))
            ROT = esB.enter_context(nc.sbuf_tensor("ROT", [128, D], BF16))
            VTMP = esB.enter_context(nc.sbuf_tensor("VTMP", [128, D], BF16))
            T0 = esB.enter_context(nc.sbuf_tensor("T0", [128, D], F32))
            T1 = esB.enter_context(nc.sbuf_tensor("T1", [128, D], F32))
            KS = esB.enter_context(nc.sbuf_tensor("KS", [128, D], BF16))
            COSB = esB.enter_context(nc.sbuf_tensor("COSB", [128, NCH, DH],
                                                    F32))
            SINB = esB.enter_context(nc.sbuf_tensor("SINB", [128, NCH, DH],
                                                    F32))
            GCT = esB.enter_context(nc.sbuf_tensor("GCT", [128, 512], F32))
            KSC = esB.enter_context(nc.sbuf_tensor("KSC", [128, D], F32))
            with nc.Block() as blk:
                def s_lw2(sy):
                    qS.inc(sy.dma_start(
                        out=COSB[:],
                        in_=cos_oth.rearrange("(n p) d -> p n d", p=128)), 16)
                    qS.inc(sy.dma_start(
                        out=SINB[:],
                        in_=sinS_oth.rearrange("(n p) d -> p n d", p=128)), 16)
                    qS.inc(sy.dma_start(out=GCT[:], in_=gctab_i), 16)
                    qS.inc(sy.dma_start(out=KSC[:], in_=kscT_i), 16)
                blk.sync(s_lw2)

                def g_lw2(g):
                    qG.inc(g.dma_start(
                        out=WK[:],
                        in_=wkT.rearrange("(c p) d -> p c d", p=128)), 16)
                    qG.inc(g.dma_start(
                        out=WV[:],
                        in_=wvT.rearrange("(c p) d -> p c d", p=128)), 16)
                blk.gpsimd(g_lw2)
                m_w2 = qG.n
                m_c2 = qS.n
                mv_xab = vD.n

                def v_z(v):
                    v.memset(SST[:], 0.0)
                blk.vector(v_z)

                ev_k = []
                for tt in range(NCH):
                    def t_kv(te, tt=tt):
                        if tt == 0:
                            te.wait_ge(qG.h, m_w2)
                            te.wait_ge(qS.h, m_c2)
                            te.wait_ge(vD.h, mv_xab)
                        else:
                            te.wait_ge(aD.h, ev_k[tt - 1][0])
                            te.wait_ge(vD.h, ev_k[tt - 1][1])
                        for dq in range(2):
                            for c in range(CT):
                                mmk = te.matmul(
                                    PB[dq][:, :],
                                    XAB[:, c, tt * 128:(tt + 1) * 128],
                                    WK[:, c, dq * 512:(dq + 1) * 512],
                                    start=(c == 0), stop=(c == CT - 1))
                        for dq in range(2):
                            for c in range(CT):
                                mmv = te.matmul(
                                    PB[2 + dq][:, :],
                                    XAB[:, c, tt * 128:(tt + 1) * 128],
                                    WV[:, c, dq * 512:(dq + 1) * 512],
                                    start=(c == 0), stop=(c == CT - 1))
                        pD.inc(mmv)
                    blk.tensor(t_kv)
                    mpk = pD.n

                    def s_ev(sc, tt=tt, mpk=mpk):
                        sc.wait_ge(pD.h, mpk)
                        sc.copy(KTMP[:, 0:512], PB[0][:, :])
                        sc.copy(KTMP[:, 512:1024], PB[1][:, :])
                        sc.copy(VTMP[:, 0:512], PB[2][:, :])
                        sc.copy(VTMP[:, 512:1024], PB[3][:, :])
                        kv = KTMP[:].rearrange("p (h g j) -> p h g j", g=2,
                                               j=32)
                        rv = ROT[:].rearrange("p (h g j) -> p h g j", g=2,
                                              j=32)
                        sc.copy(rv[:, :, 0, :], kv[:, :, 1, :])
                        sc.copy(rv[:, :, 1, :], kv[:, :, 0, :])
                    SCA(blk, s_ev)
                    mak = aD.n

                    def v_rp(v, tt=tt, mak=mak):
                        v.wait_ge(aD.h, mak)
                        cb = COSB[:, tt, :].unsqueeze(1).broadcast_to(
                            [128, NH, DH])
                        sb = SINB[:, tt, :].unsqueeze(1).broadcast_to(
                            [128, NH, DH])
                        kv = KTMP[:].rearrange("p (h d) -> p h d", d=DH)
                        rv = ROT[:].rearrange("p (h d) -> p h d", d=DH)
                        v.tensor_tensor(T0[:].rearrange("p (h d) -> p h d",
                                                        d=DH), kv, cb,
                                        op=ALU.mult)
                        v.tensor_tensor(T1[:].rearrange("p (h d) -> p h d",
                                                        d=DH), rv, sb,
                                        op=ALU.mult)
                        v.tensor_tensor(KTMP[:], T0[:], T1[:], op=ALU.add)
                        v.tensor_tensor(KS[:], KTMP[:], KSC[:], op=ALU.mult)
                    VEC(blk, v_rp)
                    mvk = vD.n

                    def t_st(te, tt=tt, mvk=mvk):
                        te.wait_ge(vD.h, mvk)
                        for h in range(NH):
                            base = 64 * (h % 2)
                            col = (h // 2) * DH
                            mm = te.matmul(
                                PB[4][base:base + 64, col:col + DH],
                                KS[:, h * DH:(h + 1) * DH],
                                VTMP[:, h * DH:(h + 1) * DH],
                                start=True, stop=True)
                        pD.inc(mm)
                    blk.tensor(t_st)
                    mps = pD.n

                    def v_su(v, tt=tt, mps=mps):
                        v.wait_ge(pD.h, mps)
                        v.tensor_tensor(SST[:], SST[:], GCT[:], op=ALU.mult)
                        v.tensor_tensor(SST[:], SST[:], PB[4][:, :],
                                        op=ALU.add)
                    VEC(blk, v_su)
                    ev_k.append((aD.n, vD.n))

            with nc.Block() as blk:
                def v_mk(v):
                    v.wait_ge(vD.h, vD.n)
                    v.tensor_scalar(SST[:], SST[:], MSKC[:, 0:1], None,
                                    op0=ALU.mult)
                VEC(blk, v_mk)

                def s_sb(sc, mv=vD.n):
                    sc.wait_ge(vD.h, mv)
                    sc.copy(SSB[:], SST[:])
                SCA(blk, s_sb)

            if STAGE < 1.7:
                with nc.Block() as blk:
                    def s_o15(sy, mv=vD.n, ma=aD.n):
                        sy.wait_ge(vD.h, mv)
                        sy.wait_ge(aD.h, ma)
                        qS.inc(sy.dma_start(
                            out=y.rearrange("c p t -> p c t"), in_=XB[:]), 16)
                        sy.wait_ge(qS.h, qS.n)
                    blk.sync(s_o15)
                return nc

        esAT = contextlib.ExitStack()
        KT = esAT.enter_context(nc.sbuf_tensor("KT", [128, NCH, D], BF16))
        VT = esAT.enter_context(nc.sbuf_tensor("VT", [128, NCH, D], BF16))
        GSIL = esAT.enter_context(nc.sbuf_tensor("GSIL", [128, CT, T], BF16))
        with contextlib.ExitStack() as esA2:
            XAP = esA2.enter_context(nc.sbuf_tensor("XAP", [128, CT, T],
                                                    BF16))
            WB1 = esA2.enter_context(nc.sbuf_tensor("WB1", [128, CT, D],
                                                    BF16))
            KTMP = esA2.enter_context(nc.sbuf_tensor("KTMP2", [128, D], BF16))
            ROT = esA2.enter_context(nc.sbuf_tensor("ROT2", [128, D], BF16))
            T0 = esA2.enter_context(nc.sbuf_tensor("T02", [128, D], F32))
            T1 = esA2.enter_context(nc.sbuf_tensor("T12", [128, D], F32))
            COSB = esA2.enter_context(nc.sbuf_tensor("COSB2", [128, NCH, DH],
                                                     F32))
            SINB = esA2.enter_context(nc.sbuf_tensor("SINB2", [128, NCH, DH],
                                                     F32))

            with nc.Block() as blk:
                def s_lo(sy):
                    qS.inc(sy.dma_start(
                        out=XAP[:], in_=xaow_d.rearrange("c p t -> p c t")),
                        16)
                    qS.inc(sy.dma_start(
                        out=COSB[:],
                        in_=cos_own.rearrange("(n p) d -> p n d", p=128)), 16)
                    qS.inc(sy.dma_start(
                        out=SINB[:],
                        in_=sinS_own.rearrange("(n p) d -> p n d", p=128)), 16)
                blk.sync(s_lo)

            def proj_tm(wsrc, dst, rope):
                with nc.Block() as blk:
                    mp_prev = pD.n

                    def g_w(g, wsrc=wsrc, mp_prev=mp_prev):
                        g.wait_ge(pD.h, mp_prev)
                        qG.inc(g.dma_start(
                            out=WB1[:],
                            in_=wsrc.rearrange("(c p) d -> p c d", p=128)),
                            16)
                    blk.gpsimd(g_w)
                    m_w = qG.n
                    m_ld = qS.n
                    ev_m = []
                    for tt in range(NCH):
                        def t_p(te, tt=tt):
                            if tt == 0:
                                te.wait_ge(qG.h, m_w)
                                te.wait_ge(qS.h, m_ld)
                            if tt >= 2:
                                te.wait_ge(vD.h, ev_m[tt - 2][1])
                                te.wait_ge(aD.h, ev_m[tt - 2][0])
                            for dq in range(2):
                                for c in range(CT):
                                    mm = te.matmul(
                                        PB[2 * (tt % 2) + dq][:, :],
                                        XAP[:, c, tt * 128:(tt + 1) * 128],
                                        WB1[:, c, dq * 512:(dq + 1) * 512],
                                        start=(c == 0), stop=(c == CT - 1))
                            pD.inc(mm)
                        blk.tensor(t_p)
                        mpk = pD.n

                        def s_e(sc, tt=tt, mpk=mpk, rope=rope, dst=dst,
                                m_ld=m_ld):
                            sc.wait_ge(pD.h, mpk)
                            if tt == 0:
                                sc.wait_ge(qS.h, m_ld)
                            if rope and tt >= 1:
                                sc.wait_ge(vD.h, ev_m[tt - 1][1])
                            tgt = KTMP if rope else dst
                            off = 0 if rope else tt
                            if rope:
                                sc.copy(KTMP[:, 0:512],
                                        PB[2 * (tt % 2)][:, :])
                                sc.copy(KTMP[:, 512:1024],
                                        PB[2 * (tt % 2) + 1][:, :])
                                kv = KTMP[:].rearrange(
                                    "p (h g j) -> p h g j", g=2, j=32)
                                rv = ROT[:].rearrange(
                                    "p (h g j) -> p h g j", g=2, j=32)
                                sc.copy(rv[:, :, 0, :], kv[:, :, 1, :])
                                sc.copy(rv[:, :, 1, :], kv[:, :, 0, :])
                            else:
                                sc.copy(dst[:, tt, 0:512],
                                        PB[2 * (tt % 2)][:, :])
                                sc.copy(dst[:, tt, 512:1024],
                                        PB[2 * (tt % 2) + 1][:, :])
                        SCA(blk, s_e)
                        mak = aD.n

                        def v_r(v, tt=tt, mak=mak, rope=rope, dst=dst):
                            v.wait_ge(aD.h, mak)
                            if rope:
                                cb = COSB[:, tt, :].unsqueeze(1).broadcast_to(
                                    [128, NH, DH])
                                sb = SINB[:, tt, :].unsqueeze(1).broadcast_to(
                                    [128, NH, DH])
                                kv = KTMP[:].rearrange("p (h d) -> p h d",
                                                       d=DH)
                                rv = ROT[:].rearrange("p (h d) -> p h d",
                                                      d=DH)
                                v.tensor_tensor(
                                    T0[:].rearrange("p (h d) -> p h d", d=DH),
                                    kv, cb, op=ALU.mult)
                                v.tensor_tensor(
                                    T1[:].rearrange("p (h d) -> p h d", d=DH),
                                    rv, sb, op=ALU.mult)
                                v.tensor_tensor(dst[:, tt, :], T0[:], T1[:],
                                                op=ALU.add)
                        VEC(blk, v_r)
                        ev_m.append((aD.n, vD.n))

            proj_tm(wqT, KT, True)
            with nc.Block() as blk:
                def s_bq(sy, mv=vD.n):
                    sy.wait_ge(vD.h, mv)
                    qS.inc(sy.dma_start(
                        out=trb_d.rearrange("(n p) d -> p n d", p=128),
                        in_=KT[:]), 16)
                blk.sync(s_bq)
            proj_tm(wkT, KT, True)
            proj_tm(wvT, VT, False)

            with nc.Block() as blk:
                mp_prev = pD.n

                def g_wg(g, mp_prev=mp_prev):
                    g.wait_ge(pD.h, mp_prev)
                    qG.inc(g.dma_start(
                        out=WB1[:],
                        in_=wgTT.rearrange("(c p) d -> p c d", p=128)), 16)
                blk.gpsimd(g_wg)
                m_w = qG.n
                ev_g = []
                gk = 0
                for dg in range(CT):
                    for q in range(4):
                        def t_gp(te, dg=dg, q=q, gk=gk):
                            if gk == 0:
                                te.wait_ge(qG.h, m_w)
                            if gk >= 2:
                                te.wait_ge(aD.h, ev_g[gk - 2])
                            for c in range(CT):
                                mm = te.matmul(
                                    PB[4 + gk % 2][:, :],
                                    WB1[:, c, dg * 128:(dg + 1) * 128],
                                    XAP[:, c, q * 512:(q + 1) * 512],
                                    start=(c == 0), stop=(c == CT - 1))
                            pD.inc(mm)
                        blk.tensor(t_gp)
                        mpg = pD.n

                        def s_gs(sc, dg=dg, q=q, gk=gk, mpg=mpg):
                            sc.wait_ge(pD.h, mpg)
                            sc.activation(
                                GSIL[:, dg, q * 512:(q + 1) * 512],
                                PB[4 + gk % 2][:, :], AF.Silu)
                        SCA(blk, s_gs)
                        ev_g.append(aD.n)
                        gk += 1

            with nc.Block() as blk:
                def s_bn(sy, mv=vD.n, ma=aD.n):
                    sy.wait_ge(vD.h, mv)
                    sy.wait_ge(aD.h, ma)
                    qS.inc(sy.dma_start(
                        out=trb2_d.rearrange("(n p) d -> p n d", p=128),
                        in_=KT[:]), 16)
                blk.sync(s_bn)

            if STAGE < 1.8:
                with nc.Block() as blk:
                    def g_o17(g, mv=vD.n, ma=aD.n):
                        g.wait_ge(vD.h, mv)
                        g.wait_ge(aD.h, ma)
                        qG.inc(g.dma_start(
                            out=y.rearrange("c p t -> p c t"),
                            in_=GSIL[:]), 16)
                        g.wait_ge(qG.h, qG.n)
                    blk.gpsimd(g_o17)
                esA2.close()
                esAT.close()
                return nc

        if STAGE == 1.85:
            with contextlib.ExitStack() as esT:
                QF85 = esT.enter_context(
                    nc.sbuf_tensor("QF85", [128, CT, T], BF16))
                KF85 = esT.enter_context(
                    nc.sbuf_tensor("KF85", [128, CT, T], BF16))
                YF85 = esT.enter_context(
                    nc.sbuf_tensor("YF85", [128, T], F32))
                with nc.Block() as blk:
                    def s_tr85(sy):
                        sy.wait_ge(qS.h, qS.n)
                        for i in range(CT):
                            qS.inc(sy.dma_start_transpose(
                                QF85[:, i, :],
                                trb_d[:, i * 128:(i + 1) * 128]), 16)
                        for i in range(CT):
                            qS.inc(sy.dma_start_transpose(
                                KF85[:, i, :],
                                trb2_d[:, i * 128:(i + 1) * 128]), 16)
                    blk.sync(s_tr85)
                    m_tr85 = qS.n
                    srcs = [(QF85[:, 0, :], T), (QF85[:, 7, :], T),
                            (KF85[:, 0, :], T), (KF85[:, 7, :], T),
                            (SST[:], 512), (SSB[:], 512),
                            (KT[:, 0, :], D), (VT[:, 0, :], D)]
                    for i85, (src, w85) in enumerate(srcs):
                        def v_c85(v, i85=i85, src=src, w85=w85,
                                  m_tr85=m_tr85, mq=qS.n, ma=aD.n):
                            v.wait_ge(qS.h, m_tr85 if i85 == 0 else mq)
                            if i85 == 0:
                                v.wait_ge(aD.h, ma)
                            if w85 < T:
                                v.memset(YF85[:], 0.0)
                            v.tensor_copy(YF85[:, 0:w85], src)
                        VEC(blk, v_c85)
                        mv85 = vD.n

                        def s_s85(sy, i85=i85, mv85=mv85):
                            sy.wait_ge(vD.h, mv85)
                            qS.inc(sy.dma_start(out=y[i85], in_=YF85[:]), 16)
                            if i85 == 7:
                                sy.wait_ge(qS.h, qS.n)
                        blk.sync(s_s85)
            esAT.close()
            return nc

        with contextlib.ExitStack() as esA3:
            QF = esA3.enter_context(nc.sbuf_tensor("QF", [128, CT, T], BF16))
            KF = esA3.enter_context(nc.sbuf_tensor("KF", [128, CT, T], BF16))
            DCT = esA3.enter_context(nc.sbuf_tensor("DCT", [128, NH * CK],
                                                    F32))
            QSC = esA3.enter_context(nc.sbuf_tensor("QSC", [128, CT * CK],
                                                    F32))
            KSC2 = esA3.enter_context(nc.sbuf_tensor("KSC2", [128, D], F32))
            GCT2 = esA3.enter_context(nc.sbuf_tensor("GCT2", [128, 512], F32))
            QS = esA3.enter_context(nc.sbuf_tensor("QS", [128, CT, CK], BF16))
            KS2 = esA3.enter_context(nc.sbuf_tensor("KS2", [128, D], BF16))
            ATS = esA3.enter_context(nc.sbuf_tensor("ATS", [128, 4, 512],
                                                    BF16))
            GTD = esA3.enter_context(nc.sbuf_tensor("GTD", [128, 2, 512],
                                                    BF16))

            with nc.Block() as blk:
                def s_tr(sy):
                    sy.wait_ge(qS.h, qS.n)
                    for i in range(CT):
                        qS.inc(sy.dma_start_transpose(
                            QF[:, i, :], trb_d[:, i * 128:(i + 1) * 128]), 16)
                    for i in range(CT):
                        qS.inc(sy.dma_start_transpose(
                            KF[:, i, :], trb2_d[:, i * 128:(i + 1) * 128]),
                            16)
                    qS.inc(sy.dma_start(out=DCT[:], in_=decayT_i), 16)
                    qS.inc(sy.dma_start(out=QSC[:], in_=qscT_i), 16)
                    qS.inc(sy.dma_start(out=KSC2[:], in_=kscT_i), 16)
                    qS.inc(sy.dma_start(out=GCT2[:], in_=gctab_i), 16)
                blk.sync(s_tr)
                m_tr = qS.n

                prev = {(-1, k): 0 for k in
                        ("mm", "ats", "su", "ssb", "gated", "gst")}
                QSCv = QSC[:].rearrange("p (c k) -> p c k", c=CT)
                GSILv = GSIL
                NSC = int(os.environ.get("KNSC", str(NCH)))
                SCP = int(os.environ.get("KSCPART", "5"))
                for n_ in range(NSC):
                    def v_qk(v, n_=n_):
                        if n_ == 0:
                            v.wait_ge(qS.h, m_tr)
                        v.wait_ge(pD.h, prev[(n_ - 1, "mm")])
                        v.tensor_tensor(QS[:],
                                        QF[:, :, n_ * 128:(n_ + 1) * 128],
                                        QSCv, op=ALU.mult)
                        v.tensor_tensor(KS2[:], KT[:, n_, :], KSC2[:],
                                        op=ALU.mult)
                    VEC(blk, v_qk)
                    prev[(n_, "qk")] = vD.n

                    if SCP < 2:
                        prev[(n_, "ats")] = vD.n
                        prev[(n_, "mm")] = pD.n
                        prev[(n_, "su")] = vD.n
                        prev[(n_, "ssb")] = aD.n
                        prev[(n_, "gated")] = vD.n
                        continue
                    at_m = {}
                    NBG = int(os.environ.get("KNBG", "4"))
                    for bg in range(NBG):
                        def t_sc(te, n_=n_, bg=bg):
                            if bg == 0:
                                te.wait_ge(vD.h, prev[(n_, "qk")])
                                te.wait_ge(pD.h, prev[(n_ - 1, "mm")])
                            if bg >= 2:
                                te.wait_ge(vD.h, at_m[bg - 2])
                            for hh in range(4):
                                h = bg * 4 + hh
                                base = 64 * (h % 2)
                                for _ in range(4):
                                    te.nop()
                                mm = te.matmul(
                                    PB[bg % 2][:, hh * 128:(hh + 1) * 128],
                                    KF[base:base + 64, h // 2,
                                       n_ * 128:(n_ + 1) * 128],
                                    QF[base:base + 64, h // 2,
                                       n_ * 128:(n_ + 1) * 128],
                                    start=True, stop=True)
                            pD.inc(mm)
                        blk.tensor(t_sc)
                        msc = pD.n

                        def v_at(v, n_=n_, bg=bg, msc=msc):
                            v.wait_ge(pD.h, msc)
                            v.tensor_tensor(
                                ATS[:, bg, :], PB[bg % 2][:, :],
                                DCT[:, bg * 512:(bg + 1) * 512], op=ALU.mult)
                        VEC(blk, v_at)
                        at_m[bg] = vD.n
                    prev[(n_, "ats")] = vD.n
                    if NBG < 4:
                        prev[(n_, "mm")] = pD.n
                        prev[(n_, "su")] = vD.n
                        prev[(n_, "ssb")] = aD.n
                        prev[(n_, "gated")] = vD.n
                        continue

                    if SCP < 3:
                        prev[(n_, "mm")] = pD.n
                        prev[(n_, "su")] = vD.n
                        prev[(n_, "ssb")] = aD.n
                        prev[(n_, "gated")] = vD.n
                        continue

                    def t_ov(te, n_=n_):
                        te.wait_ge(vD.h, prev[(n_, "ats")])
                        te.wait_ge(vD.h, prev[(n_ - 1, "gated")])
                        te.wait_ge(vD.h, prev[(n_ - 1, "su")])
                        te.wait_ge(aD.h, prev[(n_ - 1, "ssb")])
                        for _ in range(4):
                            te.nop()
                        for h in range(NH):
                            base = 64 * (h % 2)
                            bank = 2 + (h // 8)
                            col = 128 * ((h // 2) % 4)
                            te.matmul(
                                PB[bank][base:base + 64, col:col + 128],
                                VT[:, n_, h * DH:(h + 1) * DH],
                                ATS[:, h // 4,
                                    (h % 4) * 128:(h % 4 + 1) * 128],
                                start=True, stop=False)
                            for _ in range(4 if base else 0):
                                te.nop()
                            te.matmul(
                                PB[bank][base:base + 64, col:col + 128],
                                SSB[base:base + 64,
                                    (h // 2) * DH:(h // 2) * DH + DH],
                                QS[base:base + 64, h // 2, :],
                                start=False, stop=True)
                            for _ in range(4 if base else 0):
                                te.nop()
                        for h in range(NH):
                            base = 64 * (h % 2)
                            col = (h // 2) * DH
                            mm = te.matmul(
                                PB[4][base:base + 64, col:col + DH],
                                KS2[:, h * DH:(h + 1) * DH],
                                VT[:, n_, h * DH:(h + 1) * DH],
                                start=True, stop=True)
                        pD.inc(mm)
                    blk.tensor(t_ov)
                    prev[(n_, "mm")] = pD.n

                    if SCP < 4:
                        prev[(n_, "su")] = vD.n
                        prev[(n_, "ssb")] = aD.n
                        prev[(n_, "gated")] = vD.n
                        continue

                    def v_su(v, n_=n_):
                        v.wait_ge(pD.h, prev[(n_, "mm")])
                        v.tensor_tensor(SST[:], SST[:], GCT2[:], op=ALU.mult)
                        v.tensor_tensor(SST[:], SST[:], PB[4][:, :],
                                        op=ALU.add)
                    VEC(blk, v_su)
                    prev[(n_, "su")] = vD.n

                    def s_sb2(sc, n_=n_):
                        sc.wait_ge(vD.h, prev[(n_, "su")])
                        sc.copy(SSB[:], SST[:])
                    SCA(blk, s_sb2)
                    prev[(n_, "ssb")] = aD.n

                    if SCP < 5:
                        prev[(n_, "gated")] = vD.n
                        continue

                    gt_m = {}
                    for b2 in range(2):
                        def v_gt(v, n_=n_, b2=b2, mgt=dsGT.n(b2)):
                            v.wait_ge(pD.h, prev[(n_, "mm")])
                            v.wait_ge(dsGT.h(b2), mgt)
                            v.tensor_tensor(
                                GTD[:, b2, :].rearrange("p (c k) -> p c k",
                                                        k=128),
                                PB[2 + b2][:, :].rearrange(
                                    "p (c k) -> p c k", k=128),
                                GSILv[:, 4 * b2:4 * b2 + 4,
                                      n_ * 128:(n_ + 1) * 128], op=ALU.mult)
                        VEC(blk, v_gt)
                        gt_m[b2] = vD.n
                    prev[(n_, "gated")] = vD.n

                    def g_gst(g, n_=n_, gt_m=dict(gt_m)):
                        for b2 in range(2):
                            g.wait_ge(vD.h, gt_m[b2])
                            dsGT.issue(g.dma_start(
                                out=gated_d[4 * b2:4 * b2 + 4, :,
                                            n_ * 128:(n_ + 1) * 128
                                            ].rearrange("c p k -> p c k"),
                                in_=GTD[:, b2, :].rearrange(
                                    "p (c k) -> p c k", k=128)), b2)
                    blk.gpsimd(g_gst)
                    prev[(n_, "gst")] = 0
        esAT.close()

        if STAGE < 2:
            dbg18 = nc.dram_tensor("dbg18", [CT, 128, T], I16,
                                   kind="ExternalOutput").ap()
            with (
                nc.sbuf_tensor("Y18F", [128, T], F32) as Y18F,
                nc.sbuf_tensor("G18B", [128, T], BF16) as G18B,
                nc.sbuf_tensor("N18I", [128, T], I16) as N18I,
                nc.Block() as blk,
            ):
                def v_z18(v):
                    v.memset(Y18F[:], 0.0)
                VEC(blk, v_z18)
                mvz = vD.n

                def s_y18(sy, mvz=mvz):
                    sy.wait_ge(vD.h, mvz)
                    for c18 in range(CT):
                        qS.inc(sy.dma_start(out=y[c18], in_=Y18F[:]), 16)
                blk.sync(s_y18)
                for c18 in range(CT):
                    def s_l18(sy, c18=c18, mg0=dsGT.n(0), mg1=dsGT.n(1),
                              mvp=vD.n):
                        if c18 == 0:
                            sy.wait_ge(dsGT.h(0), mg0)
                            sy.wait_ge(dsGT.h(1), mg1)
                        sy.wait_ge(vD.h, mvp)
                        qS.inc(sy.dma_start(out=G18B[:], in_=gated_d[c18]),
                               16)
                    blk.sync(s_l18)
                    mq18 = qS.n

                    def v_n18(v, mq18=mq18):
                        v.wait_ge(qS.h, mq18)
                        v.tensor_tensor(N18I[:], G18B[:], G18B[:],
                                        op=ALU.is_equal)
                    VEC(blk, v_n18)
                    mv18 = vD.n

                    def s_s18(sy, c18=c18, mv18=mv18):
                        sy.wait_ge(vD.h, mv18)
                        qS.inc(sy.dma_start(out=dbg18[c18], in_=N18I[:]), 16)
                        if c18 == CT - 1:
                            sy.wait_ge(qS.h, qS.n)
                    blk.sync(s_s18)
            return nc

        esX2 = contextlib.ExitStack()
        X2 = esX2.enter_context(nc.sbuf_tensor("X2", [128, CT, T], F32))
        with contextlib.ExitStack() as esA4:
            GSTF = esA4.enter_context(nc.sbuf_tensor("GSTF", [128, CT, T],
                                                     BF16))
            WO = esA4.enter_context(nc.sbuf_tensor("WO", [128, CT, D], BF16))
            X1S = esA4.enter_context(nc.sbuf_tensor("X1S", [128, 2, T], F32))
            with nc.Block() as blk:
                gt_gates = [dsGT.n(0), dsGT.n(1)]
                mq_x1 = qS.n

                def s_lg(sy, gt_gates=gt_gates, mq_x1=mq_x1):
                    sy.wait_ge(dsGT.h(0), gt_gates[0])
                    sy.wait_ge(dsGT.h(1), gt_gates[1])
                    sy.wait_ge(qS.h, mq_x1)
                    qS.inc(sy.dma_start(
                        out=GSTF[:],
                        in_=gated_d.rearrange("c p t -> p c t")), 16)
                blk.sync(s_lg)

                def g_lw(g):
                    qG.inc(g.dma_start(
                        out=WO[:],
                        in_=woT.rearrange("(c p) d -> p c d", p=128)), 16)
                blk.gpsimd(g_lw)
                m_lg = qS.n
                m_lw = qG.n
                ev_w = {-1: vD.n, -2: vD.n}
                for dt in range(CT):
                    def s_lx(sy, dt=dt):
                        sy.wait_ge(vD.h, ev_w[dt - 2])
                        dsX1.issue(sy.dma_start(out=X1S[:, dt % 2, :],
                                                in_=x1_d[dt]), dt % 2)
                    blk.sync(s_lx)
                    m_lx = dsX1.n(dt % 2)
                    for q in range(4):
                        gk = dt * 4 + q

                        def t_wo(te, dt=dt, q=q, gk=gk, m_lx=m_lx):
                            if gk == 0:
                                te.wait_ge(qS.h, m_lg)
                                te.wait_ge(qG.h, m_lw)
                            if gk >= 2:
                                te.wait_ge(vD.h, ev_w[-2] if gk < 2 else
                                           ev_w.get(("g", gk - 2), 0))
                            for c in range(CT):
                                mm = te.matmul(
                                    PB[gk % 2][:, :],
                                    WO[:, c, dt * 128:(dt + 1) * 128],
                                    GSTF[:, c, q * 512:(q + 1) * 512],
                                    start=(c == 0), stop=(c == CT - 1))
                            pD.inc(mm)
                        blk.tensor(t_wo)
                        mpw = pD.n

                        def v_x2(v, dt=dt, q=q, mpw=mpw, m_lx=m_lx):
                            v.wait_ge(pD.h, mpw)
                            v.wait_ge(dsX1.h(dt % 2), m_lx)
                            v.tensor_tensor(
                                X2[:, dt, q * 512:(q + 1) * 512],
                                X1S[:, dt % 2, q * 512:(q + 1) * 512],
                                PB[gk % 2][:, :], op=ALU.add)
                        VEC(blk, v_x2)
                        ev_w[("g", gk)] = vD.n
                    ev_w[dt] = vD.n

        if STAGE >= 3:
            emit_ffn(X2)

        with nc.Block() as blk:
            def s_out(sy, mv=vD.n):
                sy.wait_ge(vD.h, mv)
                qS.inc(sy.dma_start(
                    out=y.rearrange("c p t -> p c t"), in_=X2[:]), 16)
                sy.wait_ge(qS.h, qS.n)
            blk.sync(s_out)
        esX2.close()



    return nc


_CONSTS = None


def kernel(**inputs):
    global _CONSTS
    if _CONSTS is None:
        _CONSTS = _host_consts()
    cns = _CONSTS
    x = np.ascontiguousarray(inputs["L0_attn_x"])
    xT = np.ascontiguousarray(np.swapaxes(x, 1, 2))

    def wT(a):
        return np.ascontiguousarray(np.asarray(a).T)

    base = dict(
        wqT=wT(inputs["L0_attn_wq"]), wkT=wT(inputs["L0_attn_wk"]),
        wvT=wT(inputs["L0_attn_wv"]), wgTT=wT(inputs["L0_attn_wg"]),
        woT=wT(inputs["L0_attn_wo"]),
        w_gate=np.ascontiguousarray(inputs["L0_ffn_w_gate"]),
        w_up=np.ascontiguousarray(inputs["L0_ffn_w_up"]),
        w_down=np.ascontiguousarray(inputs["L0_ffn_w_down"]),
        decayT=cns["decayT"], qscT=cns["qscT"], kscT=cns["kscT"],
        gctab=cns["gctab"], ident=cns["ident"],
    )
    in_maps = []
    for c in range(8):
        b, h = c // 2, c % 2
        own = slice(h * T, (h + 1) * T)
        oth = slice((1 - h) * T, (2 - h) * T)
        m = dict(base)
        m["x_own"] = np.ascontiguousarray(
            xT[b, :, own].reshape(CT, 128, T))
        m["x_oth"] = np.ascontiguousarray(
            xT[b, :, oth].reshape(CT, 128, T))
        m["cos_own"] = np.ascontiguousarray(cns["cos_t"][own])
        m["sinS_own"] = np.ascontiguousarray(cns["sinS_t"][own])
        m["cos_oth"] = np.ascontiguousarray(cns["cos_t"][oth])
        m["sinS_oth"] = np.ascontiguousarray(cns["sinS_t"][oth])
        m["maskS"] = np.full((128, 1), float(h), np.float32)
        in_maps.append(m)

    nc = build_nc()
    res = run_bass_kernel_spmd(nc, in_maps, core_ids=list(range(8)))
    out = np.empty((B, S, D), np.float32)
    for c in range(8):
        b, h = c // 2, c % 2
        yv = res.results[c]["y"].reshape(D, T)
        out[b, h * T:(h + 1) * T, :] = yv.T
    return out


# version 10 (final)
# speedup vs baseline: 1.7739x; 1.0455x over previous
import os
import contextlib
import numpy as np
import ml_dtypes
import concourse.bass as bass
import concourse.mybir as mybir
from concourse.bass_utils import run_bass_kernel_spmd

F32 = mybir.dt.float32
BF16 = mybir.dt.bfloat16
I16 = mybir.dt.int16
AF = mybir.ActivationFunctionType
ALU = mybir.AluOpType
AX = mybir.AxisListType

MAGIC = float(3 * 2 ** 22)
B, S, D, I3 = 4, 4096, 1024, 3072
T = S // 2
NH, DH, CK = 16, 64, 128
NCH = T // CK
CT = D // 128
OT = I3 // 128
QMAX = 32767.0
EPS = 1e-5
STAGE = float(os.environ.get("KSTAGE", "3"))


def _host_consts():
    f64 = np.float64
    inv = 1.0 / (10000.0 ** (np.arange(0, DH, 2, dtype=f64) / DH))
    t = np.arange(S, dtype=f64)
    fr = np.outer(t, inv)
    cos_t = np.concatenate([np.cos(fr), np.cos(fr)], axis=1).astype(np.float32)
    sn = np.sin(fr)
    sinS_t = np.concatenate([-sn, sn], axis=1).astype(np.float32)
    gam = 1.0 - np.exp2(-5.0 - np.arange(NH, dtype=f64))
    i = np.arange(CK)
    dm = i[None, :] - i[:, None]
    low = (dm >= 0).astype(f64)
    decayT = np.zeros((CK, NH * CK), np.float32)
    for h in range(NH):
        decayT[:, h * CK:(h + 1) * CK] = (
            np.power(gam[h], np.maximum(dm, 0)) * low / 8.0).astype(np.float32)
    qscT = np.zeros((128, CT * CK), np.float32)
    for m in range(CT):
        for hf in range(2):
            qscT[hf * 64:(hf + 1) * 64, m * CK:(m + 1) * CK] = (
                np.power(gam[2 * m + hf], i + 1) / 8.0).astype(np.float32)[None, :]
    kscT = np.zeros((CK, D), np.float32)
    for h in range(NH):
        kscT[:, h * DH:(h + 1) * DH] = np.power(
            gam[h], CK - 1 - i).astype(np.float32)[:, None]
    gctab = np.zeros((128, 512), np.float32)
    for m in range(CT):
        for hf in range(2):
            gctab[hf * 64:(hf + 1) * 64, m * DH:(m + 1) * DH] = float(
                np.power(gam[2 * m + hf], CK))
    ident = np.eye(128, dtype=np.float32)
    return dict(cos_t=cos_t, sinS_t=sinS_t, decayT=decayT, qscT=qscT,
                kscT=kscT, gctab=gctab, ident=ident)


class Sem:
    def __init__(self, es, nc, name):
        self.h = es.enter_context(nc.semaphore(name))
        self.n = 0

    def inc(self, instr, k=1):
        instr.then_inc(self.h, k)
        self.n += k
        return self.n


_FENCED = {"tensor_tensor", "tensor_scalar", "tensor_reduce", "reciprocal",
           "tensor_copy", "memset", "activation", "copy",
           "scalar_tensor_tensor", "mul", "add"}


class DSlot:

    def __init__(self, es, nc, name, nslots):
        self.sems = [Sem(es, nc, "%s%d" % (name, i)) for i in range(nslots)]

    def issue(self, instr, slot):
        return self.sems[slot].inc(instr, 16)

    def wait(self, eng, slot):
        eng.wait_ge(self.sems[slot].h, self.sems[slot].n)

    def n(self, slot):
        return self.sems[slot].n

    def h(self, slot):
        return self.sems[slot].h


class FE:

    def __init__(self, eng, sem):
        self._e = eng
        self._s = sem

    def wait_ge(self, h, n):
        self._e.wait_ge(h, n)

    def dma_start(self, *a, **k):
        return self._e.dma_start(*a, **k)

    def dma_start_transpose(self, *a, **k):
        return self._e.dma_start_transpose(*a, **k)

    def __getattr__(self, name):
        fn = getattr(self._e, name)
        if name in _FENCED:
            s = self._s

            def wrapped(*a, **k):
                ins = fn(*a, **k)
                s.n += 1
                ins.then_inc(s.h, 1)
                self._e.wait_ge(s.h, s.n)
                return ins
            return wrapped
        return fn


def build_nc():
    nc = bass.Bass(target_bir_lowering=False)

    def din(name, shape, dt=F32):
        return nc.dram_tensor(name, shape, dt, kind="ExternalInput").ap()

    x_own = din("x_own", [CT, 128, T])
    x_oth = din("x_oth", [CT, 128, T])
    wqT = din("wqT", [D, D]); wkT = din("wkT", [D, D]); wvT = din("wvT", [D, D])
    wgTT = din("wgTT", [D, D]); woT = din("woT", [D, D])
    w_gate = din("w_gate", [I3, D]); w_up = din("w_up", [I3, D])
    w_down = din("w_down", [D, I3])
    cos_own = din("cos_own", [T, DH]); sinS_own = din("sinS_own", [T, DH])
    cos_oth = din("cos_oth", [T, DH]); sinS_oth = din("sinS_oth", [T, DH])
    decayT_i = din("decayT", [CK, NH * CK])
    qscT_i = din("qscT", [128, CT * CK])
    kscT_i = din("kscT", [CK, D])
    gctab_i = din("gctab", [128, 512])
    maskS_i = din("maskS", [128, 1])
    ident_i = din("ident", [128, 128])

    y = nc.dram_tensor("y", [CT, 128, T], F32, kind="ExternalOutput").ap()

    def dsc(name, shape, dt):
        return nc.dram_tensor(name, shape, dt).ap()

    wg_q = dsc("wg_q", [I3, D], BF16)
    wu_q = dsc("wu_q", [I3, D], BF16)
    wd_q = dsc("wd_q", [D, I3], BF16)
    U_d = dsc("U_d", [OT, 128, T], F32)
    G_d = dsc("G_d", [OT, 128, T], F32)
    g16_d = dsc("g16_d", [OT, 128, T], I16)
    g216_d = dsc("g216_d", [OT, 128, T], I16)
    u16_d = dsc("u16_d", [OT, 128, T], I16)
    x1_d = dsc("x1_d", [CT, 128, T], F32)
    gated_d = dsc("gated_d", [CT, 128, T], BF16)
    trb_d = dsc("trb_d", [T, D], BF16)
    trb2_d = dsc("trb2_d", [T, D], BF16)

    es = contextlib.ExitStack()
    ones_col = es.enter_context(nc.sbuf_tensor("ones_col", [128, 1], F32))
    ones_row = es.enter_context(nc.sbuf_tensor("ones_row", [1, 128], F32))
    COLS = es.enter_context(nc.sbuf_tensor("COLS", [128, 16], F32))
    SSC = es.enter_context(nc.sbuf_tensor("SSC", [1, 64], F32))
    MXC = es.enter_context(nc.sbuf_tensor("MXC", [128, 128], F32))
    IDN = es.enter_context(nc.sbuf_tensor("IDN", [128, 128], F32))
    MSKC = es.enter_context(nc.sbuf_tensor("MSKC", [128, 1], F32))
    SST = es.enter_context(nc.sbuf_tensor("SST", [128, 512], F32))
    SSB = es.enter_context(nc.sbuf_tensor("SSB", [128, 512], BF16))
    PB = [es.enter_context(nc.psum_tensor("pb%d" % k, [128, 512], F32))
          for k in range(8)]

    qS = Sem(es, nc, "qS")
    gD = Sem(es, nc, "gD")
    qA = Sem(es, nc, "qA")
    qV = Sem(es, nc, "qV")
    qG = Sem(es, nc, "qG")
    vD = Sem(es, nc, "vD")
    aD = Sem(es, nc, "aD")
    pD = Sem(es, nc, "pD")
    dsWR = DSlot(es, nc, "dsWR", 2)
    dsWQ = DSlot(es, nc, "dsWQ", 2)
    dsEV = DSlot(es, nc, "dsEV", 2)
    dsGS = DSlot(es, nc, "dsGS", 2)
    dsGI = DSlot(es, nc, "dsGI", 3)
    dsUS = DSlot(es, nc, "dsUS", 2)
    dsUI = DSlot(es, nc, "dsUI", 2)
    dsG2 = DSlot(es, nc, "dsG2", 2)
    dsL4 = DSlot(es, nc, "dsL4", 2)
    dsGT = DSlot(es, nc, "dsGT", 2)
    dsX1 = DSlot(es, nc, "dsX1", 2)

    _uid = [0]

    def un(base):
        _uid[0] += 1
        return "%s_%d" % (base, _uid[0])

    def VEC(blk, fn):
        blk.vector(lambda e: fn(FE(e, vD)))

    def SCA(blk, fn):
        blk.scalar(lambda e: fn(FE(e, aD)))

    def PLC(blk, fn):
        blk.gpsimd(lambda e: fn(FE(e, gD)))

    def EW(blk, fn, pool):
        (PLC if pool else VEC)(blk, fn)

    def ewm(pool):
        s = gD if pool else vD
        return (s.h, s.n)

    iRX, iSX, iRUP, iSUP, iRG, iSG, iK2, iK3, iOS, iSO = range(10)
    (jEPS, jMX, jSX, jRX, jT0, jT1, jSG, jSUP, jSG2, jRG2, jK2, jSO, jRO,
     jK3, jMG, jMUP, jMP, jMP2, jRG, jRUP) = range(20)

    _strbox = [None]

    def STRr(r):
        return _strbox[0][0:1, r * T:(r + 1) * T]

    def final_max(blk, ncols, out_j, c0=0):
        def v1(v):
            v.tensor_reduce(MXC[:, 127:128], MXC[:, c0:c0 + ncols],
                                   op=ALU.max, axis=AX.X)
        VEC(blk, v1)
        mv = vD.n

        def t1(te):
            te.wait_ge(vD.h, mv)
            pD.inc(te.transpose(PB[7][0:1, 0:128], MXC[:, 127:128], IDN[:]))
        blk.tensor(t1)
        mp = pD.n

        def v2(v):
            v.wait_ge(pD.h, mp)
            v.tensor_reduce(SSC[0:1, out_j:out_j + 1], PB[7][0:1, 0:128],
                            op=ALU.max, axis=AX.X)
            v.memset(MXC[:, c0:c0 + ncols], 0.0)
            v.memset(MXC[:, 127:128], 0.0)
        VEC(blk, v2)

    def bcast_scalar(blk, src_j, dst_i):
        mv = vD.n
        ma = aD.n

        def t1(te):
            te.wait_ge(vD.h, mv)
            te.wait_ge(aD.h, ma)
            pD.inc(te.matmul(PB[7][:, 0:1], ones_row[:],
                             SSC[0:1, src_j:src_j + 1], start=True, stop=True))
        blk.tensor(t1)
        mp = pD.n

        def s1(sc):
            sc.wait_ge(pD.h, mp)
            sc.copy(COLS[:, dst_i:dst_i + 1], PB[7][:, 0:1])
        SCA(blk, s1)

    def qdq_scale(blk, m_j, s_j, r_j):
        def v1(v):
            v.wait_ge(vD.h, vD.n)
            v.tensor_scalar(SSC[0:1, s_j:s_j + 1], SSC[0:1, m_j:m_j + 1],
                            1.0 / QMAX, 1e-12, op0=ALU.mult, op1=ALU.max)
            v.reciprocal(SSC[0:1, r_j:r_j + 1], SSC[0:1, s_j:s_j + 1])
        VEC(blk, v1)

    def emit_ln(XH, SQ, qdq, XQ):
        esL = contextlib.ExitStack()
        _strbox[0] = esL.enter_context(nc.sbuf_tensor(un("STR"), [1, 2 * T],
                                                      F32))
        mq0 = qS.n
        with nc.Block() as blk:
            pe_done = []
            for ct in range(CT):
                ma_prev = aD.n

                def s_sq(sc, ct=ct, mq0=mq0):
                    if ct == 0:
                        sc.wait_ge(qS.h, mq0)
                    if ct >= 2:
                        sc.wait_ge(pD.h, pe_done[ct - 2])
                    sc.activation(SQ[:, ct % 2, :], XH[:, ct, :],
                                         AF.Square)
                SCA(blk, s_sq)
                ma = aD.n

                def t_mm(te, ct=ct, ma=ma):
                    te.wait_ge(aD.h, ma)
                    for q in range(4):
                        st = (ct == 0)
                        sp = (ct == CT - 1)
                        te.matmul(PB[q][0:1, :], ones_col[:],
                                  XH[:, ct, q * 512:(q + 1) * 512],
                                  start=st, stop=sp)
                        mm = te.matmul(PB[4 + q][0:1, :], ones_col[:],
                                       SQ[:, ct % 2, q * 512:(q + 1) * 512],
                                       start=st, stop=sp)
                    pD.inc(mm)
                blk.tensor(t_mm)
                pe_done.append(pD.n)

            mp = pD.n

            def s_ev(sc):
                sc.wait_ge(pD.h, mp)
                for q in range(4):
                    sc.copy(STRr(0)[0:1, q * 512:(q + 1) * 512], PB[q][0:1, :])
                    sc.copy(STRr(1)[0:1, q * 512:(q + 1) * 512],
                                   PB[4 + q][0:1, :])
            SCA(blk, s_ev)
            ma = aD.n

            def v_st(v):
                v.wait_ge(aD.h, ma)
                v.tensor_scalar(STRr(0), STRr(0), 1.0 / D, None, op0=ALU.mult)
                v.tensor_scalar(STRr(1), STRr(1), 1.0 / D, None, op0=ALU.mult)
                v.tensor_tensor(SQ[0:1, 0, :], STRr(0), STRr(0), op=ALU.mult)
                v.tensor_tensor(STRr(1), STRr(1), SQ[0:1, 0, :],
                                       op=ALU.subtract)
            VEC(blk, v_st)
            mv = vD.n

            def s_sd(sc):
                sc.wait_ge(vD.h, mv)
                sc.activation(STRr(1), STRr(1), AF.Sqrt,
                                     bias=SSC[0:1, jEPS:jEPS + 1])
            SCA(blk, s_sd)
            ma2 = aD.n

            def v_rs(v):
                v.wait_ge(aD.h, ma2)
                v.reciprocal(STRr(1), STRr(1))
            VEC(blk, v_rs)

        if not qdq:
            with nc.Block() as blk:
                last_reads = [vD.n, vD.n]
                for q in range(4):
                    mv0 = last_reads[0]
                    mv1 = last_reads[1]

                    def t_bc(te, q=q, mv0=mv0, mv1=mv1):
                        te.wait_ge(vD.h, max(mv0, mv1, 0))
                        te.matmul(PB[4][:, :], ones_row[:],
                                  STRr(0)[0:1, q * 512:(q + 1) * 512],
                                  start=True, stop=True)
                        pD.inc(te.matmul(PB[5][:, :], ones_row[:],
                                         STRr(1)[0:1, q * 512:(q + 1) * 512],
                                         start=True, stop=True))
                    blk.tensor(t_bc)
                    mp = pD.n

                    def v_nm(v, q=q, mp=mp):
                        v.wait_ge(pD.h, mp)
                        for ct in range(CT):
                            sl = slice(q * 512, (q + 1) * 512)
                            v.tensor_tensor(SQ[:, 0, 0:512], XH[:, ct, sl],
                                            PB[4][:, :], op=ALU.subtract)
                            mm = v.tensor_tensor(XQ[:, ct, sl],
                                                 SQ[:, 0, 0:512],
                                                 PB[5][:, :], op=ALU.mult)
                        mm
                    VEC(blk, v_nm)
                    last_reads = [vD.n, vD.n]
            esL.close()
            return

        with nc.Block() as blk:
            for q in range(4):
                mv = vD.n

                def t_bc(te, q=q, mv=mv):
                    te.wait_ge(vD.h, mv)
                    te.matmul(PB[4][:, :], ones_row[:],
                              STRr(0)[0:1, q * 512:(q + 1) * 512],
                              start=True, stop=True)
                    pD.inc(te.matmul(PB[5][:, :], ones_row[:],
                                     STRr(1)[0:1, q * 512:(q + 1) * 512],
                                     start=True, stop=True))
                blk.tensor(t_bc)
                mp = pD.n

                def v_nm(v, q=q, mp=mp):
                    v.wait_ge(pD.h, mp)
                    for ct in range(CT):
                        sl = slice(q * 512, (q + 1) * 512)
                        v.tensor_tensor(SQ[:, 0, 0:512], XH[:, ct, sl],
                                        PB[4][:, :], op=ALU.subtract)
                        v.tensor_tensor(SQ[:, 1, 0:512], SQ[:, 0, 0:512],
                                        PB[5][:, :], op=ALU.mult)
                        mm = v.tensor_reduce(
                            MXC[:, q * 8 + ct:q * 8 + ct + 1],
                            SQ[:, 1, 0:512], op=ALU.max, axis=AX.X,
                            apply_absolute_value=True)
                    mm
                VEC(blk, v_nm)

        with nc.Block() as blk:
            final_max(blk, 32, jMX)
            qdq_scale(blk, jMX, jSX, jRX)
            bcast_scalar(blk, jSX, iSX)

            def v_rr(v):
                v.wait_ge(vD.h, vD.n)
                v.tensor_scalar(STRr(1), STRr(1), SSC[0:1, jRX:jRX + 1], None,
                                op0=ALU.mult)
                v.tensor_tensor(STRr(0), STRr(0), STRr(1), op=ALU.mult)
            VEC(blk, v_rr)

        with nc.Block() as blk:
            ma0 = aD.n
            for q in range(4):
                mv = vD.n

                def t_bc(te, q=q, mv=mv):
                    te.wait_ge(vD.h, mv)
                    te.wait_ge(aD.h, ma0)
                    te.matmul(PB[4][:, :], ones_row[:],
                              STRr(1)[0:1, q * 512:(q + 1) * 512],
                              start=True, stop=True)
                    pD.inc(te.matmul(PB[5][:, :], ones_row[:],
                                     STRr(0)[0:1, q * 512:(q + 1) * 512],
                                     start=True, stop=True))
                blk.tensor(t_bc)
                mp = pD.n

                def v_qz(v, q=q, mp=mp):
                    v.wait_ge(pD.h, mp)
                    for ct in range(CT):
                        sl = slice(q * 512, (q + 1) * 512)
                        v.tensor_tensor(SQ[:, 0, 0:512], XH[:, ct, sl],
                                        PB[4][:, :], op=ALU.mult)
                        v.tensor_tensor(SQ[:, 1, 0:512], SQ[:, 0, 0:512],
                                        PB[5][:, :], op=ALU.subtract)
                        v.tensor_scalar(SQ[:, 0, 0:512], SQ[:, 1, 0:512],
                                        MAGIC, MAGIC, op0=ALU.add,
                                        op1=ALU.subtract)
                        mm = v.tensor_scalar(XQ[:, ct, sl], SQ[:, 0, 0:512],
                                             COLS[:, iSX:iSX + 1], None,
                                             op0=ALU.mult)
                    mm
                VEC(blk, v_qz)
        esL.close()

    def emit_ffn(XH):
        esF = contextlib.ExitStack()
        WT = esF.enter_context(nc.sbuf_tensor(un("WT_f"), [128, OT, D], BF16))
        WTg = WT[:].rearrange("p o d -> p (o d)").rearrange(
            "p (i o) -> p i o", i=CT)

        with contextlib.ExitStack() as esQ:
            XQ = esQ.enter_context(nc.sbuf_tensor(un("XQ_f"), [128, CT, T], BF16))
            EV = esQ.enter_context(nc.sbuf_tensor(un("EV_f"), [128, 2, 512], F32))
            esSQ = contextlib.ExitStack()
            SQ = esSQ.enter_context(nc.sbuf_tensor(un("SQ_f"), [128, 2, T], F32))
            emit_ln(XH, SQ, True, XQ)
            esSQ.close()

            def ph1_groups(wsta, wdst, blk, ivl=None):
                mp_wt = pD.n
                mq0 = dsWQ.n(0)
                mq1 = dsWQ.n(1)

                def s_ld(sy, wsta=wsta, mp_wt=mp_wt, mq0=mq0, mq1=mq1):
                    sy.wait_ge(pD.h, mp_wt)
                    sy.wait_ge(qS.h, qS.n)
                    sy.wait_ge(dsWQ.h(0), mq0)
                    sy.wait_ge(dsWQ.h(1), mq1)
                    for i in range(CT):
                        qS.inc(sy.dma_start_transpose(
                            WTg[:, i, :],
                            wsta[:, i * 128:(i + 1) * 128]), 16)
                blk.sync(s_ld)
                m_ld = qS.n
                mv_xq = vD.n

                evac_done = []
                red_done = []
                g = 0
                for ot in range(OT):
                    for q in range(4):
                        def t_g(te, ot=ot, q=q, g=g, m_ld=m_ld,
                                mv_xq=mv_xq):
                            if g == 0:
                                te.wait_ge(qS.h, m_ld)
                                te.wait_ge(vD.h, mv_xq)
                            if g >= 2:
                                te.wait_ge(aD.h, evac_done[g - 2])
                            for i in range(CT):
                                mm = te.matmul(
                                    PB[g % 2][:, :],
                                    WTg[:, i, ot * 128:(ot + 1) * 128],
                                    XQ[:, i, q * 512:(q + 1) * 512],
                                    start=(i == 0), stop=(i == CT - 1))
                            pD.inc(mm)
                        blk.tensor(t_g)
                        mp = pD.n

                        def s_g(sc, ot=ot, q=q, g=g, mp=mp, wdst=wdst,
                                mev=dsEV.n(g % 2), mv_xq=mv_xq):
                            sc.wait_ge(pD.h, mp)
                            sc.wait_ge(dsEV.h(g % 2), mev)
                            sc.wait_ge(vD.h, red_done[g - 2]
                                       if g >= 2 else mv_xq)
                            sc.copy(EV[:, g % 2, :],
                                           PB[g % 2][:, :])
                            dsEV.issue(sc.dma_start(
                                out=wdst[ot, :, q * 512:(q + 1) * 512],
                                in_=EV[:, g % 2, :]), g % 2)
                        SCA(blk, s_g)
                        evac_done.append(aD.n)

                        def v_g(v, g=g, ma=aD.n):
                            v.wait_ge(aD.h, ma)
                            v.tensor_reduce(
                                MXC[:, g:g + 1], EV[:, g % 2, :],
                                op=ALU.max, axis=AX.X,
                                apply_absolute_value=True)
                        VEC(blk, v_g)
                        red_done.append(vD.n)
                        g += 1
                    if ivl is not None:
                        ivl(ot)

            with nc.Block() as blk:
                ph1_groups(wg_q, G_d, blk)
            with nc.Block() as blk:
                final_max(blk, 96, jMG)
                qdq_scale(blk, jMG, jSG, jRG)
                bcast_scalar(blk, jRG, iRG)
                bcast_scalar(blk, jSG, iSG)

            with contextlib.ExitStack() as es2:
                GS = es2.enter_context(
                    nc.sbuf_tensor(un("GS_f"), [128, 2, T], F32))
                GI = es2.enter_context(
                    nc.sbuf_tensor(un("GI_f"), [128, 3, T], I16))
                SIG = es2.enter_context(
                    nc.sbuf_tensor(un("SIG_f"), [128, 2, T], F32))
                U16 = es2.enter_context(
                    nc.sbuf_tensor(un("U16_f"), [128, T], F32))
                ts_done = []
                sig_done = []
                fin_done = []
                red2_done = []
                ev0, ev1 = dsEV.n(0), dsEV.n(1)

                with nc.Block() as blk:
                    def ph2_iter(ot):
                        def s_ld(sy, ot=ot):
                            if ot == 0:
                                sy.wait_ge(dsEV.h(0), ev0)
                                sy.wait_ge(dsEV.h(1), ev1)
                            if ot >= 2:
                                sy.wait_ge(*ts_done[ot - 2])
                            dsGS.issue(sy.dma_start(out=GS[:, ot % 2, :],
                                                    in_=G_d[ot]), ot % 2)
                        blk.sync(s_ld)

                        def v_a(v, ot=ot, mld=dsGS.n(ot % 2),
                                mgi=dsGI.n(ot % 3)):
                            v.wait_ge(dsGS.h(ot % 2), mld)
                            if ot >= 2:
                                v.wait_ge(vD.h, red2_done[ot - 2])
                            if ot >= 3:
                                v.wait_ge(aD.h, sig_done[ot - 3])
                                v.wait_ge(*fin_done[ot - 3])
                                v.wait_ge(dsGI.h(ot % 3), mgi)
                            v.tensor_scalar(SIG[:, ot % 2, :],
                                            GS[:, ot % 2, :],
                                            COLS[:, iRG:iRG + 1], MAGIC,
                                            op0=ALU.mult, op1=ALU.add)
                            v.tensor_scalar(GI[:, ot % 3, :],
                                                   SIG[:, ot % 2, :],
                                                   MAGIC, None,
                                                   op0=ALU.subtract)
                        VEC(blk, v_a)
                        ts_done.append((vD.h, vD.n))

                        def s_sig(sc, ot=ot, mts=ts_done[ot]):
                            sc.wait_ge(*mts)
                            sc.activation(SIG[:, ot % 2, :],
                                                 GI[:, ot % 3, :], AF.Sigmoid,
                                                 scale=COLS[:, iSG:iSG + 1])
                        SCA(blk, s_sig)
                        sig_done.append(aD.n)

                        def g_st(g, ot=ot, mts=ts_done[ot]):
                            g.wait_ge(*mts)
                            dsGI.issue(g.dma_start(out=g16_d[ot],
                                                   in_=GI[:, ot % 3, :]),
                                       ot % 3)
                        blk.scalar(g_st)

                        def v_b(v, ot=ot, ma=aD.n):
                            v.wait_ge(aD.h, ma)
                            v.tensor_scalar(U16[:, :],
                                            SIG[:, ot % 2, :],
                                            65536.0, MAGIC, op0=ALU.mult,
                                            op1=ALU.add)
                            v.tensor_scalar(U16[:, :],
                                            U16[:, :],
                                            MAGIC, 65535.0, op0=ALU.subtract,
                                            op1=ALU.min)
                            v.tensor_tensor(SIG[:, ot % 2, :],
                                            GI[:, ot % 3, :],
                                            U16[:, :], op=ALU.mult)
                        VEC(blk, v_b)
                        fin_done.append((vD.h, vD.n))

                        def v_red(v, ot=ot, mfb=fin_done[ot]):
                            v.wait_ge(*mfb)
                            v.tensor_reduce(MXC[:, 96 + ot:97 + ot],
                                                   SIG[:, ot % 2, :],
                                                   op=ALU.max, axis=AX.X,
                                                   apply_absolute_value=True)
                        VEC(blk, v_red)
                        red2_done.append(vD.n)

                    ph1_groups(wu_q, U_d, blk, ivl=ph2_iter)

                with nc.Block() as blk:
                    final_max(blk, 96, jMUP)
                    qdq_scale(blk, jMUP, jSUP, jRUP)
                    bcast_scalar(blk, jRUP, iRUP)
                with nc.Block() as blk:
                    final_max(blk, OT, jMP, c0=96)

                    def v_s2(v):
                        v.wait_ge(vD.h, vD.n)
                        v.tensor_tensor(SSC[0:1, jT0:jT0 + 1],
                                        SSC[0:1, jMP:jMP + 1],
                                        SSC[0:1, jSG:jSG + 1], op=ALU.mult)
                        v.tensor_scalar(SSC[0:1, jSG2:jSG2 + 1],
                                        SSC[0:1, jT0:jT0 + 1],
                                        float(2.0 ** -16 / QMAX), 1e-12,
                                        op0=ALU.mult, op1=ALU.max)
                        v.reciprocal(SSC[0:1, jRG2:jRG2 + 1],
                                     SSC[0:1, jSG2:jSG2 + 1])
                        v.tensor_tensor(SSC[0:1, jT0:jT0 + 1],
                                        SSC[0:1, jSG:jSG + 1],
                                        SSC[0:1, jRG2:jRG2 + 1], op=ALU.mult)
                        v.tensor_scalar(SSC[0:1, jK2:jK2 + 1],
                                               SSC[0:1, jT0:jT0 + 1],
                                               float(2.0 ** -16), None,
                                               op0=ALU.mult)
                    VEC(blk, v_s2)
                    bcast_scalar(blk, jK2, iK2)

        with contextlib.ExitStack() as es3:
            US = es3.enter_context(nc.sbuf_tensor(un("US_f"), [128, 2, T], F32))
            GI = es3.enter_context(nc.sbuf_tensor(un("GI3_f"), [128, 2, T], I16))
            UI = es3.enter_context(nc.sbuf_tensor(un("UI3_f"), [128, 2, T], I16))
            G2 = es3.enter_context(nc.sbuf_tensor(un("G23_f"), [128, 2, T], I16))
            SG3 = es3.enter_context(nc.sbuf_tensor(un("SG3_f"), [128, 2, T], F32))
            with nc.Block() as blk:
                a_done = []; b_done = []; sg_done = []; red_done = []
                gi_gates = [dsGI.n(i) for i in range(3)]
                ev_gates = [dsEV.n(0), dsEV.n(1)]
                for ot in range(OT):
                    pool = False

                    def s_ld(sy, ot=ot):
                        if ot == 0:
                            for i in range(3):
                                sy.wait_ge(dsGI.h(i), gi_gates[i])
                            sy.wait_ge(dsEV.h(0), ev_gates[0])
                            sy.wait_ge(dsEV.h(1), ev_gates[1])
                        if ot >= 2:
                            sy.wait_ge(*b_done[ot - 2])
                            sy.wait_ge(aD.h, sg_done[ot - 2])
                        dsUS.issue(sy.dma_start(out=US[:, ot % 2, :],
                                                in_=U_d[ot]), ot % 2)
                        dsUS.issue(sy.dma_start(out=GI[:, ot % 2, :],
                                                in_=g16_d[ot]), ot % 2)
                    blk.sync(s_ld)

                    def v_a(v, ot=ot, mld=dsUS.n(ot % 2),
                            mui=dsUI.n(ot % 2)):
                        v.wait_ge(dsUS.h(ot % 2), mld)
                        if ot >= 2:
                            v.wait_ge(*b_done[ot - 2])
                            v.wait_ge(vD.h, red_done[ot - 2])
                            v.wait_ge(dsUI.h(ot % 2), mui)
                        v.tensor_scalar(SG3[:, ot % 2, :], US[:, ot % 2, :],
                                        COLS[:, iRUP:iRUP + 1], MAGIC,
                                        op0=ALU.mult, op1=ALU.add)
                        v.tensor_scalar(UI[:, ot % 2, :],
                                               SG3[:, ot % 2, :],
                                               MAGIC, None, op0=ALU.subtract)
                    EW(blk, v_a, pool)
                    a_done.append(ewm(pool))

                    def g_su(g, ot=ot, mva=a_done[ot]):
                        g.wait_ge(*mva)
                        dsUI.issue(g.dma_start(out=u16_d[ot],
                                               in_=UI[:, ot % 2, :]), ot % 2)
                    blk.scalar(g_su)

                    def s_sig(sc, ot=ot, mva=a_done[ot]):
                        sc.wait_ge(*mva)
                        sc.activation(SG3[:, ot % 2, :],
                                             GI[:, ot % 2, :], AF.Sigmoid,
                                             scale=COLS[:, iSG:iSG + 1])
                    SCA(blk, s_sig)
                    sg_done.append(aD.n)

                    g2m = {}

                    def v_b(v, ot=ot, ma=aD.n, g2m=g2m, pool=pool,
                            mg2=dsG2.n(ot % 2)):
                        v.wait_ge(aD.h, ma)
                        if ot >= 2:
                            v.wait_ge(dsG2.h(ot % 2), mg2)
                        v.tensor_scalar(SG3[:, ot % 2, :], SG3[:, ot % 2, :],
                                        65536.0, MAGIC, op0=ALU.mult,
                                        op1=ALU.add)
                        v.tensor_scalar(SG3[:, ot % 2, :], SG3[:, ot % 2, :],
                                        MAGIC, 65535.0, op0=ALU.subtract,
                                        op1=ALU.min)
                        v.tensor_tensor(SG3[:, ot % 2, :], GI[:, ot % 2, :],
                                        SG3[:, ot % 2, :], op=ALU.mult)
                        v.tensor_scalar(SG3[:, ot % 2, :], SG3[:, ot % 2, :],
                                        COLS[:, iK2:iK2 + 1], MAGIC,
                                        op0=ALU.mult, op1=ALU.add)
                        v.tensor_scalar(G2[:, ot % 2, :],
                                               SG3[:, ot % 2, :],
                                               MAGIC, None, op0=ALU.subtract)
                        g2m[0] = (gD.h, gD.n) if pool else (vD.h, vD.n)
                        v.tensor_tensor(SG3[:, ot % 2, :], G2[:, ot % 2, :],
                                        UI[:, ot % 2, :], op=ALU.mult)
                    EW(blk, v_b, pool)
                    b_done.append(ewm(pool))

                    def v_red(v, ot=ot, mfb=b_done[ot]):
                        v.wait_ge(*mfb)
                        v.tensor_reduce(MXC[:, ot:ot + 1],
                                               SG3[:, ot % 2, :],
                                               op=ALU.max, axis=AX.X,
                                               apply_absolute_value=True)
                    VEC(blk, v_red)
                    red_done.append(vD.n)

                    def g_sg(g, ot=ot, mg=g2m[0]):
                        g.wait_ge(*mg)
                        dsG2.issue(g.dma_start(out=g216_d[ot],
                                               in_=G2[:, ot % 2, :]), ot % 2)
                    blk.scalar(g_sg)

            with nc.Block() as blk:
                final_max(blk, OT, jMP2)

                def v_so(v):
                    v.wait_ge(vD.h, vD.n)
                    v.tensor_tensor(SSC[0:1, jT0:jT0 + 1],
                                    SSC[0:1, jMP2:jMP2 + 1],
                                    SSC[0:1, jSG2:jSG2 + 1], op=ALU.mult)
                    v.tensor_tensor(SSC[0:1, jT0:jT0 + 1],
                                    SSC[0:1, jT0:jT0 + 1],
                                    SSC[0:1, jSUP:jSUP + 1], op=ALU.mult)
                    v.tensor_scalar(SSC[0:1, jSO:jSO + 1],
                                    SSC[0:1, jT0:jT0 + 1],
                                    1.0 / QMAX, 1e-12, op0=ALU.mult,
                                    op1=ALU.max)
                    v.reciprocal(SSC[0:1, jRO:jRO + 1], SSC[0:1, jSO:jSO + 1])
                    v.tensor_tensor(SSC[0:1, jT1:jT1 + 1],
                                    SSC[0:1, jSG2:jSG2 + 1],
                                    SSC[0:1, jSUP:jSUP + 1], op=ALU.mult)
                    v.tensor_tensor(SSC[0:1, jK3:jK3 + 1],
                                    SSC[0:1, jT1:jT1 + 1],
                                    SSC[0:1, jRO:jRO + 1], op=ALU.mult)
                    v.tensor_scalar(SSC[0:1, jT0:jT0 + 1],
                                           SSC[0:1, jSO:jSO + 1], 0.5, None,
                                           op0=ALU.mult)
                VEC(blk, v_so)
                bcast_scalar(blk, jK3, iK3)
                bcast_scalar(blk, jT0, iOS)

        with contextlib.ExitStack() as es4:
            OB = es4.enter_context(nc.sbuf_tensor(un("OB_f"), [128, OT, 1024],
                                                  BF16))
            G2 = es4.enter_context(nc.sbuf_tensor(un("G24_f"), [128, 2, 1024],
                                                  I16))
            UI = es4.enter_context(nc.sbuf_tensor(un("UI4_f"), [128, 2, 1024],
                                                  I16))
            PR = es4.enter_context(nc.sbuf_tensor(un("PR4_f"), [128, 2, 1024],
                                                  F32))
            with nc.Block() as blk:
                mp_wt = pD.n

                mg_wd = qG.n

                def s_wd(sy, mp_wt=mp_wt, mg_wd=mg_wd):
                    sy.wait_ge(pD.h, mp_wt)
                    sy.wait_ge(qS.h, qS.n)
                    sy.wait_ge(qG.h, mg_wd)
                    for o in range(OT):
                        qS.inc(sy.dma_start_transpose(
                            WT[:, o, :], wd_q[:, o * 128:(o + 1) * 128]), 16)
                blk.sync(s_wd)
                m_wd = qS.n

                g4_gates = [dsUI.n(0), dsUI.n(1), dsG2.n(0), dsG2.n(1)]
                ob_done = []
                for th in range(2):
                    th_pgate = pD.n
                    for ot in range(OT):
                        gk2 = th * OT + ot
                        pool = False

                        def s_l4(sy, th=th, ot=ot, gk2=gk2):
                            if th == 0 and ot == 0:
                                sy.wait_ge(dsUI.h(0), g4_gates[0])
                                sy.wait_ge(dsUI.h(1), g4_gates[1])
                                sy.wait_ge(dsG2.h(0), g4_gates[2])
                                sy.wait_ge(dsG2.h(1), g4_gates[3])
                            if gk2 >= 2:
                                sy.wait_ge(*ob_done[gk2 - 2])
                            sl = slice(th * 1024, (th + 1) * 1024)
                            dsL4.issue(sy.dma_start(out=G2[:, ot % 2, :],
                                                    in_=g216_d[ot, :, sl]),
                                       ot % 2)
                            dsL4.issue(sy.dma_start(out=UI[:, ot % 2, :],
                                                    in_=u16_d[ot, :, sl]),
                                       ot % 2)
                        blk.sync(s_l4)

                        def v_ob(v, ot=ot, mld=dsL4.n(ot % 2),
                                 th_pgate=th_pgate):
                            v.wait_ge(dsL4.h(ot % 2), mld)
                            v.wait_ge(pD.h, th_pgate)
                            v.tensor_tensor(PR[:, ot % 2, :], G2[:, ot % 2, :],
                                            UI[:, ot % 2, :], op=ALU.mult)
                            v.tensor_scalar(PR[:, ot % 2, :], PR[:, ot % 2, :],
                                            COLS[:, iK3:iK3 + 1], MAGIC,
                                            op0=ALU.mult, op1=ALU.add)
                            v.tensor_scalar(
                                OB[:, ot, :], PR[:, ot % 2, :],
                                MAGIC, COLS[:, iOS:iOS + 1],
                                op0=ALU.subtract, op1=ALU.mult)
                        EW(blk, v_ob, pool)
                        ob_done.append(ewm(pool))
                    m_ob = vD.n
                    m_obg = gD.n

                    ev_done = {}
                    for dt in range(CT):
                        for qq in range(2):
                            gk = dt * 2 + qq
                            def t_dn(te, th=th, dt=dt, qq=qq, gk=gk,
                                     m_ob=m_ob, m_obg=m_obg):
                                if gk == 0:
                                    te.wait_ge(qS.h, m_wd)
                                    te.wait_ge(vD.h, m_ob)
                                    te.wait_ge(gD.h, m_obg)
                                if gk >= 2:
                                    te.wait_ge(vD.h, ev_done[gk - 2])
                                tsl = slice(qq * 512, (qq + 1) * 512)
                                for o in range(OT):
                                    mm = te.matmul(
                                        PB[gk % 2][:, :],
                                        WT[:, o, dt * 128:(dt + 1) * 128],
                                        OB[:, o, tsl],
                                        start=(o == 0), stop=(o == OT - 1))
                                pD.inc(mm)
                            blk.tensor(t_dn)
                            mp = pD.n

                            def v_rs(v, th=th, dt=dt, qq=qq, gk=gk, mp=mp):
                                v.wait_ge(pD.h, mp)
                                sl = slice(th * 1024 + qq * 512,
                                           th * 1024 + (qq + 1) * 512)
                                v.tensor_tensor(
                                    XH[:, dt, sl], XH[:, dt, sl],
                                    PB[gk % 2][:, :], op=ALU.add)
                            VEC(blk, v_rs)
                            ev_done[gk] = vD.n
        esF.close()

    with es:
        esXA = contextlib.ExitStack()
        XA = esXA.enter_context(nc.sbuf_tensor("XA", [128, CT, T], F32))
        esW = contextlib.ExitStack()
        WRAW = esW.enter_context(nc.sbuf_tensor("WRAW", [128, 2, I3], F32))
        WQB = esW.enter_context(nc.sbuf_tensor("WQB", [128, 2, I3], BF16))
        WST = esW.enter_context(nc.sbuf_tensor("WST", [128, 2, I3], F32))
        SCM = esW.enter_context(nc.sbuf_tensor("SCM", [128, 2, 96], F32))
        SCR = esW.enter_context(nc.sbuf_tensor("SCR", [128, 2, 96], F32))

        with nc.Block() as blk:
            def s_0(sy):
                qS.inc(sy.dma_start(out=IDN[:], in_=ident_i), 16)
                qS.inc(sy.dma_start(out=MSKC[:], in_=maskS_i), 16)
                qS.inc(sy.dma_start(out=XA[:],
                                    in_=x_own.rearrange("c p t -> p c t")), 16)
            blk.sync(s_0)

            def v_0(v):
                v.memset(ones_col[:], 1.0)
                v.memset(ones_row[:], 1.0)
                v.memset(SSC[:], 0.0)
                v.memset(SSC[0:1, jEPS:jEPS + 1], EPS)
                v.memset(MXC[:], 0.0)
            VEC(blk, v_0)

        for wsrc, wdst, osz, rl in ((w_gate, wg_q, OT, D), (w_up, wu_q, OT, D),
                                    (w_down, wd_q, CT, I3)):
            nb = rl // 32
            with nc.Block() as blk:
                dq_done = []
                mv_blk = vD.n
                mg_blk = gD.n
                wsv = wsrc.rearrange("(k p) i -> k p i", p=128)
                wdv = wdst.rearrange("(k p) i -> k p i", p=128)
                for k in range(osz):
                    kp = k % 2
                    pool = False

                    def s_lw(sy, k=k, rl=rl, kp=kp):
                        if k >= 2:
                            sy.wait_ge(*dq_done[k - 2])
                        else:
                            sy.wait_ge(vD.h, mv_blk)
                            sy.wait_ge(gD.h, mg_blk)
                        dsWR.issue(sy.dma_start(out=WRAW[:, kp, 0:rl],
                                                in_=wsv[k]), kp)
                    blk.sync(s_lw)

                    def v_sc(v, k=k, rl=rl, nb=nb, kp=kp,
                             mwr=dsWR.n(kp)):
                        v.wait_ge(dsWR.h(kp), mwr)
                        wv = WRAW[:, kp, 0:rl].rearrange(
                            "p (b j) -> p b j", j=32)
                        v.tensor_reduce(SCM[:, kp, 0:nb], wv, op=ALU.max,
                                        axis=AX.X, apply_absolute_value=True)
                        v.tensor_scalar(SCM[:, kp, 0:nb], SCM[:, kp, 0:nb],
                                        1.0 / 7.0, 1e-12, op0=ALU.mult,
                                        op1=ALU.max)
                        v.reciprocal(SCR[:, kp, 0:nb], SCM[:, kp, 0:nb])
                    VEC(blk, v_sc)
                    mv_sc = vD.n

                    def v_dq(v, k=k, rl=rl, nb=nb, kp=kp, mv_sc=mv_sc,
                             mwq=dsWQ.n(kp)):
                        v.wait_ge(vD.h, mv_sc)
                        v.wait_ge(dsWQ.h(kp), mwq)
                        wv = WRAW[:, kp, 0:rl].rearrange(
                            "p (b j) -> p b j", j=32)
                        rb = SCR[:, kp, 0:nb].unsqueeze(2).broadcast_to(
                            [128, nb, 32])
                        sb = SCM[:, kp, 0:nb].unsqueeze(2).broadcast_to(
                            [128, nb, 32])
                        sv = WST[:, kp, 0:rl].rearrange(
                            "p (b j) -> p b j", j=32)
                        v.tensor_tensor(sv, wv, rb, op=ALU.mult)
                        v.tensor_scalar(WST[:, kp, 0:rl], WST[:, kp, 0:rl],
                                        MAGIC, MAGIC,
                                        op0=ALU.add, op1=ALU.subtract)
                        qv = WQB[:, kp, 0:rl].rearrange(
                            "p (b j) -> p b j", j=32)
                        v.tensor_tensor(qv, sv, sb, op=ALU.mult)
                    EW(blk, v_dq, pool)
                    dq_done.append(ewm(pool))

                    def g_st(g, k=k, rl=rl, kp=kp, mdq=dq_done[k]):
                        g.wait_ge(*mdq)
                        dsWQ.issue(g.dma_start(out=wdv[k],
                                               in_=WQB[:, kp, 0:rl]),
                                   kp)
                    blk.scalar(g_st)

        esW.close()

        emit_ffn(XA)

        if STAGE == 1:
            if os.environ.get("KTAP"):
                dbg_ssc = nc.dram_tensor("dbg_ssc", [1, 64], F32,
                                         kind="ExternalOutput").ap()
                dbg_cols = nc.dram_tensor("dbg_cols", [128, 16], F32,
                                          kind="ExternalOutput").ap()
                dbg_g = nc.dram_tensor("dbg_g", [128, T], F32,
                                       kind="ExternalOutput").ap()
                dbg_u = nc.dram_tensor("dbg_u", [128, T], F32,
                                       kind="ExternalOutput").ap()
                dbg_g16 = nc.dram_tensor("dbg_g16", [128, T], I16,
                                         kind="ExternalOutput").ap()
                dbg_g216 = nc.dram_tensor("dbg_g216", [128, T], I16,
                                          kind="ExternalOutput").ap()
                dbg_u16 = nc.dram_tensor("dbg_u16", [128, T], I16,
                                         kind="ExternalOutput").ap()
                dbg_wgq = nc.dram_tensor("dbg_wgq", [256, D], BF16,
                                         kind="ExternalOutput").ap()
                with nc.Block() as blk:
                    def s_tap(sy, mv=vD.n):
                        sy.wait_ge(vD.h, mv)
                        qS.inc(sy.dma_start(out=dbg_wgq, in_=wg_q[0:256, :]), 16)
                        qS.inc(sy.dma_start(out=dbg_ssc, in_=SSC[:]), 16)
                        qS.inc(sy.dma_start(out=dbg_cols, in_=COLS[:]), 16)
                        qS.inc(sy.dma_start(out=dbg_g, in_=G_d[0]), 16)
                        qS.inc(sy.dma_start(out=dbg_u, in_=U_d[0]), 16)
                        qS.inc(sy.dma_start(out=dbg_g16, in_=g16_d[0]), 16)
                        qS.inc(sy.dma_start(out=dbg_g216, in_=g216_d[0]), 16)
                        qS.inc(sy.dma_start(out=dbg_u16, in_=u16_d[0]), 16)
                    blk.sync(s_tap)
            with nc.Block() as blk:
                def s_out(sy, mv=vD.n):
                    sy.wait_ge(vD.h, mv)
                    qS.inc(sy.dma_start(
                        out=y.rearrange("c p t -> p c t"), in_=XA[:]), 16)
                    sy.wait_ge(qS.h, qS.n)
                blk.sync(s_out)
            esXA.close()
            return nc

        xaow_d = dsc("xaow_d", [CT, 128, T], BF16)
        with contextlib.ExitStack() as esL2:
            XAOW = esL2.enter_context(nc.sbuf_tensor("XAOW", [128, CT, T],
                                                     BF16))
            SQ2 = esL2.enter_context(nc.sbuf_tensor("SQ2", [128, 2, T], F32))
            emit_ln(XA, SQ2, False, XAOW)
            with nc.Block() as blk:
                def s_sp(sy, mv=vD.n):
                    sy.wait_ge(vD.h, mv)
                    qS.inc(sy.dma_start(out=x1_d.rearrange("c p t -> p c t"),
                                        in_=XA[:]), 16)
                    qS.inc(sy.dma_start(out=xaow_d.rearrange("c p t -> p c t"),
                                        in_=XAOW[:]), 16)
                blk.sync(s_sp)
        esXA.close()

        with contextlib.ExitStack() as esB:
            XB = esB.enter_context(nc.sbuf_tensor("XB", [128, CT, T], F32))
            with nc.Block() as blk:
                def s_lb(sy, mq_sp=qS.n):
                    sy.wait_ge(qS.h, mq_sp)
                    qS.inc(sy.dma_start(
                        out=XB[:], in_=x_oth.rearrange("c p t -> p c t")), 16)
                blk.sync(s_lb)
            emit_ffn(XB)
            XAB = esB.enter_context(nc.sbuf_tensor("XAB", [128, CT, T], BF16))
            SQ3 = esB.enter_context(nc.sbuf_tensor("SQ3", [128, 2, T], F32))
            emit_ln(XB, SQ3, False, XAB)

            WK = esB.enter_context(nc.sbuf_tensor("WK", [128, CT, D], BF16))
            WV = esB.enter_context(nc.sbuf_tensor("WV", [128, CT, D], BF16))
            KTMP = esB.enter_context(nc.sbuf_tensor("KTMP", [128, D], BF16))
            ROT = esB.enter_context(nc.sbuf_tensor("ROT", [128, D], BF16))
            VTMP = esB.enter_context(nc.sbuf_tensor("VTMP", [128, D], BF16))
            T0 = esB.enter_context(nc.sbuf_tensor("T0", [128, D], F32))
            T1 = esB.enter_context(nc.sbuf_tensor("T1", [128, D], F32))
            KS = esB.enter_context(nc.sbuf_tensor("KS", [128, D], BF16))
            COSB = esB.enter_context(nc.sbuf_tensor("COSB", [128, NCH, DH],
                                                    F32))
            SINB = esB.enter_context(nc.sbuf_tensor("SINB", [128, NCH, DH],
                                                    F32))
            GCT = esB.enter_context(nc.sbuf_tensor("GCT", [128, 512], F32))
            KSC = esB.enter_context(nc.sbuf_tensor("KSC", [128, D], F32))
            with nc.Block() as blk:
                def s_lw2(sy):
                    qS.inc(sy.dma_start(
                        out=COSB[:],
                        in_=cos_oth.rearrange("(n p) d -> p n d", p=128)), 16)
                    qS.inc(sy.dma_start(
                        out=SINB[:],
                        in_=sinS_oth.rearrange("(n p) d -> p n d", p=128)), 16)
                    qS.inc(sy.dma_start(out=GCT[:], in_=gctab_i), 16)
                    qS.inc(sy.dma_start(out=KSC[:], in_=kscT_i), 16)
                blk.sync(s_lw2)

                def g_lw2(g):
                    qG.inc(g.dma_start(
                        out=WK[:],
                        in_=wkT.rearrange("(c p) d -> p c d", p=128)), 16)
                    qG.inc(g.dma_start(
                        out=WV[:],
                        in_=wvT.rearrange("(c p) d -> p c d", p=128)), 16)
                blk.gpsimd(g_lw2)
                m_w2 = qG.n
                m_c2 = qS.n
                mv_xab = vD.n

                def v_z(v):
                    v.memset(SST[:], 0.0)
                blk.vector(v_z)

                ev_k = []
                for tt in range(NCH):
                    def t_kv(te, tt=tt):
                        if tt == 0:
                            te.wait_ge(qG.h, m_w2)
                            te.wait_ge(qS.h, m_c2)
                            te.wait_ge(vD.h, mv_xab)
                        else:
                            te.wait_ge(aD.h, ev_k[tt - 1][0])
                            te.wait_ge(vD.h, ev_k[tt - 1][1])
                        for dq in range(2):
                            for c in range(CT):
                                mmk = te.matmul(
                                    PB[dq][:, :],
                                    XAB[:, c, tt * 128:(tt + 1) * 128],
                                    WK[:, c, dq * 512:(dq + 1) * 512],
                                    start=(c == 0), stop=(c == CT - 1))
                        for dq in range(2):
                            for c in range(CT):
                                mmv = te.matmul(
                                    PB[2 + dq][:, :],
                                    XAB[:, c, tt * 128:(tt + 1) * 128],
                                    WV[:, c, dq * 512:(dq + 1) * 512],
                                    start=(c == 0), stop=(c == CT - 1))
                        pD.inc(mmv)
                    blk.tensor(t_kv)
                    mpk = pD.n

                    def s_ev(sc, tt=tt, mpk=mpk):
                        sc.wait_ge(pD.h, mpk)
                        sc.copy(KTMP[:, 0:512], PB[0][:, :])
                        sc.copy(KTMP[:, 512:1024], PB[1][:, :])
                        sc.copy(VTMP[:, 0:512], PB[2][:, :])
                        sc.copy(VTMP[:, 512:1024], PB[3][:, :])
                        kv = KTMP[:].rearrange("p (h g j) -> p h g j", g=2,
                                               j=32)
                        rv = ROT[:].rearrange("p (h g j) -> p h g j", g=2,
                                              j=32)
                        sc.copy(rv[:, :, 0, :], kv[:, :, 1, :])
                        sc.copy(rv[:, :, 1, :], kv[:, :, 0, :])
                    SCA(blk, s_ev)
                    mak = aD.n

                    def v_rp(v, tt=tt, mak=mak):
                        v.wait_ge(aD.h, mak)
                        cb = COSB[:, tt, :].unsqueeze(1).broadcast_to(
                            [128, NH, DH])
                        sb = SINB[:, tt, :].unsqueeze(1).broadcast_to(
                            [128, NH, DH])
                        kv = KTMP[:].rearrange("p (h d) -> p h d", d=DH)
                        rv = ROT[:].rearrange("p (h d) -> p h d", d=DH)
                        v.tensor_tensor(T0[:].rearrange("p (h d) -> p h d",
                                                        d=DH), kv, cb,
                                        op=ALU.mult)
                        v.tensor_tensor(T1[:].rearrange("p (h d) -> p h d",
                                                        d=DH), rv, sb,
                                        op=ALU.mult)
                        v.tensor_tensor(KTMP[:], T0[:], T1[:], op=ALU.add)
                        v.tensor_tensor(KS[:], KTMP[:], KSC[:], op=ALU.mult)
                    VEC(blk, v_rp)
                    mvk = vD.n

                    def t_st(te, tt=tt, mvk=mvk):
                        te.wait_ge(vD.h, mvk)
                        for h in range(NH):
                            base = 64 * (h % 2)
                            col = (h // 2) * DH
                            mm = te.matmul(
                                PB[4][base:base + 64, col:col + DH],
                                KS[:, h * DH:(h + 1) * DH],
                                VTMP[:, h * DH:(h + 1) * DH],
                                start=True, stop=True)
                        pD.inc(mm)
                    blk.tensor(t_st)
                    mps = pD.n

                    def v_su(v, tt=tt, mps=mps):
                        v.wait_ge(pD.h, mps)
                        v.tensor_tensor(SST[:], SST[:], GCT[:], op=ALU.mult)
                        v.tensor_tensor(SST[:], SST[:], PB[4][:, :],
                                        op=ALU.add)
                    VEC(blk, v_su)
                    ev_k.append((aD.n, vD.n))

            with nc.Block() as blk:
                def v_mk(v):
                    v.wait_ge(vD.h, vD.n)
                    v.tensor_scalar(SST[:], SST[:], MSKC[:, 0:1], None,
                                    op0=ALU.mult)
                VEC(blk, v_mk)

                def s_sb(sc, mv=vD.n):
                    sc.wait_ge(vD.h, mv)
                    sc.copy(SSB[:], SST[:])
                SCA(blk, s_sb)

            if STAGE < 1.7:
                with nc.Block() as blk:
                    def s_o15(sy, mv=vD.n, ma=aD.n):
                        sy.wait_ge(vD.h, mv)
                        sy.wait_ge(aD.h, ma)
                        qS.inc(sy.dma_start(
                            out=y.rearrange("c p t -> p c t"), in_=XB[:]), 16)
                        sy.wait_ge(qS.h, qS.n)
                    blk.sync(s_o15)
                return nc

        esAT = contextlib.ExitStack()
        KT = esAT.enter_context(nc.sbuf_tensor("KT", [128, NCH, D], BF16))
        VT = esAT.enter_context(nc.sbuf_tensor("VT", [128, NCH, D], BF16))
        GSIL = esAT.enter_context(nc.sbuf_tensor("GSIL", [128, CT, T], BF16))
        with contextlib.ExitStack() as esA2:
            XAP = esA2.enter_context(nc.sbuf_tensor("XAP", [128, CT, T],
                                                    BF16))
            WB1 = esA2.enter_context(nc.sbuf_tensor("WB1", [128, CT, D],
                                                    BF16))
            KTMP = esA2.enter_context(nc.sbuf_tensor("KTMP2", [128, D], BF16))
            ROT = esA2.enter_context(nc.sbuf_tensor("ROT2", [128, D], BF16))
            T0 = esA2.enter_context(nc.sbuf_tensor("T02", [128, D], F32))
            T1 = esA2.enter_context(nc.sbuf_tensor("T12", [128, D], F32))
            COSB = esA2.enter_context(nc.sbuf_tensor("COSB2", [128, NCH, DH],
                                                     F32))
            SINB = esA2.enter_context(nc.sbuf_tensor("SINB2", [128, NCH, DH],
                                                     F32))

            with nc.Block() as blk:
                def s_lo(sy):
                    qS.inc(sy.dma_start(
                        out=XAP[:], in_=xaow_d.rearrange("c p t -> p c t")),
                        16)
                    qS.inc(sy.dma_start(
                        out=COSB[:],
                        in_=cos_own.rearrange("(n p) d -> p n d", p=128)), 16)
                    qS.inc(sy.dma_start(
                        out=SINB[:],
                        in_=sinS_own.rearrange("(n p) d -> p n d", p=128)), 16)
                blk.sync(s_lo)

            def proj_tm(wsrc, dst, rope):
                with nc.Block() as blk:
                    mp_prev = pD.n

                    def g_w(g, wsrc=wsrc, mp_prev=mp_prev):
                        g.wait_ge(pD.h, mp_prev)
                        qG.inc(g.dma_start(
                            out=WB1[:],
                            in_=wsrc.rearrange("(c p) d -> p c d", p=128)),
                            16)
                    blk.gpsimd(g_w)
                    m_w = qG.n
                    m_ld = qS.n
                    ev_m = []
                    for tt in range(NCH):
                        def t_p(te, tt=tt):
                            if tt == 0:
                                te.wait_ge(qG.h, m_w)
                                te.wait_ge(qS.h, m_ld)
                            if tt >= 2:
                                te.wait_ge(vD.h, ev_m[tt - 2][1])
                                te.wait_ge(aD.h, ev_m[tt - 2][0])
                            for dq in range(2):
                                for c in range(CT):
                                    mm = te.matmul(
                                        PB[2 * (tt % 2) + dq][:, :],
                                        XAP[:, c, tt * 128:(tt + 1) * 128],
                                        WB1[:, c, dq * 512:(dq + 1) * 512],
                                        start=(c == 0), stop=(c == CT - 1))
                            pD.inc(mm)
                        blk.tensor(t_p)
                        mpk = pD.n

                        def s_e(sc, tt=tt, mpk=mpk, rope=rope, dst=dst,
                                m_ld=m_ld):
                            sc.wait_ge(pD.h, mpk)
                            if tt == 0:
                                sc.wait_ge(qS.h, m_ld)
                            if rope and tt >= 1:
                                sc.wait_ge(vD.h, ev_m[tt - 1][1])
                            tgt = KTMP if rope else dst
                            off = 0 if rope else tt
                            if rope:
                                sc.copy(KTMP[:, 0:512],
                                        PB[2 * (tt % 2)][:, :])
                                sc.copy(KTMP[:, 512:1024],
                                        PB[2 * (tt % 2) + 1][:, :])
                                kv = KTMP[:].rearrange(
                                    "p (h g j) -> p h g j", g=2, j=32)
                                rv = ROT[:].rearrange(
                                    "p (h g j) -> p h g j", g=2, j=32)
                                sc.copy(rv[:, :, 0, :], kv[:, :, 1, :])
                                sc.copy(rv[:, :, 1, :], kv[:, :, 0, :])
                            else:
                                sc.copy(dst[:, tt, 0:512],
                                        PB[2 * (tt % 2)][:, :])
                                sc.copy(dst[:, tt, 512:1024],
                                        PB[2 * (tt % 2) + 1][:, :])
                        SCA(blk, s_e)
                        mak = aD.n

                        def v_r(v, tt=tt, mak=mak, rope=rope, dst=dst):
                            v.wait_ge(aD.h, mak)
                            if rope:
                                cb = COSB[:, tt, :].unsqueeze(1).broadcast_to(
                                    [128, NH, DH])
                                sb = SINB[:, tt, :].unsqueeze(1).broadcast_to(
                                    [128, NH, DH])
                                kv = KTMP[:].rearrange("p (h d) -> p h d",
                                                       d=DH)
                                rv = ROT[:].rearrange("p (h d) -> p h d",
                                                      d=DH)
                                v.tensor_tensor(
                                    T0[:].rearrange("p (h d) -> p h d", d=DH),
                                    kv, cb, op=ALU.mult)
                                v.tensor_tensor(
                                    T1[:].rearrange("p (h d) -> p h d", d=DH),
                                    rv, sb, op=ALU.mult)
                                v.tensor_tensor(dst[:, tt, :], T0[:], T1[:],
                                                op=ALU.add)
                        VEC(blk, v_r)
                        ev_m.append((aD.n, vD.n))

            proj_tm(wqT, KT, True)
            with nc.Block() as blk:
                def s_bq(sy, mv=vD.n):
                    sy.wait_ge(vD.h, mv)
                    qS.inc(sy.dma_start(
                        out=trb_d.rearrange("(n p) d -> p n d", p=128),
                        in_=KT[:]), 16)
                blk.sync(s_bq)
            proj_tm(wkT, KT, True)
            proj_tm(wvT, VT, False)

            with nc.Block() as blk:
                mp_prev = pD.n

                def g_wg(g, mp_prev=mp_prev):
                    g.wait_ge(pD.h, mp_prev)
                    qG.inc(g.dma_start(
                        out=WB1[:],
                        in_=wgTT.rearrange("(c p) d -> p c d", p=128)), 16)
                blk.gpsimd(g_wg)
                m_w = qG.n
                ev_g = []
                gk = 0
                for dg in range(CT):
                    for q in range(4):
                        def t_gp(te, dg=dg, q=q, gk=gk):
                            if gk == 0:
                                te.wait_ge(qG.h, m_w)
                            if gk >= 2:
                                te.wait_ge(aD.h, ev_g[gk - 2])
                            for c in range(CT):
                                mm = te.matmul(
                                    PB[4 + gk % 2][:, :],
                                    WB1[:, c, dg * 128:(dg + 1) * 128],
                                    XAP[:, c, q * 512:(q + 1) * 512],
                                    start=(c == 0), stop=(c == CT - 1))
                            pD.inc(mm)
                        blk.tensor(t_gp)
                        mpg = pD.n

                        def s_gs(sc, dg=dg, q=q, gk=gk, mpg=mpg):
                            sc.wait_ge(pD.h, mpg)
                            sc.activation(
                                GSIL[:, dg, q * 512:(q + 1) * 512],
                                PB[4 + gk % 2][:, :], AF.Silu)
                        SCA(blk, s_gs)
                        ev_g.append(aD.n)
                        gk += 1

            with nc.Block() as blk:
                def s_bn(sy, mv=vD.n, ma=aD.n):
                    sy.wait_ge(vD.h, mv)
                    sy.wait_ge(aD.h, ma)
                    qS.inc(sy.dma_start(
                        out=trb2_d.rearrange("(n p) d -> p n d", p=128),
                        in_=KT[:]), 16)
                blk.sync(s_bn)

            if STAGE < 1.8:
                with nc.Block() as blk:
                    def g_o17(g, mv=vD.n, ma=aD.n):
                        g.wait_ge(vD.h, mv)
                        g.wait_ge(aD.h, ma)
                        qG.inc(g.dma_start(
                            out=y.rearrange("c p t -> p c t"),
                            in_=GSIL[:]), 16)
                        g.wait_ge(qG.h, qG.n)
                    blk.gpsimd(g_o17)
                esA2.close()
                esAT.close()
                return nc

        if STAGE == 1.85:
            with contextlib.ExitStack() as esT:
                QF85 = esT.enter_context(
                    nc.sbuf_tensor("QF85", [128, CT, T], BF16))
                KF85 = esT.enter_context(
                    nc.sbuf_tensor("KF85", [128, CT, T], BF16))
                YF85 = esT.enter_context(
                    nc.sbuf_tensor("YF85", [128, T], F32))
                with nc.Block() as blk:
                    def s_tr85(sy):
                        sy.wait_ge(qS.h, qS.n)
                        for i in range(CT):
                            qS.inc(sy.dma_start_transpose(
                                QF85[:, i, :],
                                trb_d[:, i * 128:(i + 1) * 128]), 16)
                        for i in range(CT):
                            qS.inc(sy.dma_start_transpose(
                                KF85[:, i, :],
                                trb2_d[:, i * 128:(i + 1) * 128]), 16)
                    blk.sync(s_tr85)
                    m_tr85 = qS.n
                    srcs = [(QF85[:, 0, :], T), (QF85[:, 7, :], T),
                            (KF85[:, 0, :], T), (KF85[:, 7, :], T),
                            (SST[:], 512), (SSB[:], 512),
                            (KT[:, 0, :], D), (VT[:, 0, :], D)]
                    for i85, (src, w85) in enumerate(srcs):
                        def v_c85(v, i85=i85, src=src, w85=w85,
                                  m_tr85=m_tr85, mq=qS.n, ma=aD.n):
                            v.wait_ge(qS.h, m_tr85 if i85 == 0 else mq)
                            if i85 == 0:
                                v.wait_ge(aD.h, ma)
                            if w85 < T:
                                v.memset(YF85[:], 0.0)
                            v.tensor_copy(YF85[:, 0:w85], src)
                        VEC(blk, v_c85)
                        mv85 = vD.n

                        def s_s85(sy, i85=i85, mv85=mv85):
                            sy.wait_ge(vD.h, mv85)
                            qS.inc(sy.dma_start(out=y[i85], in_=YF85[:]), 16)
                            if i85 == 7:
                                sy.wait_ge(qS.h, qS.n)
                        blk.sync(s_s85)
            esAT.close()
            return nc

        with contextlib.ExitStack() as esA3:
            QF = esA3.enter_context(nc.sbuf_tensor("QF", [128, CT, T], BF16))
            KF = esA3.enter_context(nc.sbuf_tensor("KF", [128, CT, T], BF16))
            DCT = esA3.enter_context(nc.sbuf_tensor("DCT", [128, NH * CK],
                                                    F32))
            QSC = esA3.enter_context(nc.sbuf_tensor("QSC", [128, CT * CK],
                                                    F32))
            KSC2 = esA3.enter_context(nc.sbuf_tensor("KSC2", [128, D], F32))
            GCT2 = esA3.enter_context(nc.sbuf_tensor("GCT2", [128, 512], F32))
            QS = esA3.enter_context(nc.sbuf_tensor("QS", [128, CT, CK], BF16))
            KS2 = esA3.enter_context(nc.sbuf_tensor("KS2", [128, D], BF16))
            ATS = esA3.enter_context(nc.sbuf_tensor("ATS", [128, 4, 512],
                                                    BF16))
            GTD = esA3.enter_context(nc.sbuf_tensor("GTD", [128, 2, 512],
                                                    BF16))

            with nc.Block() as blk:
                def s_tr(sy):
                    sy.wait_ge(qS.h, qS.n)
                    for i in range(CT):
                        qS.inc(sy.dma_start_transpose(
                            QF[:, i, :], trb_d[:, i * 128:(i + 1) * 128]), 16)
                    for i in range(CT):
                        qS.inc(sy.dma_start_transpose(
                            KF[:, i, :], trb2_d[:, i * 128:(i + 1) * 128]),
                            16)
                    qS.inc(sy.dma_start(out=DCT[:], in_=decayT_i), 16)
                    qS.inc(sy.dma_start(out=QSC[:], in_=qscT_i), 16)
                    qS.inc(sy.dma_start(out=KSC2[:], in_=kscT_i), 16)
                    qS.inc(sy.dma_start(out=GCT2[:], in_=gctab_i), 16)
                blk.sync(s_tr)
                m_tr = qS.n

                prev = {(-1, k): 0 for k in
                        ("mm", "ats", "su", "ssb", "gated", "gst")}
                QSCv = QSC[:].rearrange("p (c k) -> p c k", c=CT)
                GSILv = GSIL
                NSC = int(os.environ.get("KNSC", str(NCH)))
                SCP = int(os.environ.get("KSCPART", "5"))
                for n_ in range(NSC):
                    def v_qk(v, n_=n_):
                        if n_ == 0:
                            v.wait_ge(qS.h, m_tr)
                        v.wait_ge(pD.h, prev[(n_ - 1, "mm")])
                        v.tensor_tensor(QS[:],
                                        QF[:, :, n_ * 128:(n_ + 1) * 128],
                                        QSCv, op=ALU.mult)
                        v.tensor_tensor(KS2[:], KT[:, n_, :], KSC2[:],
                                        op=ALU.mult)
                    VEC(blk, v_qk)
                    prev[(n_, "qk")] = vD.n

                    if SCP < 2:
                        prev[(n_, "ats")] = vD.n
                        prev[(n_, "mm")] = pD.n
                        prev[(n_, "su")] = vD.n
                        prev[(n_, "ssb")] = aD.n
                        prev[(n_, "gated")] = vD.n
                        continue
                    at_m = {}
                    NBG = int(os.environ.get("KNBG", "4"))
                    for bg in range(NBG):
                        def t_sc(te, n_=n_, bg=bg):
                            if bg == 0:
                                te.wait_ge(vD.h, prev[(n_, "qk")])
                                te.wait_ge(pD.h, prev[(n_ - 1, "mm")])
                            if bg >= 2:
                                te.wait_ge(vD.h, at_m[bg - 2])
                            for hh in range(4):
                                h = bg * 4 + hh
                                base = 64 * (h % 2)
                                for _ in range(4):
                                    te.nop()
                                mm = te.matmul(
                                    PB[bg % 2][:, hh * 128:(hh + 1) * 128],
                                    KF[base:base + 64, h // 2,
                                       n_ * 128:(n_ + 1) * 128],
                                    QF[base:base + 64, h // 2,
                                       n_ * 128:(n_ + 1) * 128],
                                    start=True, stop=True)
                            pD.inc(mm)
                        blk.tensor(t_sc)
                        msc = pD.n

                        def v_at(v, n_=n_, bg=bg, msc=msc):
                            v.wait_ge(pD.h, msc)
                            v.tensor_tensor(
                                ATS[:, bg, :], PB[bg % 2][:, :],
                                DCT[:, bg * 512:(bg + 1) * 512], op=ALU.mult)
                        VEC(blk, v_at)
                        at_m[bg] = vD.n
                    prev[(n_, "ats")] = vD.n
                    if NBG < 4:
                        prev[(n_, "mm")] = pD.n
                        prev[(n_, "su")] = vD.n
                        prev[(n_, "ssb")] = aD.n
                        prev[(n_, "gated")] = vD.n
                        continue

                    if SCP < 3:
                        prev[(n_, "mm")] = pD.n
                        prev[(n_, "su")] = vD.n
                        prev[(n_, "ssb")] = aD.n
                        prev[(n_, "gated")] = vD.n
                        continue

                    def t_ov(te, n_=n_):
                        te.wait_ge(vD.h, prev[(n_, "ats")])
                        te.wait_ge(vD.h, prev[(n_ - 1, "gated")])
                        te.wait_ge(vD.h, prev[(n_ - 1, "su")])
                        te.wait_ge(aD.h, prev[(n_ - 1, "ssb")])
                        for _ in range(4):
                            te.nop()
                        for h in range(NH):
                            base = 64 * (h % 2)
                            bank = 2 + (h // 8)
                            col = 128 * ((h // 2) % 4)
                            te.matmul(
                                PB[bank][base:base + 64, col:col + 128],
                                VT[:, n_, h * DH:(h + 1) * DH],
                                ATS[:, h // 4,
                                    (h % 4) * 128:(h % 4 + 1) * 128],
                                start=True, stop=False)
                            for _ in range(4 if base else 0):
                                te.nop()
                            te.matmul(
                                PB[bank][base:base + 64, col:col + 128],
                                SSB[base:base + 64,
                                    (h // 2) * DH:(h // 2) * DH + DH],
                                QS[base:base + 64, h // 2, :],
                                start=False, stop=True)
                            for _ in range(4 if base else 0):
                                te.nop()
                        for h in range(NH):
                            base = 64 * (h % 2)
                            col = (h // 2) * DH
                            mm = te.matmul(
                                PB[4][base:base + 64, col:col + DH],
                                KS2[:, h * DH:(h + 1) * DH],
                                VT[:, n_, h * DH:(h + 1) * DH],
                                start=True, stop=True)
                        pD.inc(mm)
                    blk.tensor(t_ov)
                    prev[(n_, "mm")] = pD.n

                    if SCP < 4:
                        prev[(n_, "su")] = vD.n
                        prev[(n_, "ssb")] = aD.n
                        prev[(n_, "gated")] = vD.n
                        continue

                    def v_su(v, n_=n_):
                        v.wait_ge(pD.h, prev[(n_, "mm")])
                        v.tensor_tensor(SST[:], SST[:], GCT2[:], op=ALU.mult)
                        v.tensor_tensor(SST[:], SST[:], PB[4][:, :],
                                        op=ALU.add)
                    VEC(blk, v_su)
                    prev[(n_, "su")] = vD.n

                    def s_sb2(sc, n_=n_):
                        sc.wait_ge(vD.h, prev[(n_, "su")])
                        sc.copy(SSB[:], SST[:])
                    SCA(blk, s_sb2)
                    prev[(n_, "ssb")] = aD.n

                    if SCP < 5:
                        prev[(n_, "gated")] = vD.n
                        continue

                    gt_m = {}
                    for b2 in range(2):
                        def v_gt(v, n_=n_, b2=b2, mgt=dsGT.n(b2)):
                            v.wait_ge(pD.h, prev[(n_, "mm")])
                            v.wait_ge(dsGT.h(b2), mgt)
                            v.tensor_tensor(
                                GTD[:, b2, :].rearrange("p (c k) -> p c k",
                                                        k=128),
                                PB[2 + b2][:, :].rearrange(
                                    "p (c k) -> p c k", k=128),
                                GSILv[:, 4 * b2:4 * b2 + 4,
                                      n_ * 128:(n_ + 1) * 128], op=ALU.mult)
                        VEC(blk, v_gt)
                        gt_m[b2] = vD.n
                    prev[(n_, "gated")] = vD.n

                    def g_gst(g, n_=n_, gt_m=dict(gt_m)):
                        for b2 in range(2):
                            g.wait_ge(vD.h, gt_m[b2])
                            dsGT.issue(g.dma_start(
                                out=gated_d[4 * b2:4 * b2 + 4, :,
                                            n_ * 128:(n_ + 1) * 128
                                            ].rearrange("c p k -> p c k"),
                                in_=GTD[:, b2, :].rearrange(
                                    "p (c k) -> p c k", k=128)), b2)
                    blk.gpsimd(g_gst)
                    prev[(n_, "gst")] = 0
        esAT.close()

        if STAGE < 2:
            dbg18 = nc.dram_tensor("dbg18", [CT, 128, T], I16,
                                   kind="ExternalOutput").ap()
            with (
                nc.sbuf_tensor("Y18F", [128, T], F32) as Y18F,
                nc.sbuf_tensor("G18B", [128, T], BF16) as G18B,
                nc.sbuf_tensor("N18I", [128, T], I16) as N18I,
                nc.Block() as blk,
            ):
                def v_z18(v):
                    v.memset(Y18F[:], 0.0)
                VEC(blk, v_z18)
                mvz = vD.n

                def s_y18(sy, mvz=mvz):
                    sy.wait_ge(vD.h, mvz)
                    for c18 in range(CT):
                        qS.inc(sy.dma_start(out=y[c18], in_=Y18F[:]), 16)
                blk.sync(s_y18)
                for c18 in range(CT):
                    def s_l18(sy, c18=c18, mg0=dsGT.n(0), mg1=dsGT.n(1),
                              mvp=vD.n):
                        if c18 == 0:
                            sy.wait_ge(dsGT.h(0), mg0)
                            sy.wait_ge(dsGT.h(1), mg1)
                        sy.wait_ge(vD.h, mvp)
                        qS.inc(sy.dma_start(out=G18B[:], in_=gated_d[c18]),
                               16)
                    blk.sync(s_l18)
                    mq18 = qS.n

                    def v_n18(v, mq18=mq18):
                        v.wait_ge(qS.h, mq18)
                        v.tensor_tensor(N18I[:], G18B[:], G18B[:],
                                        op=ALU.is_equal)
                    VEC(blk, v_n18)
                    mv18 = vD.n

                    def s_s18(sy, c18=c18, mv18=mv18):
                        sy.wait_ge(vD.h, mv18)
                        qS.inc(sy.dma_start(out=dbg18[c18], in_=N18I[:]), 16)
                        if c18 == CT - 1:
                            sy.wait_ge(qS.h, qS.n)
                    blk.sync(s_s18)
            return nc

        esX2 = contextlib.ExitStack()
        X2 = esX2.enter_context(nc.sbuf_tensor("X2", [128, CT, T], F32))
        with contextlib.ExitStack() as esA4:
            GSTF = esA4.enter_context(nc.sbuf_tensor("GSTF", [128, CT, T],
                                                     BF16))
            WO = esA4.enter_context(nc.sbuf_tensor("WO", [128, CT, D], BF16))
            X1S = esA4.enter_context(nc.sbuf_tensor("X1S", [128, 2, T], F32))
            with nc.Block() as blk:
                gt_gates = [dsGT.n(0), dsGT.n(1)]
                mq_x1 = qS.n

                def s_lg(sy, gt_gates=gt_gates, mq_x1=mq_x1):
                    sy.wait_ge(dsGT.h(0), gt_gates[0])
                    sy.wait_ge(dsGT.h(1), gt_gates[1])
                    sy.wait_ge(qS.h, mq_x1)
                    qS.inc(sy.dma_start(
                        out=GSTF[:],
                        in_=gated_d.rearrange("c p t -> p c t")), 16)
                blk.sync(s_lg)

                def g_lw(g):
                    qG.inc(g.dma_start(
                        out=WO[:],
                        in_=woT.rearrange("(c p) d -> p c d", p=128)), 16)
                blk.gpsimd(g_lw)
                m_lg = qS.n
                m_lw = qG.n
                ev_w = {-1: vD.n, -2: vD.n}
                for dt in range(CT):
                    def s_lx(sy, dt=dt):
                        sy.wait_ge(vD.h, ev_w[dt - 2])
                        dsX1.issue(sy.dma_start(out=X1S[:, dt % 2, :],
                                                in_=x1_d[dt]), dt % 2)
                    blk.sync(s_lx)
                    m_lx = dsX1.n(dt % 2)
                    for q in range(4):
                        gk = dt * 4 + q

                        def t_wo(te, dt=dt, q=q, gk=gk, m_lx=m_lx):
                            if gk == 0:
                                te.wait_ge(qS.h, m_lg)
                                te.wait_ge(qG.h, m_lw)
                            if gk >= 2:
                                te.wait_ge(vD.h, ev_w[-2] if gk < 2 else
                                           ev_w.get(("g", gk - 2), 0))
                            for c in range(CT):
                                mm = te.matmul(
                                    PB[gk % 2][:, :],
                                    WO[:, c, dt * 128:(dt + 1) * 128],
                                    GSTF[:, c, q * 512:(q + 1) * 512],
                                    start=(c == 0), stop=(c == CT - 1))
                            pD.inc(mm)
                        blk.tensor(t_wo)
                        mpw = pD.n

                        def v_x2(v, dt=dt, q=q, mpw=mpw, m_lx=m_lx):
                            v.wait_ge(pD.h, mpw)
                            v.wait_ge(dsX1.h(dt % 2), m_lx)
                            v.tensor_tensor(
                                X2[:, dt, q * 512:(q + 1) * 512],
                                X1S[:, dt % 2, q * 512:(q + 1) * 512],
                                PB[gk % 2][:, :], op=ALU.add)
                        VEC(blk, v_x2)
                        ev_w[("g", gk)] = vD.n
                    ev_w[dt] = vD.n

        if STAGE >= 3:
            emit_ffn(X2)

        with nc.Block() as blk:
            def s_out(sy, mv=vD.n):
                sy.wait_ge(vD.h, mv)
                qS.inc(sy.dma_start(
                    out=y.rearrange("c p t -> p c t"), in_=X2[:]), 16)
                sy.wait_ge(qS.h, qS.n)
            blk.sync(s_out)
        esX2.close()



    return nc


_CONSTS = None


def kernel(**inputs):
    global _CONSTS
    if _CONSTS is None:
        _CONSTS = _host_consts()
    cns = _CONSTS
    x = np.ascontiguousarray(inputs["L0_attn_x"])
    xT = np.ascontiguousarray(np.swapaxes(x, 1, 2))

    def wT(a):
        return np.ascontiguousarray(np.asarray(a).T)

    base = dict(
        wqT=wT(inputs["L0_attn_wq"]), wkT=wT(inputs["L0_attn_wk"]),
        wvT=wT(inputs["L0_attn_wv"]), wgTT=wT(inputs["L0_attn_wg"]),
        woT=wT(inputs["L0_attn_wo"]),
        w_gate=np.ascontiguousarray(inputs["L0_ffn_w_gate"]),
        w_up=np.ascontiguousarray(inputs["L0_ffn_w_up"]),
        w_down=np.ascontiguousarray(inputs["L0_ffn_w_down"]),
        decayT=cns["decayT"], qscT=cns["qscT"], kscT=cns["kscT"],
        gctab=cns["gctab"], ident=cns["ident"],
    )
    in_maps = []
    for c in range(8):
        b, h = c // 2, c % 2
        own = slice(h * T, (h + 1) * T)
        oth = slice((1 - h) * T, (2 - h) * T)
        m = dict(base)
        m["x_own"] = np.ascontiguousarray(
            xT[b, :, own].reshape(CT, 128, T))
        m["x_oth"] = np.ascontiguousarray(
            xT[b, :, oth].reshape(CT, 128, T))
        m["cos_own"] = np.ascontiguousarray(cns["cos_t"][own])
        m["sinS_own"] = np.ascontiguousarray(cns["sinS_t"][own])
        m["cos_oth"] = np.ascontiguousarray(cns["cos_t"][oth])
        m["sinS_oth"] = np.ascontiguousarray(cns["sinS_t"][oth])
        m["maskS"] = np.full((128, 1), float(h), np.float32)
        in_maps.append(m)

    nc = build_nc()
    res = run_bass_kernel_spmd(nc, in_maps, core_ids=list(range(8)))
    out = np.empty((B, S, D), np.float32)
    for c in range(8):
        b, h = c // 2, c % 2
        yv = res.results[c]["y"].reshape(D, T)
        out[b, h * T:(h + 1) * T, :] = yv.T
    return out
